# version 04
# baseline (speedup 1.0000x reference)
import contextlib

import numpy as np
import ml_dtypes

import concourse.bass as bass
import concourse.mybir as mybir
from concourse.bass_utils import run_bass_kernel_spmd

F32 = mybir.dt.float32
BF16 = mybir.dt.bfloat16
AF = mybir.ActivationFunctionType
ALU = mybir.AluOpType
AX = mybir.AxisListType

S = 2048
DM = 2048
H = 16
HD = 128
NK = 16
NT = 16
NE = 8
INTER = 8192
NFC = 64
ALIBI_SHIFT = 14.0
INV_SQRT_HD = np.float32(1.0 / np.sqrt(np.float32(HD)))
LN_EPS = np.float32(1e-5)
VW = H * 129

ENGINES = ("tensor", "vector", "scalar", "sync", "gpsimd")
NO_AV_PIPELINE = False


class Sem:
    def __init__(self, nc, stack, name):
        self.h = stack.enter_context(nc.semaphore(name))
        self.v = 0

    def bump(self, k):
        self.v += k
        return self.v


class Phase:

    def __init__(self, nc):
        self.nc = nc
        self.ops = {e: [] for e in ENGINES}

    def add(self, eng, fn):
        self.ops[eng].append(fn)

    def emit(self):
        with self.nc.Block() as blk:
            for name in ENGINES:
                fns = self.ops[name]
                if fns:
                    getattr(blk, name)(lambda eng, fns=fns: [f(eng) for f in fns])
        self.ops = {e: [] for e in ENGINES}


def host_constants():
    c = {}
    c["ident_bf"] = np.eye(128, dtype=ml_dtypes.bfloat16)
    c["ident_f32"] = np.eye(128, dtype=np.float32)
    ratio = np.float32(2.0 ** (-8.0 / H))
    slopes = ratio ** np.arange(1, H + 1, dtype=np.float32)
    al = np.zeros((128, H * NT), dtype=np.float32)
    p = np.arange(128, dtype=np.float32)
    for h in range(H):
        for d in range(NT):
            al[:, h * NT + d] = slopes[h] * (p - 128.0 * d - 64.0) - 6.0
    c["alibi"] = al
    tri = np.zeros((128, 4 * 512), dtype=ml_dtypes.bfloat16)
    p = np.arange(128)[:, None]
    f = np.arange(512)[None, :]
    for r in range(4):
        tri[:, r * 512:(r + 1) * 512] = (p <= f - 128 * r).astype(ml_dtypes.bfloat16)
    c["trimask"] = tri
    hm = np.array([[1.0]])
    while hm.shape[0] < 128:
        hm = np.block([[hm, hm], [hm, -hm]])
    c["h128"] = hm.astype(ml_dtypes.bfloat16)
    return c


CONST_SPECS = {
    "ident_bf": ([128, 128], BF16),
    "ident_f32": ([128, 128], F32),
    "alibi": ([128, H * NT], F32),
    "trimask": ([128, 4 * 512], BF16),
    "h128": ([128, 128], BF16),
}

WEIGHT_SPECS = {
    "L0_attn_Wq": ([DM, DM], F32), "L0_attn_bq": ([DM], F32),
    "L0_attn_Wk": ([DM, DM], F32), "L0_attn_bk": ([DM], F32),
    "L0_attn_Wv": ([DM, DM], F32), "L0_attn_bv": ([DM], F32),
    "L0_attn_Wo": ([DM, DM], F32), "L0_attn_bo": ([DM], F32),
    "L0_ffn_switch_w": ([NE, S * DM], F32), "L0_ffn_switch_b": ([NE], F32),
    "L0_ffn_w1": ([NE, 128, 128], F32), "L0_ffn_b1": ([NE, 128], F32),
    "L0_ffn_w2": ([NE, 128, 128], F32), "L0_ffn_b2": ([NE, 128], F32),
    "L0_ffn_w3": ([DM, DM], F32), "L0_ffn_b3": ([DM], F32),
    "L1_attn_Wqk": ([DM, DM], F32), "L1_attn_bqk": ([DM], F32),
    "L1_attn_Wv": ([DM, DM], F32), "L1_attn_bv": ([DM], F32),
    "L1_attn_Wu": ([DM, DM], F32), "L1_attn_bu": ([DM], F32),
    "L1_attn_Wo": ([DM, DM], F32), "L1_attn_bo": ([DM], F32),
    "L1_attn_ln_g": ([DM], F32), "L1_attn_ln_b": ([DM], F32),
    "L1_ffn_SU_ug": ([DM], F32), "L1_ffn_SU_dp": ([INTER], F32),
    "L1_ffn_Wscale_ug": ([2 * INTER], F32), "L1_ffn_Wscale_dp": ([DM], F32),
    "L1_ffn_W_up": ([INTER, DM], F32), "L1_ffn_W_gate": ([INTER, DM], F32),
    "L1_ffn_W_down": ([DM, INTER], F32),
}


class Builder:
    def __init__(self, stages=4, debug_taps=(), max_phase=99):
        self.max_phase = max_phase
        self._phase_no = 0
        self.stages = stages
        self.debug_taps = set(debug_taps)
        self.nc = bass.Bass("TRN2", target_bir_lowering=False, detect_race_conditions=False)
        self.stack = contextlib.ExitStack()
        nc = self.nc
        self.x0 = nc.dram_tensor("x0", [S, DM], F32, kind="ExternalInput")
        self.out = nc.dram_tensor("out", [S, DM], F32, kind="ExternalOutput")
        self.w = {n: nc.dram_tensor(n, sh, dt, kind="ExternalInput")
                  for n, (sh, dt) in WEIGHT_SPECS.items()}
        self.c = {n: nc.dram_tensor(n, sh, dt, kind="ExternalInput")
                  for n, (sh, dt) in CONST_SPECS.items()}
        kind = lambda n: "ExternalOutput" if n in self.debug_taps else "Internal"
        dr = nc.dram_tensor
        dbg = bool(self.debug_taps)
        if dbg:
            self.x0bf = dr("x0bf", [S, DM], BF16, kind=kind("x0bf"))
            self.x1f = dr("x1f", [S, DM], F32, kind=kind("x1f"))
            self.x1bf = dr("x1bf", [S, DM], BF16, kind=kind("x1bf"))
            self.x2f = dr("x2f", [S, DM], F32, kind=kind("x2f"))
            self.x2bf = dr("x2bf", [S, DM], BF16, kind=kind("x2bf"))
            self.x3f = dr("x3f", [S, DM], F32, kind=kind("x3f"))
            self.x3bf = dr("x3bf", [S, DM], BF16, kind=kind("x3bf"))
        else:
            xres = dr("xres", [S, DM], F32, kind="Internal")
            xbf = dr("xbf", [S, DM], BF16, kind="Internal")
            self.x0bf = xbf
            self.x1f = self.x2f = self.x3f = xres
            self.x1bf = self.x2bf = self.x3bf = xbf
        self.ctxd = dr("ctxd", [H, S, HD], BF16, kind=kind("ctxd"))
        if "vtmd" in self.debug_taps:
            self.vtmd = dr("vtmd", [128, NT * VW], BF16, kind="ExternalOutput")
        if "qkd" in self.debug_taps:
            self.qkd = dr("qkd", [2, 128, 2 * S], BF16, kind="ExternalOutput")
        if "expd" in self.debug_taps:
            self.expd = dr("expd", [16, 128, 128], BF16, kind="ExternalOutput")
        if "accq" in self.debug_taps:
            self.accq = dr("accq", [2, NT, 128, 129], F32, kind="ExternalOutput")
        if "ysd" in self.debug_taps:
            self.ysd = dr("ysd", [NT, 128, DM], F32, kind="ExternalOutput")
            self.std = dr("std", [NT, 128, 8], F32, kind="ExternalOutput")
        if "accd" in self.debug_taps:
            self.accd = dr("accd", [128, 129], F32, kind="ExternalOutput")
        self.ctxdF = dr("ctxdF", [H, HD, S], BF16, kind=kind("ctxdF"))
        al = nc.alloc_sbuf_tensor
        self.ident_bf = al("identbf", [128, 128], BF16)
        self.ident_f32 = al("identf32", [128, 128], F32)
        self.alibi = al("alibisb", [128, H * NT], F32)
        self.trimask = al("trimasksb", [128, 4 * 512], BF16)
        self.h128 = al("h128sb", [128, 128], BF16)
        self.ones1_bf = al("ones1bf", [1, 512], BF16)
        self.ones1_f32 = al("ones1f32", [128, 1], F32)
        self.onesr_f32 = al("onesrf32", [1, 128], F32)
        self.PS = nc.alloc_psum_tensor("PS", [128, 4096], F32)
        self.sem = {}
        for name in ("mm", "sv", "ss", "sg"):
            self.sem[name] = Sem(nc, self.stack, "s_" + name)
        self._consts_loaded = False

    def phase_done(self):
        self._phase_no += 1
        return self._phase_no >= self.max_phase

    def bank(self, i, width=512):
        assert 0 <= i < 8
        return self.PS[:, i * 512:i * 512 + width]

    def dma(self, ph, eng, sem, dst, src, **kw):
        tgt = sem.bump(16)
        ph.add(eng, lambda e, d=dst, s=src, m=sem.h, kw=kw:
               e.dma_start(d, s, **kw).then_inc(m, 16))
        return tgt

    def dmaT(self, ph, eng, sem, dst, src):
        tgt = sem.bump(16)
        ph.add(eng, lambda e, d=dst, s=src, m=sem.h:
               e.dma_start_transpose(d, s).then_inc(m, 16))
        return tgt

    def wait(self, ph, eng, sem, val):
        ph.add(eng, lambda e, m=sem.h, v=val: e.wait_ge(m, v))

    def mmdone(self, ph):
        mm = self.sem["mm"]
        t = mm.bump(1)
        ph.add("tensor", lambda e, m=mm.h: e.nop().then_inc(m, 1))
        return t

    def dsem(self, name):
        if name not in self.sem:
            self.sem[name] = Sem(self.nc, self.stack, "d_" + name)
        return self.sem[name]

    def load_consts(self, ph):
        dc = self.dsem("cst")
        for name, dst in (("ident_bf", self.ident_bf), ("ident_f32", self.ident_f32),
                          ("alibi", self.alibi), ("trimask", self.trimask),
                          ("h128", self.h128)):
            self.dma(ph, "sync", dc, dst[:, :], self.c[name].ap())
        ph.add("gpsimd", lambda e: e.memset(self.ones1_bf[:, :], 1.0))
        ph.add("gpsimd", lambda e: e.memset(self.ones1_f32[:, :], 1.0))
        ph.add("gpsimd", lambda e: e.memset(self.onesr_f32[:, :], 1.0))
        return dc

    def bias_row(self, ph, sc, name, wname, dsem_name="brow"):
        n = WEIGHT_SPECS[wname][0][0]
        buf = sc.enter_context(self.nc.sbuf_tensor(name, [1, n], BF16))
        ds = self.dsem(dsem_name)
        t = self.dma(ph, "gpsimd", ds, buf[:, :], self.w[wname].ap().unsqueeze(0))
        return buf, ds, t

    def bias_cols(self, ph, sc, name, wname, dc):
        nc = self.nc
        mm, sv = self.sem["mm"], self.sem["sv"]
        stg = sc.enter_context(nc.sbuf_tensor(name + "_stg", [16, 128], F32))
        col = sc.enter_context(nc.sbuf_tensor(name, [128, 16], F32))
        self.dma(ph, "sync", dc, stg[:, :],
                 self.w[wname].ap().rearrange("(a b) -> a b", b=128))
        self._pending_cols.append((stg, col))
        return col

    def flush_bias_cols(self, ph, dc):
        mm, sv = self.sem["mm"], self.sem["sv"]
        self.wait(ph, "tensor", dc, dc.v)
        out = 0
        for stg, col in self._pending_cols:
            tm = mm.bump(1)
            ph.add("tensor", lambda e, stg=stg, m=mm.h: e.transpose(
                self.bank(7, 16), stg[:, :], self.ident_f32[0:16, 0:16]).then_inc(m, 1))
            self.wait(ph, "vector", mm, tm)
            tv = sv.bump(1)
            ph.add("vector", lambda e, col=col, m=sv.h: e.tensor_copy(
                col[:, :], self.bank(7, 16)).then_inc(m, 1))
            self.wait(ph, "tensor", sv, tv)
            out = tv
        self._pending_cols = []
        return out

    def vchain(self, ph, eng, fns):
        sem = {"vector": self.sem["sv"], "scalar": self.sem["ss"], "tensor": self.sem["mm"],
               "gpsimd": self.sem["sg"]}[eng]
        t = None
        for i, fn in enumerate(fns):
            if t is not None:
                self.wait(ph, eng, sem, t)
            t = sem.bump(1)
            ph.add(eng, lambda e, fn=fn, m=sem.h: fn(e).then_inc(m, 1))
        return t

    def ln_tail(self, ph, y_sb, stat, out_dst):
        sv, ss = self.sem["sv"], self.sem["ss"]
        inv_n = 1.0 / DM
        A = ALU
        t1 = self.vchain(ph, "vector", [
            lambda e: e.tensor_scalar(stat[:, 2:3], stat[:, 0:1], inv_n, None, A.mult),
            lambda e: e.tensor_scalar(stat[:, 3:4], stat[:, 1:2], inv_n, None, A.mult),
            lambda e: e.tensor_tensor(stat[:, 4:5], stat[:, 2:3], stat[:, 2:3], A.mult),
            lambda e: e.tensor_tensor(stat[:, 4:5], stat[:, 3:4], stat[:, 4:5], A.subtract),
        ])
        self.wait(ph, "scalar", sv, t1)
        ta2 = ss.bump(1)
        ph.add("scalar", lambda e, m=ss.h: e.activation(
            stat[:, 5:6], stat[:, 4:5], AF.Sqrt, bias=np.float32(LN_EPS)).then_inc(m, 1))
        self.wait(ph, "vector", ss, ta2)
        tv = self.vchain(ph, "vector", [
            lambda e: e.reciprocal(stat[:, 6:7], stat[:, 5:6]),
            lambda e: e.tensor_tensor(stat[:, 7:8], stat[:, 6:7], stat[:, 6:7], A.mult),
            lambda e: e.tensor_scalar(stat[:, 4:5], stat[:, 4:5], float(LN_EPS), 0.5, A.add, A.mult),
            lambda e: e.tensor_tensor(stat[:, 7:8], stat[:, 4:5], stat[:, 7:8], A.mult),
            lambda e: e.tensor_scalar(stat[:, 7:8], stat[:, 7:8], -1.0, 1.5, A.mult, A.add),
            lambda e: e.tensor_tensor(stat[:, 6:7], stat[:, 6:7], stat[:, 7:8], A.mult),
            lambda e: e.tensor_scalar(out_dst, y_sb, stat[:, 2:3], stat[:, 6:7], A.subtract, A.mult),
        ])
        return tv

    def stage1(self):
        nc = self.nc
        mm, sv, ss = self.sem["mm"], self.sem["sv"], self.sem["ss"]
        self._pending_cols = []

        with contextlib.ExitStack() as sc:
            sb = lambda name, shape, dt: sc.enter_context(nc.sbuf_tensor(name, shape, dt))
            xfm = sb("xfm", [128, NK * S], BF16)
            vtm = sb("vtm", [128, NT * VW], BF16)

            ph = Phase(nc)
            dc = self.load_consts(ph)
            dg = self.dsem("g1")
            t_x0bf = self.dma(ph, "gpsimd", dg, self.x0bf.ap(), self.x0.ap())
            brow, d_brow, _ = self.bias_row(ph, sc, "brow_v", "L0_attn_bv")
            bqcol = self.bias_cols(ph, sc, "bqcol", "L0_attn_bq", dc)
            bkcol = self.bias_cols(ph, sc, "bkcol", "L0_attn_bk", dc)
            t_cols = self.flush_bias_cols(ph, dc)
            dxf = self.dsem("xfm")
            self.wait(ph, "sync", dg, t_x0bf)
            for dcc in range(NK):
                self.dmaT(ph, "sync", dxf, xfm[:, dcc * S:(dcc + 1) * S],
                          self.x0bf.ap()[:, dcc * 128:(dcc + 1) * 128])
            for ts in range(NT):
                vt = vtm[:, ts * VW:(ts + 1) * VW]
                ph.add("gpsimd", lambda e, vt=vt: e.memset(
                    vt.rearrange("p (h c) -> p h c", c=129)[:, :, 128:129], 1.0))
            ph.emit()
            if self.phase_done():
                return

            sc1 = contextlib.ExitStack()
            ph = Phase(nc)
            wvs = sc1.enter_context(nc.sbuf_tensor("wvs", [128, 2 * NK * 512], BF16))
            self.wait(ph, "tensor", dxf, dxf.v)
            self.wait(ph, "tensor", dc, dc.v)
            self.wait(ph, "tensor", d_brow, d_brow.v)
            dwv = [self.dsem("wv0"), self.dsem("wv1")]
            evac_at = {}
            mm_g = {}
            for g in range(4):
                par = g % 2
                buf = wvs[:, par * NK * 512:(par + 1) * NK * 512]
                if g >= 2:
                    self.wait(ph, "gpsimd", mm, mm_g[g - 2])
                twv = self.dma(
                    ph, "gpsimd", dwv[par], buf.rearrange("p (kc n) -> p kc n", n=512),
                    self.w["L0_attn_Wv"].ap().rearrange("(kc p) n -> p kc n", p=128)[:, :, g * 512:(g + 1) * 512])
                self.wait(ph, "tensor", dwv[par], twv)
                for ts in range(NT):
                    b = (g * NT + ts) % 2
                    if b in evac_at:
                        self.wait(ph, "tensor", sv, evac_at[b])
                    for kc in range(NK):
                        ph.add("tensor", lambda e, b=b, kc=kc, ts=ts, buf=buf: e.matmul(
                            self.bank(b),
                            xfm[:, kc * S + ts * 128: kc * S + ts * 128 + 128],
                            buf[:, kc * 512:(kc + 1) * 512],
                            start=(kc == 0), stop=False))
                    tmv = mm.bump(1)
                    ph.add("tensor", lambda e, b=b, g=g, m=mm.h: e.matmul(
                        self.bank(b), self.ones1_bf[0:1, 0:128],
                        brow[0:1, g * 512:(g + 1) * 512], start=False, stop=True).then_inc(m, 1))
                    self.wait(ph, "vector", mm, tmv)
                    vt = vtm[:, ts * VW:(ts + 1) * VW]
                    dst = vt.rearrange("p (h c) -> p h c", c=129)[:, 4 * g:4 * g + 4, 0:128]
                    tev = sv.bump(1)
                    ph.add("vector", lambda e, b=b, dst=dst, m=sv.h: e.tensor_copy(
                        dst, self.bank(b).rearrange("p (h c) -> p h c", c=128)).then_inc(m, 1))
                    evac_at[b] = tev
                mm_g[g] = mm.v
            ph.emit()
            sc1.close()
            if self.phase_done():
                return

            ph = Phase(nc)
            wqh = sc.enter_context(nc.sbuf_tensor("wqh", [128, 2 * NK * 128], BF16))
            wkh = sc.enter_context(nc.sbuf_tensor("wkh", [128, 2 * NK * 128], BF16))
            qh = sc.enter_context(nc.sbuf_tensor("qh", [128, 2 * S], BF16))
            kh = sc.enter_context(nc.sbuf_tensor("kh", [128, 2 * S], BF16))
            expb = sc.enter_context(nc.sbuf_tensor("expb", [128, 3 * 128], BF16))
            rcol = sc.enter_context(nc.sbuf_tensor("rcol", [128, 2], F32))
            ctx_sb = sc.enter_context(nc.sbuf_tensor("ctx_sb", [128, S], BF16))
            self.wait(ph, "scalar", dc, dc.v)
            self.wait(ph, "vector", dc, dc.v)
            if "accq" in self.debug_taps:
                self._dbg_accs = sc.enter_context(nc.sbuf_tensor("accs", [128, 129], F32))
            dwq = [self.dsem("wq0"), self.dsem("wq1")]
            dwk = [self.dsem("wk0"), self.dsem("wk1")]
            dcx = self.dsem("ctxs")
            mm_headqk = {}
            ctx_store = {}
            ctx_ev = {}
            evq_at = {}
            for h in range(H):
                par = h % 2
                if h >= 2:
                    self.wait(ph, "gpsimd", mm, mm_headqk[h - 2])
                tq = self.dma(ph, "gpsimd", dwq[par],
                              wqh[:, par * NK * 128:(par + 1) * NK * 128].rearrange("p (kc n) -> p kc n", n=128),
                              self.w["L0_attn_Wq"].ap().rearrange("(kc p) n -> p kc n", p=128)[:, :, h * 128:(h + 1) * 128])
                tk = self.dma(ph, "gpsimd", dwk[par],
                              wkh[:, par * NK * 128:(par + 1) * NK * 128].rearrange("p (kc n) -> p kc n", n=128),
                              self.w["L0_attn_Wk"].ap().rearrange("(kc p) n -> p kc n", p=128)[:, :, h * 128:(h + 1) * 128])
                if h == 0:
                    self.wait(ph, "scalar", sv, t_cols)
                for wsl, dstt, dws, tw, bi, bcolx in ((wqh, qh, dwq[par], tq, 0, bqcol),
                                                      (wkh, kh, dwk[par], tk, 1, bkcol)):
                    self.wait(ph, "tensor", dws, tw)
                    for g in range(4):
                        if bi in evq_at:
                            self.wait(ph, "tensor", ss, evq_at[bi])
                        tmv = mm.bump(1)
                        for kc in range(NK):
                            ph.add("tensor", lambda e, b=bi, kc=kc, g=g, wsl=wsl, par=par, m=mm.h: e.matmul(
                                self.bank(b),
                                wsl[:, par * NK * 128 + kc * 128: par * NK * 128 + (kc + 1) * 128],
                                xfm[:, kc * S + g * 512: kc * S + (g + 1) * 512],
                                start=(kc == 0), stop=(kc == NK - 1)).then_maybe_inc((m, 1) if kc == NK - 1 else None))
                        self.wait(ph, "scalar", mm, tmv)
                        tev = ss.bump(1)
                        ph.add("scalar", lambda e, b=bi, g=g, dstt=dstt, par=par, bc=bcolx[:, h:h + 1], m=ss.h: e.activation(
                            dstt[:, par * S + g * 512: par * S + (g + 1) * 512],
                            self.bank(b), AF.Identity, bias=bc).then_inc(m, 1))
                        evq_at[bi] = tev
                self.wait(ph, "tensor", ss, ss.v)
                mm_headqk[h] = mm.v
                if h >= 1:
                    self.wait(ph, "scalar", dcx, ctx_store[h - 1])
                for qt in range(NT):
                    njt = qt + 1
                    cb = 2 + qt % 2
                    if cb in ctx_ev:
                        self.wait(ph, "tensor", ss, ctx_ev[cb])
                    av_pend = None
                    for jt in range(njt):
                        b = jt % 2
                        rot = jt % 3
                        tmv = mm.bump(1)
                        ph.add("tensor", lambda e, b=b, jt=jt, qt=qt, par=par, m=mm.h: e.matmul(
                            self.bank(b, 128),
                            kh[:, par * S + jt * 128: par * S + (jt + 1) * 128],
                            qh[:, par * S + qt * 128: par * S + (qt + 1) * 128],
                            start=True, stop=True).then_inc(m, 1))
                        self.wait(ph, "scalar", mm, tmv)
                        tex = ss.bump(1)
                        ph.add("scalar", lambda e, b=b, rot=rot, h=h, d=qt - jt, m=ss.h: e.activation(
                            expb[:, rot * 128:(rot + 1) * 128], self.bank(b, 128), AF.Exp,
                            scale=INV_SQRT_HD,
                            bias=self.alibi[:, h * NT + d: h * NT + d + 1]).then_inc(m, 1))
                        exdep = (ss, tex)
                        if jt == qt:
                            self.wait(ph, "vector", ss, tex)
                            tmx = sv.bump(1)
                            ph.add("vector", lambda e, rot=rot, m=sv.h: e.tensor_mul(
                                expb[:, rot * 128:(rot + 1) * 128],
                                expb[:, rot * 128:(rot + 1) * 128],
                                self.trimask[:, 0:128]).then_inc(m, 1))
                            exdep = (sv, tmx)
                        if av_pend is not None:
                            self._emit_av(ph, av_pend, h, expb, vtm)
                        av_pend = (jt, rot, exdep, qt, njt)
                        if NO_AV_PIPELINE:
                            if jt == njt - 1:
                                tmv = mm.bump(1)
                            self._emit_av(ph, av_pend, h, expb, vtm, inc=(jt == njt - 1))
                            av_pend = None
                    if not NO_AV_PIPELINE:
                        tmv = mm.bump(1)
                        self._emit_av(ph, av_pend, h, expb, vtm, inc=True)
                    self.wait(ph, "vector", mm, tmv)
                    if h in (0, 15) and "accq" in self.debug_taps:
                        hi = 0 if h == 0 else 1
                        taq = sv.bump(1)
                        ph.add("vector", lambda e, cb=cb, m=sv.h: e.tensor_copy(
                            self._dbg_accs[:, :], self.PS[:, cb * 512: cb * 512 + 129]).then_inc(m, 1))
                        self.wait(ph, "sync", sv, taq)
                        tdq = self.dma(ph, "sync", self.dsem("accq"), self.accq.ap()[hi, qt, :, :], self._dbg_accs[:, :])
                        self.wait(ph, "vector", self.dsem("accq"), tdq)
                    if cb in ctx_ev:
                        self.wait(ph, "vector", ss, ctx_ev[cb])
                    rsl = qt % 2
                    tvv = sv.bump(1)
                    ph.add("vector", lambda e, cb=cb, rsl=rsl, m=sv.h: e.reciprocal(
                        rcol[:, rsl:rsl + 1],
                        self.PS[:, cb * 512 + 128: cb * 512 + 129]).then_inc(m, 1))
                    self.wait(ph, "scalar", sv, tvv)
                    tev = ss.bump(1)
                    ph.add("scalar", lambda e, cb=cb, qt=qt, rsl=rsl, m=ss.h: e.activation(
                        ctx_sb[:, qt * 128:(qt + 1) * 128],
                        self.PS[:, cb * 512: cb * 512 + 128],
                        AF.Identity, scale=rcol[:, rsl:rsl + 1]).then_inc(m, 1))
                    ctx_ev[cb] = tev
                self.wait(ph, "sync", ss, ss.v)
                ctx_store[h] = self.dma(ph, "sync", dcx,
                                        self.ctxd.ap()[h].rearrange("(b p) c -> p b c", p=128),
                                        ctx_sb[:, :].rearrange("p (b c) -> p b c", c=128))
            ph.emit()
            t_ctx_total = dcx.v
            if self.phase_done():
                return

        with contextlib.ExitStack() as sc:
            sb = lambda name, shape, dt: sc.enter_context(nc.sbuf_tensor(name, shape, dt))
            ph = Phase(nc)
            ctxf = sb("ctxf", [128, 2 * S], BF16)
            wos = sb("wos", [128, 4 * NK * 512], BF16)
            x0t = sb("x0t", [128, 2 * DM], F32)
            y_sb = sb("y_sb", [128, DM], F32)
            sq_sb = sb("sq_sb", [128, DM], F32)
            x1t = sb("x1t", [128, DM], F32)
            stat = sb("stat", [128, 8], F32)
            browo, d_bro, _ = self.bias_row(ph, sc, "brow_o2", "L0_attn_bo", "bro")
            dcx = self.dsem("ctxs")
            dfT = [self.dsem("cfT0"), self.dsem("cfT1")]
            dcF = [self.dsem("cF0"), self.dsem("cF1")]
            dct = [self.dsem("cts0"), self.dsem("cts1")]
            dwo = self.dsem("wo")
            dx0 = [self.dsem("x00"), self.dsem("x01")]
            dx1 = self.dsem("x1s")
            dx1b = self.dsem("x1b")
            self.wait(ph, "sync", dcx, t_ctx_total)
            tF = {}
            for h in range(H):
                par = h % 2
                buf = ctxf[:, par * S:(par + 1) * S]
                if h >= 2:
                    self.wait(ph, "sync", dcF[par], tF[h - 2])
                t1 = self.dmaT(ph, "sync", dfT[par], buf, self.ctxd.ap()[h])
                self.wait(ph, "sync", dfT[par], t1)
                tF[h] = self.dma(ph, "sync", dcF[par], self.ctxdF.ap()[h], buf)
            for g in range(4):
                bufw = wos[:, g * NK * 512:(g + 1) * NK * 512]
                self.dma(ph, "gpsimd", dwo,
                         bufw.rearrange("p (kc n) -> p kc n", n=512),
                         self.w["L0_attn_Wo"].ap().rearrange("(kc p) n -> p kc n", p=128)[:, :, g * 512:(g + 1) * 512])
            self.wait(ph, "sync", dcF[0], dcF[0].v)
            self.wait(ph, "sync", dcF[1], dcF[1].v)
            mm_ts = {}
            ln_done = {}
            for ts in range(NT):
                par = ts % 2
                buf = ctxf[:, par * S:(par + 1) * S]
                if ts >= 2:
                    self.wait(ph, "sync", mm, mm_ts[ts - 2])
                tld = self.dma(ph, "sync", dct[par],
                               buf.rearrange("p (h q) -> p h q", q=128),
                               self.ctxdF.ap()[:, :, ts * 128:(ts + 1) * 128].rearrange("h p q -> p h q"))
                if ts >= 2:
                    self.wait(ph, "scalar", sv, ln_done[ts - 2])
                tx0 = self.dma(ph, "scalar", dx0[par], x0t[:, par * DM:(par + 1) * DM],
                               self.x0.ap()[ts * 128:(ts + 1) * 128, :])
                self.wait(ph, "tensor", dct[par], tld)
                if ts == 0:
                    self.wait(ph, "tensor", d_bro, d_bro.v)
                    self.wait(ph, "tensor", dwo, dwo.v)
                for g in range(4):
                    bk = 4 + g
                    for hc in range(H):
                        ph.add("tensor", lambda e, bk=bk, hc=hc, g=g, buf=buf: e.matmul(
                            self.bank(bk), buf[:, hc * 128:(hc + 1) * 128],
                            wos[:, g * NK * 512 + hc * 512: g * NK * 512 + (hc + 1) * 512],
                            start=(hc == 0), stop=False))
                    upd = (mm.h, 1) if g == 3 else None
                    ph.add("tensor", lambda e, bk=bk, g=g, upd=upd: e.matmul(
                        self.bank(bk), self.ones1_bf[0:1, 0:128],
                        browo[0:1, g * 512:(g + 1) * 512], start=False, stop=True).then_maybe_inc(upd))
                mm_ts[ts] = mm.bump(1)
                self.wait(ph, "vector", mm, mm_ts[ts])
                self.wait(ph, "vector", dx0[par], tx0)
                sv.bump(1)
                ph.add("vector", lambda e, m=sv.h: e.memset(stat[:, 0:2], 0.0).then_inc(m, 1))
                self.wait(ph, "vector", sv, sv.v)
                tev = sv.bump(1)
                ph.add("vector", lambda e, par=par, m=sv.h: e.scalar_tensor_tensor(
                    y_sb[:, :], self.PS[:, 4 * 512:8 * 512], 1.0,
                    x0t[:, par * DM:(par + 1) * DM], ALU.mult, ALU.add,
                    accum_out=stat[:, 0:1]).then_inc(m, 1))
                self.wait(ph, "tensor", sv, tev)
                self.wait(ph, "scalar", sv, tev)
                if "ysd" in self.debug_taps:
                    tdy = self.dma(ph, "scalar", self.dsem("ysd"), self.ysd.ap()[ts, :, :], y_sb[:, :])
                    self.wait(ph, "vector", self.dsem("ysd"), tdy)
                tsq = ss.bump(1)
                ph.add("scalar", lambda e, m=ss.h: e.activation(
                    sq_sb[:, :], y_sb[:, :], AF.Square,
                    accum_out=stat[:, 1:2]).then_inc(m, 1))
                self.wait(ph, "vector", ss, tsq)
                tln = self.ln_tail(ph, y_sb[:, :], stat, x1t[:, :])
                ln_done[ts] = tln
                if "ysd" in self.debug_taps:
                    self.wait(ph, "scalar", sv, tln)
                    tdd = self.dma(ph, "scalar", self.dsem("std"), self.std.ap()[ts, :, :], stat[:, :])
                    self.wait(ph, "vector", self.dsem("std"), tdd)
                self.wait(ph, "sync", sv, tln)
                self.wait(ph, "gpsimd", sv, tln)
                t_f32 = self.dma(ph, "sync", dx1, self.x1f.ap()[ts * 128:(ts + 1) * 128, :], x1t[:, :])
                t_bf = self.dma(ph, "gpsimd", dx1b, self.x1bf.ap()[ts * 128:(ts + 1) * 128, :], x1t[:, :])
                self.wait(ph, "vector", dx1, t_f32)
                self.wait(ph, "vector", dx1b, t_bf)
            ph.emit()
            if self.phase_done():
                return

    def _emit_av(self, ph, pend, h, expb, vtm, inc=False):
        jt, rot, exdep, qt, njt = pend
        mm = self.sem["mm"]
        cb = 2 + qt % 2
        self.wait(ph, "tensor", exdep[0], exdep[1])
        upd = (mm.h, 1) if inc else None
        ph.add("tensor", lambda e, rot=rot, jt=jt, h=h, cb=cb, upd=upd: e.matmul(
            self.PS[:, cb * 512: cb * 512 + 129],
            expb[:, rot * 128:(rot + 1) * 128],
            vtm[:, jt * VW + h * 129: jt * VW + (h + 1) * 129],
            start=(jt == 0), stop=(jt == njt - 1)).then_maybe_inc(upd))

    def stage2(self):
        nc = self.nc
        mm, sv, ss, sg = self.sem["mm"], self.sem["sv"], self.sem["ss"], self.sem["sg"]
        w = self.w
        with contextlib.ExitStack() as sc:
            sb = lambda name, shape, dt: sc.enter_context(nc.sbuf_tensor(name, shape, dt))
            A = sb("s2A", [128, NK * S], BF16)
            gates_bc = sb("gates_bc", [128, NE], F32)
            b1T = sb("b1T", [128, NE], F32)
            b2eff = sb("b2eff", [128, 1], F32)
            w1T = sb("w1T", [128, NE * 128], BF16)
            w2T = sb("w2T", [128, NE * 128], BF16)

            ph = Phase(nc)
            sc4 = contextlib.ExitStack()
            sb4 = lambda name, shape, dt: sc4.enter_context(nc.sbuf_tensor(name, shape, dt))
            swb = sb4("swb", [128, 4 * DM], F32)
            x1tl = sb4("x1tl", [128, 2 * DM], BF16)
            junk = sb4("junk", [128, 2 * DM], BF16)
            racc = sb4("racc", [128, 2 * NE * NT], F32)
            acc8 = sb4("acc8", [128, NE], F32)
            rsc = sb4("rsc", [8, 64], F32)
            b2sb = sb4("b2sb", [8, 128], F32)
            b1sb = sb4("b1sb", [8, 128], F32)
            swbias = sb4("swbias", [1, 8], F32)
            w1s = sb4("w1s", [128, NE * 128], BF16)
            w2s = sb4("w2s", [128, NE * 128], BF16)

            self.w3bf = nc.dram_tensor("w3bf", [DM, DM], BF16, kind="Internal")
            d_w3bf = self.dsem("w3bf")
            self.dma(ph, "gpsimd", d_w3bf, self.w3bf.ap(), w["L0_ffn_w3"].ap())
            d_s2c = self.dsem("s2c")
            self.dma(ph, "sync", d_s2c, swbias[:, :], w["L0_ffn_switch_b"].ap().unsqueeze(0))
            self.dma(ph, "sync", d_s2c, b1sb[:, :], w["L0_ffn_b1"].ap())
            self.dma(ph, "sync", d_s2c, b2sb[:, :], w["L0_ffn_b2"].ap())
            d_w12 = self.dsem("w12")
            self.dma(ph, "gpsimd", d_w12, w1s[:, :].rearrange("p (e f) -> p e f", f=128),
                     w["L0_ffn_w1"].ap().rearrange("e f d -> f e d"))
            self.dma(ph, "gpsimd", d_w12, w2s[:, :].rearrange("p (e f) -> p e f", f=128),
                     w["L0_ffn_w2"].ap().rearrange("e d f -> d e f"))
            d_xfm2 = self.dsem("xfm2")
            for dc in range(NK):
                self.dmaT(ph, "sync", d_xfm2, A[:, dc * S:(dc + 1) * S],
                          self.x1bf.ap()[:, dc * 128:(dc + 1) * 128])
            d_sw = [self.dsem("sw0"), self.dsem("sw1"), self.dsem("sw2"), self.dsem("sw3")]
            d_x1r = [self.dsem("x1r0"), self.dsem("x1r1")]
            sv.bump(1)
            ph.add("vector", lambda e, m=sv.h: e.memset(racc[:, 0:NE * NT], 0.0).then_inc(m, 1))
            t_r0 = sv.v
            stt_done = {}
            gtt_done = {}
            for ts in range(NT):
                par = ts % 2
                if ts >= 2:
                    self.wait(ph, "sync", sv, stt_done[(ts - 2, NE - 1)])
                tx1 = self.dma(ph, "sync", d_x1r[par], x1tl[:, par * DM:(par + 1) * DM],
                               self.x1bf.ap()[ts * 128:(ts + 1) * 128, :])
                for e in range(NE):
                    i = ts * NE + e
                    veng = "vector"
                    esem = sv
                    spar = i % 4
                    qeng = "sync" if (i % 2 == 0) else "scalar"
                    if i >= 4:
                        pi = i - 4
                        self.wait(ph, qeng, sv, stt_done[(pi // NE, pi % NE)])
                    tsw = self.dma(ph, qeng, d_sw[spar], swb[:, spar * DM:(spar + 1) * DM],
                                   w["L0_ffn_switch_w"].ap().rearrange("e (t p f) -> e t p f", p=128, f=DM)[e, ts])
                    self.wait(ph, veng, d_sw[spar], tsw)
                    if e in (0, 5):
                        self.wait(ph, veng, d_x1r[par], tx1)
                    if i == 0:
                        self.wait(ph, "vector", sv, t_r0)
                    col = e * NT + ts
                    tst = esem.bump(1)
                    jslice = junk[:, 0:DM]
                    ph.add(veng, lambda e2, par=par, spar=spar, col=col, m=esem.h, jslice=jslice: e2.scalar_tensor_tensor(
                        jslice, x1tl[:, par * DM:(par + 1) * DM], 1.0,
                        swb[:, spar * DM:(spar + 1) * DM], ALU.mult, ALU.mult,
                        accum_out=racc[:, col:col + 1]).then_inc(m, 1))
                    stt_done[(ts, e)] = tst
            t1 = self.vchain(ph, "vector", [
                lambda e: e.tensor_reduce(acc8[:, :],
                                          racc[:, 0:NE * NT].rearrange("p (e t) -> p e t", t=NT),
                                          AX.X, ALU.add)])
            self.wait(ph, "tensor", sv, t1)
            tmv = mm.bump(1)
            ph.add("tensor", lambda e, m=mm.h: e.matmul(
                self.PS[0:1, 7 * 512:7 * 512 + 8], self.ones1_f32[:, 0:1], acc8[:, :],
                start=True, stop=True).then_inc(m, 1))
            self.wait(ph, "vector", mm, tmv)
            self.wait(ph, "vector", d_s2c, d_s2c.v)
            R = rsc
            t2 = self.vchain(ph, "vector", [
                lambda e: e.tensor_tensor(R[0:1, 0:8], self.PS[0:1, 7 * 512:7 * 512 + 8], swbias[0:1, :], ALU.add),
                lambda e: e.reduce_max(R[0:1, 8:9], R[0:1, 0:8], AX.X),
                lambda e: e.tensor_scalar(R[0:1, 9:10], R[0:1, 8:9], -1.0, None, ALU.mult),
            ])
            self.wait(ph, "scalar", sv, t2)
            t3 = ss.bump(1)
            ph.add("scalar", lambda e, m=ss.h: e.activation(
                R[0:1, 10:18], R[0:1, 0:8], AF.Exp, bias=R[0:1, 9:10]).then_inc(m, 1))
            self.wait(ph, "vector", ss, t3)
            t4 = self.vchain(ph, "vector", [
                lambda e: e.reduce_sum(R[0:1, 18:19], R[0:1, 10:18], AX.X),
                lambda e: e.reciprocal(R[0:1, 19:20], R[0:1, 18:19]),
                lambda e: e.tensor_scalar(R[0:1, 20:28], R[0:1, 10:18], R[0:1, 19:20], None, ALU.mult),
                lambda e: e.reduce_max(R[0:1, 28:29], R[0:1, 20:28], AX.X),
                lambda e: e.tensor_single_scalar(R[0:1, 29:37], R[0:1, 20:28], R[0:1, 28:29], ALU.is_ge),
                lambda e: e.scalar_tensor_tensor(R[0:1, 29:37], R[0:1, 29:37], -1e30, R[0:1, 20:28], ALU.mult, ALU.add),
                lambda e: e.reduce_max(R[0:1, 37:38], R[0:1, 29:37], AX.X),
                lambda e: e.tensor_single_scalar(R[0:1, 38:46], R[0:1, 20:28], R[0:1, 37:38], ALU.is_ge),
                lambda e: e.tensor_tensor(R[0:1, 46:54], R[0:1, 20:28], R[0:1, 38:46], ALU.mult),
            ])
            self.wait(ph, "tensor", sv, t4)
            tmv = mm.bump(1)
            ph.add("tensor", lambda e, m=mm.h: e.matmul(
                self.PS[:, 6 * 512:6 * 512 + 8], self.onesr_f32[0:1, :],
                R[0:1, 46:54], start=True, stop=True).then_inc(m, 1))
            tmv2 = mm.bump(1)
            ph.add("tensor", lambda e, m=mm.h: e.transpose(
                self.PS[0:8, 7 * 512 + 16:7 * 512 + 17], R[0:1, 46:54],
                self.ident_f32[0:1, 0:1]).then_inc(m, 1))
            self.wait(ph, "vector", mm, tmv2)
            t5 = self.vchain(ph, "vector", [
                lambda e: e.tensor_copy(gates_bc[:, :], self.PS[:, 6 * 512:6 * 512 + 8]),
                lambda e: e.tensor_copy(R[0:8, 54:55], self.PS[0:8, 7 * 512 + 16:7 * 512 + 17]),
            ])
            self.wait(ph, "tensor", sv, t5)
            self.wait(ph, "tensor", d_s2c, d_s2c.v)
            tmv3 = mm.bump(1)
            ph.add("tensor", lambda e, m=mm.h: e.matmul(
                self.PS[:, 5 * 512:5 * 512 + 1], b2sb[:, :], R[0:8, 54:55],
                start=True, stop=True).then_inc(m, 1))
            tmv4 = mm.bump(1)
            ph.add("tensor", lambda e, m=mm.h: e.transpose(
                self.PS[:, 4 * 512:4 * 512 + 8], b1sb[:, :],
                self.ident_f32[0:8, 0:8]).then_inc(m, 1))
            self.wait(ph, "vector", mm, tmv4)
            t6 = self.vchain(ph, "vector", [
                lambda e: e.tensor_copy(b2eff[:, :], self.PS[:, 5 * 512:5 * 512 + 1]),
                lambda e: e.tensor_copy(b1T[:, :], self.PS[:, 4 * 512:4 * 512 + 8]),
            ])
            self.wait(ph, "tensor", d_w12, d_w12.v)
            for e_ in range(NE):
                for wsrc, wdst in ((w1s, w1T), (w2s, w2T)):
                    tmvT = mm.bump(1)
                    ph.add("tensor", lambda e, e_=e_, wsrc=wsrc, m=mm.h: e.transpose(
                        self.PS[:, 3 * 512:3 * 512 + 64].bitcast(BF16),
                        wsrc[:, e_ * 128:(e_ + 1) * 128], self.ident_bf[:, :]).then_inc(m, 1))
                    self.wait(ph, "vector", mm, tmvT)
                    tvT = sv.bump(1)
                    ph.add("vector", lambda e, e_=e_, wdst=wdst, m=sv.h: e.tensor_copy(
                        wdst[:, e_ * 128:(e_ + 1) * 128],
                        self.PS[:, 3 * 512:3 * 512 + 64].bitcast(BF16)).then_inc(m, 1))
                    self.wait(ph, "tensor", sv, tvT)
            self.wait(ph, "vector", sv, t6)
            ph.emit()
            sc4.close()
            if self.phase_done():
                return sc.close()

            ph = Phase(nc)
            sc5 = contextlib.ExitStack()
            sb5 = lambda name, shape, dt: sc5.enter_context(nc.sbuf_tensor(name, shape, dt))
            hb = sb5("hb", [128, 4 * 512], BF16)
            hg = sb5("hg", [128, 4 * 512], BF16)
            B = sb5("s2B", [128, NK * S], BF16)
            x1ft = sb5("x1ft", [128, 2 * DM], F32)
            y_sb = sb5("y2_sb", [128, DM], F32)
            sq_sb = sb5("sq2_sb", [128, DM], F32)
            x2t = sb5("x2t", [128, DM], F32)
            stat = sb5("stat2", [128, 8], F32)
            brow3, d_br3, _ = self.bias_row(ph, sc5, "brow_b3", "L0_ffn_b3", "br3")
            self.wait(ph, "tensor", d_xfm2, d_xfm2.v)
            gel_done = {}
            gmul_done = {}
            omm_done = {}
            pre_done = {}
            for h in range(H):
                for g in range(4):
                    gi = h * 4 + g
                    ob = 2 + gi % 2
                    if gi >= 2:
                        self.wait(ph, "tensor", sv, pre_done[gi - 2])
                    HB = (0, 1, 4, 5)
                    for e_ in range(NE):
                        i = gi * NE + e_
                        hbk = i % 4
                        if i >= 4:
                            self.wait(ph, "tensor", ss, gel_done[i - 4])
                        tmv = mm.bump(1)
                        ph.add("tensor", lambda e, hbk=hbk, e_=e_, h=h, g=g, m=mm.h: e.matmul(
                            self.bank(HB[hbk]), w1T[:, e_ * 128:(e_ + 1) * 128],
                            A[:, h * S + g * 512: h * S + (g + 1) * 512],
                            start=True, stop=True).then_inc(m, 1))
                        self.wait(ph, "scalar", mm, tmv)
                        if i >= 4:
                            self.wait(ph, "scalar", sv, gmul_done[i - 4])
                        tgl = ss.bump(1)
                        ph.add("scalar", lambda e, hbk=hbk, e_=e_, m=ss.h: e.activation(
                            hb[:, hbk * 512:(hbk + 1) * 512], self.bank(HB[hbk]), AF.Gelu,
                            bias=b1T[:, e_:e_ + 1]).then_inc(m, 1))
                        gel_done[i] = tgl
                        self.wait(ph, "vector", ss, tgl)
                        if i >= 4:
                            self.wait(ph, "vector", mm, omm_done.get(i - 4, 0))
                        tgm = sv.bump(1)
                        ph.add("vector", lambda e, hbk=hbk, e_=e_, m=sv.h: e.tensor_scalar(
                            hg[:, hbk * 512:(hbk + 1) * 512], hb[:, hbk * 512:(hbk + 1) * 512],
                            gates_bc[:, e_:e_ + 1], None, ALU.mult).then_inc(m, 1))
                        gmul_done[i] = tgm
                        self.wait(ph, "tensor", sv, tgm)
                        tom = mm.bump(1)
                        ph.add("tensor", lambda e, ob=ob, e_=e_, hbk=hbk, m=mm.h: e.matmul(
                            self.bank(ob), w2T[:, e_ * 128:(e_ + 1) * 128],
                            hg[:, hbk * 512:(hbk + 1) * 512],
                            start=(e_ == 0), stop=(e_ == NE - 1)).then_inc(m, 1))
                        omm_done[i] = tom
                    self.wait(ph, "vector", mm, omm_done[gi * NE + NE - 1])
                    tpre = sv.bump(1)
                    ph.add("vector", lambda e, ob=ob, h=h, g=g, m=sv.h: e.scalar_tensor_tensor(
                        A[:, h * S + g * 512: h * S + (g + 1) * 512], self.bank(ob),
                        b2eff[:, 0:1], A[:, h * S + g * 512: h * S + (g + 1) * 512],
                        ALU.add, ALU.add).then_inc(m, 1))
                    pre_done[gi] = tpre
            t_pre_all = sv.v
            d_w3T = self.dsem("w3T")
            self.wait(ph, "sync", d_w3bf, d_w3bf.v)
            for dc in range(NK):
                self.dmaT(ph, "sync", d_w3T, B[:, dc * S:(dc + 1) * S],
                          self.w3bf.ap()[:, dc * 128:(dc + 1) * 128])
            d_x1f = [self.dsem("x1f0"), self.dsem("x1f1")]
            dx2 = self.dsem("x2s")
            dx2b = self.dsem("x2b")
            self.wait(ph, "tensor", d_w3T, d_w3T.v)
            self.wait(ph, "tensor", sv, t_pre_all)
            self.wait(ph, "tensor", d_br3, d_br3.v)
            mm_ts = {}
            ln_done = {}
            for ts in range(NT):
                par = ts % 2
                if ts >= 2:
                    self.wait(ph, "scalar", sv, ln_done[ts - 2])
                tx1 = self.dma(ph, "scalar", d_x1f[par], x1ft[:, par * DM:(par + 1) * DM],
                               self.x1f.ap()[ts * 128:(ts + 1) * 128, :])
                for g in range(4):
                    bk = 4 + g
                    for dc in range(NK):
                        ph.add("tensor", lambda e, bk=bk, dc=dc, g=g, ts=ts: e.matmul(
                            self.bank(bk), A[:, dc * S + ts * 128: dc * S + ts * 128 + 128],
                            B[:, dc * S + g * 512: dc * S + (g + 1) * 512],
                            start=(dc == 0), stop=False))
                    upd = (mm.h, 1) if g == 3 else None
                    ph.add("tensor", lambda e, bk=bk, g=g, upd=upd: e.matmul(
                        self.bank(bk), self.ones1_bf[0:1, 0:128],
                        brow3[0:1, g * 512:(g + 1) * 512], start=False, stop=True).then_maybe_inc(upd))
                mm_ts[ts] = mm.bump(1)
                self.wait(ph, "vector", mm, mm_ts[ts])
                self.wait(ph, "vector", d_x1f[par], tx1)
                sv.bump(1)
                ph.add("vector", lambda e, m=sv.h: e.memset(stat[:, 0:2], 0.0).then_inc(m, 1))
                self.wait(ph, "vector", sv, sv.v)
                tev = sv.bump(1)
                ph.add("vector", lambda e, par=par, m=sv.h: e.scalar_tensor_tensor(
                    y_sb[:, :], self.PS[:, 4 * 512:8 * 512], 1.0,
                    x1ft[:, par * DM:(par + 1) * DM], ALU.mult, ALU.add,
                    accum_out=stat[:, 0:1]).then_inc(m, 1))
                self.wait(ph, "tensor", sv, tev)
                self.wait(ph, "scalar", sv, tev)
                tsq = ss.bump(1)
                ph.add("scalar", lambda e, m=ss.h: e.activation(
                    sq_sb[:, :], y_sb[:, :], AF.Square, accum_out=stat[:, 1:2]).then_inc(m, 1))
                self.wait(ph, "vector", ss, tsq)
                tln = self.ln_tail(ph, y_sb[:, :], stat, x2t[:, :])
                ln_done[ts] = tln
                self.wait(ph, "sync", sv, tln)
                self.wait(ph, "gpsimd", sv, tln)
                t_f32 = self.dma(ph, "sync", dx2, self.x2f.ap()[ts * 128:(ts + 1) * 128, :], x2t[:, :])
                t_bf = self.dma(ph, "gpsimd", dx2b, self.x2bf.ap()[ts * 128:(ts + 1) * 128, :], x2t[:, :])
                self.wait(ph, "vector", dx2, t_f32)
                self.wait(ph, "vector", dx2b, t_bf)
            ph.emit()
            sc5.close()
            if self.phase_done():
                return

    def stage3(self):
        nc = self.nc
        mm, sv, ss = self.sem["mm"], self.sem["sv"], self.sem["ss"]
        w = self.w
        kind = lambda n: "ExternalOutput" if n in self.debug_taps else "Internal"
        if self.debug_taps:
            self.kbf = nc.dram_tensor("kbf", [S, DM], BF16, kind=kind("kbf")).ap()
            self.mixbf = nc.dram_tensor("mixbf", [S, DM], BF16, kind=kind("mixbf")).ap()
            self.ubf = nc.dram_tensor("ubf", [S, DM], BF16, kind=kind("ubf")).ap()
        else:
            self.kbf = self.ctxd.ap().rearrange("h s c -> (h s c)").rearrange("(a b) -> a b", b=DM)
            self.mixbf = self.ctxdF.ap().rearrange("h c s -> (h c s)").rearrange("(a b) -> a b", b=DM)
            self.ubf = self.w3bf.ap()
        with contextlib.ExitStack() as sc:
            sb = lambda name, shape, dt: sc.enter_context(nc.sbuf_tensor(name, shape, dt))
            A = sb("s3A", [128, NK * S], BF16)
            B = sb("s3B", [128, NK * S], BF16)
            kv_sb = sb("kv_sb", [128, H * 128], BF16)
            sc6 = contextlib.ExitStack()
            sb6 = lambda name, shape, dt: sc6.enter_context(nc.sbuf_tensor(name, shape, dt))
            sc67 = contextlib.ExitStack()
            sb67 = lambda name, shape, dt: sc67.enter_context(nc.sbuf_tensor(name, shape, dt))
            wsl = sb67("wsl3", [128, 2 * NK * 512], BF16)

            def proj_pass(ph, wname, brow, out_emit, d_par, bias_row_ready):
                done = {}
                mm_g = {}
                mm_entry = mm.v
                for g in range(4):
                    par = g % 2
                    buf = wsl[:, par * NK * 512:(par + 1) * NK * 512]
                    if g >= 2:
                        self.wait(ph, "gpsimd", mm, mm_g[g - 2])
                    elif mm_entry > 0:
                        self.wait(ph, "gpsimd", mm, mm_entry)
                    twv = self.dma(ph, "gpsimd", d_par[par],
                                   buf.rearrange("p (kc n) -> p kc n", n=512),
                                   w[wname].ap().rearrange("(kc p) n -> p kc n", p=128)[:, :, g * 512:(g + 1) * 512])
                    self.wait(ph, "tensor", d_par[par], twv)
                    if g == 0:
                        self.wait(ph, "tensor", bias_row_ready[0], bias_row_ready[1])
                    for ts in range(NT):
                        i = g * NT + ts
                        b = i % 2
                        if i >= 2:
                            sem, val = done[i - 2]
                            self.wait(ph, "tensor", sem, val)
                        for kc in range(NK):
                            ph.add("tensor", lambda e, b=b, kc=kc, ts=ts, buf=buf: e.matmul(
                                self.bank(b), A[:, kc * S + ts * 128: kc * S + ts * 128 + 128],
                                buf[:, kc * 512:(kc + 1) * 512], start=(kc == 0), stop=False))
                        tmv = mm.bump(1)
                        ph.add("tensor", lambda e, b=b, g=g, brow=brow, m=mm.h: e.matmul(
                            self.bank(b), self.ones1_bf[0:1, 0:128],
                            brow[0:1, g * 512:(g + 1) * 512], start=False, stop=True).then_inc(m, 1))
                        done[i] = out_emit(ph, g, ts, b, tmv, i)
                    mm_g[g] = mm.v
                return done

            ph = Phase(nc)
            elu1 = sb67("elu1", [128, 2 * 512], F32)
            elu2 = sb67("elu2", [128, 2 * 512], F32)
            elu3 = sb67("elu3", [128, 2 * 512], F32)
            browk, d_brk, _ = self.bias_row(ph, sc67, "brow_qk", "L1_attn_bqk", "brk")
            d_xfm3 = self.dsem("xfm3")
            for dc in range(NK):
                self.dmaT(ph, "sync", d_xfm3, A[:, dc * S:(dc + 1) * S],
                          self.x2bf.ap()[:, dc * 128:(dc + 1) * 128])
            d_wqk = [self.dsem("wqk0"), self.dsem("wqk1")]
            d_kbf = self.dsem("kbfs")
            self.wait(ph, "tensor", d_xfm3, d_xfm3.v)

            kspill = {}
            def elu_emit(ph, g, ts, b, tmv, i):
                ep = i % 2
                self.wait(ph, "vector", mm, tmv)
                tv1 = self.vchain(ph, "vector", [
                    lambda e, b=b, ep=ep: e.tensor_scalar(
                        elu1[:, ep * 512:(ep + 1) * 512], self.bank(b), 0.0, None, ALU.min),
                    lambda e, b=b, ep=ep: e.tensor_relu(
                        elu3[:, ep * 512:(ep + 1) * 512], self.bank(b)),
                ])
                self.wait(ph, "scalar", sv, tv1)
                tsx = ss.bump(1)
                ph.add("scalar", lambda e, ep=ep, m=ss.h: e.activation(
                    elu2[:, ep * 512:(ep + 1) * 512], elu1[:, ep * 512:(ep + 1) * 512],
                    AF.Exp).then_inc(m, 1))
                self.wait(ph, "vector", ss, tsx)
                tst = sv.bump(1)
                ph.add("vector", lambda e, ep=ep, ts=ts, g=g, m=sv.h: e.scalar_tensor_tensor(
                    B[:, ts * DM + g * 512: ts * DM + (g + 1) * 512],
                    elu2[:, ep * 512:(ep + 1) * 512], -1.0,
                    elu3[:, ep * 512:(ep + 1) * 512], ALU.add, ALU.add).then_inc(m, 1))
                kspill.setdefault(ts, []).append(tst)
                return (sv, tv1)

            proj_pass(ph, "L1_attn_Wqk", browk, elu_emit, d_wqk, (d_brk, d_brk.v))
            for ts in range(NT):
                self.wait(ph, "sync", sv, max(kspill[ts]))
                self.dma(ph, "sync", d_kbf, self.kbf[ts * 128:(ts + 1) * 128, :],
                         B[:, ts * DM:(ts + 1) * DM])
            ph.emit()
            if self.phase_done():
                sc6.close()
                return

            ph = Phase(nc)
            vbuf = sb67("vbuf", [128, 2 * 512], BF16)
            browv, d_brv, _ = self.bias_row(ph, sc67, "brow_v3", "L1_attn_bv", "brv")
            browu, d_bru, _ = self.bias_row(ph, sc67, "brow_u3", "L1_attn_bu", "bru")
            d_wv3 = [self.dsem("wv30"), self.dsem("wv31")]
            d_wu3 = [self.dsem("wu30"), self.dsem("wu31")]
            kv_last = {}

            def v_emit(ph, g, ts, b, tmv, i):
                vp = i % 2
                self.wait(ph, "scalar", mm, tmv)
                tsl = ss.bump(1)
                ph.add("scalar", lambda e, b=b, vp=vp, m=ss.h: e.activation(
                    vbuf[:, vp * 512:(vp + 1) * 512], self.bank(b), AF.Silu).then_inc(m, 1))
                self.wait(ph, "tensor", ss, tsl)
                kvbk = 2 + g
                for hh in range(4):
                    h = g * 4 + hh
                    upd = (mm.h, 1) if hh == 3 else None
                    ph.add("tensor", lambda e, h=h, hh=hh, ts=ts, vp=vp, kvbk=kvbk, upd=upd: e.matmul(
                        self.PS[:, kvbk * 512 + hh * 128: kvbk * 512 + (hh + 1) * 128],
                        B[:, ts * DM + h * 128: ts * DM + (h + 1) * 128],
                        vbuf[:, vp * 512 + hh * 128: vp * 512 + (hh + 1) * 128],
                        start=(ts == 0 and hh == 0),
                        stop=(ts == NT - 1 and hh == 3)).then_maybe_inc(upd))
                tkv = mm.bump(1)
                kv_last[(g, ts)] = tkv
                return (ss, tsl)

            proj_pass(ph, "L1_attn_Wv", browv, v_emit, d_wv3, (d_brv, d_brv.v))
            for g in range(4):
                self.wait(ph, "vector", mm, kv_last[(g, NT - 1)])
                tve = sv.bump(1)
                ph.add("vector", lambda e, g=g, m=sv.h: e.tensor_copy(
                    kv_sb[:, g * 512:(g + 1) * 512],
                    self.PS[:, (2 + g) * 512:(3 + g) * 512]).then_inc(m, 1))
            t_kv_done = sv.v

            d_ub = self.dsem("ubs")
            ubuf = sb67("ubuf", [128, 2 * 512], BF16)
            uspill = {}

            def u_emit(ph, g, ts, b, tmv, i):
                up = i % 2
                self.wait(ph, "scalar", mm, tmv)
                if i >= 2:
                    self.wait(ph, "scalar", d_ub, uspill[i - 2])
                tsl = ss.bump(1)
                ph.add("scalar", lambda e, b=b, up=up, m=ss.h: e.activation(
                    ubuf[:, up * 512:(up + 1) * 512], self.bank(b), AF.Silu).then_inc(m, 1))
                self.wait(ph, "sync", ss, tsl)
                tu = self.dma(ph, "sync", d_ub,
                              self.ubf[ts * 128:(ts + 1) * 128, g * 512:(g + 1) * 512],
                              ubuf[:, up * 512:(up + 1) * 512])
                uspill[i] = tu
                return (ss, tsl)

            proj_pass(ph, "L1_attn_Wu", browu, u_emit, d_wu3, (d_bru, d_bru.v))
            t_u_done = d_ub.v
            ph.emit()
            sc67.close()
            if self.phase_done():
                sc6.close()
                return

            ph = Phase(nc)
            y3_sb = sb6("y3_sb", [128, DM], F32)
            sq3 = sb6("sq3", [128, DM], F32)
            stat3 = sb6("stat3", [128, 8], F32)
            sc8 = contextlib.ExitStack()
            sb8 = lambda name, shape, dt: sc8.enter_context(nc.sbuf_tensor(name, shape, dt))
            yn_sb = sb8("yn_sb", [128, DM], F32)
            mixt = sb8("mixt", [128, DM], BF16)
            ut = sb8("ut", [128, 2 * DM], BF16)
            d_ut = [self.dsem("ut0"), self.dsem("ut1")]
            lrow = sb8("lrow", [1, 2 * DM], F32)
            Af32 = A[:, 0:8192].bitcast(F32)
            gbc = Af32[:, 0:DM]
            bbc = Af32[:, DM:2 * DM]
            d_qfm = self.dsem("qfm")
            d_lnr = self.dsem("lnr")
            self.dma(ph, "sync", d_lnr, lrow[:, 0:DM], w["L1_attn_ln_g"].ap().unsqueeze(0))
            self.dma(ph, "sync", d_lnr, lrow[:, DM:2 * DM], w["L1_attn_ln_b"].ap().unsqueeze(0))
            self.wait(ph, "sync", sv, t_kv_done)
            for dc in range(NK):
                self.dmaT(ph, "sync", d_qfm, B[:, dc * S:(dc + 1) * S],
                          self.kbf[:, dc * 128:(dc + 1) * 128])
            self.wait(ph, "tensor", d_lnr, d_lnr.v)
            for which in range(2):
                for gg in range(4):
                    bk = which * 4 + gg
                    upd = (mm.h, 1) if (which == 1 and gg == 3) else None
                    if upd:
                        mm.bump(1)
                    ph.add("tensor", lambda e, bk=bk, which=which, gg=gg, upd=upd: e.matmul(
                        self.bank(bk), self.onesr_f32[0:1, :],
                        lrow[0:1, which * DM + gg * 512: which * DM + (gg + 1) * 512],
                        start=True, stop=True).then_maybe_inc(upd))
            self.wait(ph, "vector", mm, mm.v)
            tgb = self.vchain(ph, "vector", [
                lambda e: e.tensor_copy(gbc, self.PS[:, 0:2048]),
                lambda e: e.tensor_copy(bbc, self.PS[:, 2048:4096]),
            ])
            brow_o3, d_bro3, _ = self.bias_row(ph, sc8, "brow_o3", "L1_attn_bo", "bro3")
            d_mix = self.dsem("mixs")
            self.wait(ph, "tensor", d_qfm, d_qfm.v)
            self.wait(ph, "tensor", sv, tgb)
            self.wait(ph, "tensor", sv, t_kv_done)
            mix_done = {}
            for ts in range(NT):
                if ts >= 1:
                    self.wait(ph, "tensor", sv, mix_done[ts - 1])
                for h in range(H):
                    bk = 4 + h // 4
                    upd = (mm.h, 1) if h == H - 1 else None
                    if h == H - 1:
                        mm.bump(1)
                    ph.add("tensor", lambda e, h=h, bk=bk, ts=ts, upd=upd: e.matmul(
                        self.PS[:, bk * 512 + (h % 4) * 128: bk * 512 + (h % 4 + 1) * 128],
                        B[:, h * S + ts * 128: h * S + ts * 128 + 128],
                        kv_sb[:, h * 128:(h + 1) * 128],
                        start=True, stop=True).then_maybe_inc(upd))
                t_out = mm.v
                self.wait(ph, "vector", mm, t_out)
                sv.bump(1)
                ph.add("vector", lambda e, m=sv.h: e.memset(stat3[:, 0:2], 0.0).then_inc(m, 1))
                self.wait(ph, "vector", sv, sv.v)
                tev = sv.bump(1)
                ph.add("vector", lambda e, m=sv.h: e.tensor_scalar(
                    y3_sb[:, :], self.PS[:, 4 * 512:8 * 512], 1.0, 0.0, ALU.mult,
                    ALU.add, accum_out=stat3[:, 0:1]).then_inc(m, 1))
                self.wait(ph, "scalar", sv, tev)
                tsq = ss.bump(1)
                ph.add("scalar", lambda e, m=ss.h: e.activation(
                    sq3[:, :], y3_sb[:, :], AF.Square, accum_out=stat3[:, 1:2]).then_inc(m, 1))
                self.wait(ph, "vector", ss, tsq)
                if ts == 0:
                    self.wait(ph, "vector", sv, tgb)
                tln = self.ln_tail(ph, y3_sb[:, :], stat3, yn_sb[:, :])
                upar = ts % 2
                if ts == 0:
                    self.wait(ph, "gpsimd", d_ub, t_u_done)
                if ts >= 2:
                    self.wait(ph, "gpsimd", sv, mix_done[ts - 2])
                tuld = self.dma(ph, "gpsimd", d_ut[upar], ut[:, upar * DM:(upar + 1) * DM],
                                self.ubf[ts * 128:(ts + 1) * 128, :])
                self.wait(ph, "vector", sv, tln)
                self.wait(ph, "vector", d_ut[upar], tuld)
                tmx = self.vchain(ph, "vector", [
                    lambda e: e.tensor_tensor(yn_sb[:, :], yn_sb[:, :], gbc, ALU.mult),
                    lambda e: e.tensor_tensor(yn_sb[:, :], yn_sb[:, :], bbc, ALU.add),
                    lambda e, upar=upar: e.tensor_tensor(mixt[:, :], yn_sb[:, :],
                                                         ut[:, upar * DM:(upar + 1) * DM], ALU.mult),
                ])
                mix_done[ts] = tmx
                self.wait(ph, "sync", sv, tmx)
                tms = self.dma(ph, "sync", d_mix, self.mixbf[ts * 128:(ts + 1) * 128, :], mixt[:, :])
                self.wait(ph, "vector", d_mix, tms)
            ph.emit()
            sc8.close()
            if self.phase_done():
                sc6.close()
                return

            ph = Phase(nc)
            x2ft = sb6("x2ft", [128, 2 * DM], F32)
            x3t = sb6("x3t", [128, DM], F32)
            brow_o9, d_bro9, _ = self.bias_row(ph, sc6, "brow_o9", "L1_attn_bo", "bro9")
            d_mfm = self.dsem("mfm")
            for dc in range(NK):
                self.dmaT(ph, "sync", d_mfm, B[:, dc * S:(dc + 1) * S],
                          self.mixbf[:, dc * 128:(dc + 1) * 128])
            d_wo3 = [self.dsem("wo30"), self.dsem("wo31")]
            d_x2f = [self.dsem("x2f0"), self.dsem("x2f1")]
            dx3 = self.dsem("x3s")
            dx3b = self.dsem("x3b")
            self.wait(ph, "tensor", d_mfm, d_mfm.v)
            self.wait(ph, "tensor", d_bro9, d_bro9.v)
            ln_done9 = {}
            mm_ts9 = {}
            wof = A
            d_wof = self.dsem("wof")
            for g in range(4):
                self.dma(ph, "gpsimd", d_wof,
                         wof[:, g * NK * 512:(g + 1) * NK * 512].rearrange("p (kc n) -> p kc n", n=512),
                         w["L1_attn_Wo"].ap().rearrange("(kc p) n -> p kc n", p=128)[:, :, g * 512:(g + 1) * 512])
            self.wait(ph, "tensor", d_wof, d_wof.v)
            for ts in range(NT):
                par = ts % 2
                if ts >= 2:
                    self.wait(ph, "scalar", sv, ln_done9[ts - 2])
                tx2 = self.dma(ph, "scalar", d_x2f[par], x2ft[:, par * DM:(par + 1) * DM],
                               self.x2f.ap()[ts * 128:(ts + 1) * 128, :])
                for g in range(4):
                    bk = 4 + g
                    for kc in range(NK):
                        ph.add("tensor", lambda e, bk=bk, kc=kc, g=g, ts=ts: e.matmul(
                            self.bank(bk), B[:, kc * S + ts * 128: kc * S + ts * 128 + 128],
                            wof[:, g * NK * 512 + kc * 512: g * NK * 512 + (kc + 1) * 512],
                            start=(kc == 0), stop=False))
                    upd = (mm.h, 1) if g == 3 else None
                    ph.add("tensor", lambda e, bk=bk, g=g, upd=upd: e.matmul(
                        self.bank(bk), self.ones1_bf[0:1, 0:128],
                        brow_o9[0:1, g * 512:(g + 1) * 512], start=False, stop=True).then_maybe_inc(upd))
                mm_ts9[ts] = mm.bump(1)
                self.wait(ph, "vector", mm, mm_ts9[ts])
                self.wait(ph, "vector", d_x2f[par], tx2)
                sv.bump(1)
                ph.add("vector", lambda e, m=sv.h: e.memset(stat3[:, 0:2], 0.0).then_inc(m, 1))
                self.wait(ph, "vector", sv, sv.v)
                tev = sv.bump(1)
                ph.add("vector", lambda e, par=par, m=sv.h: e.scalar_tensor_tensor(
                    y3_sb[:, :], self.PS[:, 4 * 512:8 * 512], 1.0,
                    x2ft[:, par * DM:(par + 1) * DM], ALU.mult, ALU.add,
                    accum_out=stat3[:, 0:1]).then_inc(m, 1))
                self.wait(ph, "tensor", sv, tev)
                self.wait(ph, "scalar", sv, tev)
                tsq = ss.bump(1)
                ph.add("scalar", lambda e, m=ss.h: e.activation(
                    sq3[:, :], y3_sb[:, :], AF.Square, accum_out=stat3[:, 1:2]).then_inc(m, 1))
                self.wait(ph, "vector", ss, tsq)
                tln = self.ln_tail(ph, y3_sb[:, :], stat3, x3t[:, :])
                ln_done9[ts] = tln
                self.wait(ph, "sync", sv, tln)
                self.wait(ph, "gpsimd", sv, tln)
                t_f32 = self.dma(ph, "sync", dx3, self.x3f.ap()[ts * 128:(ts + 1) * 128, :], x3t[:, :])
                t_bf = self.dma(ph, "gpsimd", dx3b, self.x3bf.ap()[ts * 128:(ts + 1) * 128, :], x3t[:, :])
                self.wait(ph, "vector", dx3, t_f32)
                self.wait(ph, "vector", dx3b, t_bf)
            ph.emit()
            sc6.close()
            if self.phase_done():
                return

    def stage4(self):
        nc = self.nc
        mm, sv, ss = self.sem["mm"], self.sem["sv"], self.sem["ss"]
        w = self.w
        NP = 4
        TP = S // NP
        C1 = np.float32(1.0 / (np.sqrt(2048.0) * 64.0))
        C2 = np.float32(1.0 / (np.sqrt(8192.0) * 64.0))
        dr = nc.dram_tensor
        N1 = dr("wN1", [INTER * DM], BF16, kind="Internal")
        N2 = dr("wN2", [INTER * DM], BF16, kind="Internal")
        T1 = dr("wT1f", [INTER * DM], BF16, kind="Internal")
        T2 = dr("wT2f", [INTER * DM], BF16, kind="Internal")
        self.wupbf = N1.ap().rearrange("(a b) -> a b", b=DM)
        self.wgatebf = N2.ap().rearrange("(a b) -> a b", b=DM)
        self.wdownbf = N1.ap().rearrange("(a b) -> a b", b=INTER)
        self.wupT = T1.ap().rearrange("(a b) -> a b", b=INTER)
        self.wgateT = T2.ap().rearrange("(a b) -> a b", b=INTER)
        self.wdownT = N2.ap().rearrange("(a b) -> a b", b=DM)
        kindd = lambda n: "ExternalOutput" if n in self.debug_taps else "Internal"
        if self.debug_taps:
            self.xtd = dr("xtd", [NK, 128, S], BF16, kind=kindd("xtd")).ap()
        else:
            self.xtd = self.ctxd.ap().rearrange("h s c -> (h s c)").rearrange("(k p s) -> k p s", p=128, s=S)

        with contextlib.ExitStack() as sc:
            sb = lambda name, shape, dt: sc.enter_context(nc.sbuf_tensor(name, shape, dt))
            HSU2 = sb("HSU2", [128, NFC * 128], BF16)
            HSU1 = sb("HSU1", [128, NK * 128], BF16)
            wsu_col = sb("wsu_col", [128, 128], F32)
            wsdpb = sb("wsdpb", [128, DM], F32)

            ph = Phase(nc)
            sc10 = contextlib.ExitStack()
            sb10 = lambda name, shape, dt: sc10.enter_context(nc.sbuf_tensor(name, shape, dt))
            d_wbf = self.dsem("wbf")
            t_up_cast = self.dma(ph, "gpsimd", d_wbf, self.wupbf, w["L1_ffn_W_up"].ap())
            t_gate_cast = self.dma(ph, "gpsimd", d_wbf, self.wgatebf, w["L1_ffn_W_gate"].ap())
            sus = sb10("sus", [64, 128], F32)
            sug = sb10("sug", [16, 128], F32)
            wss = sb10("wss", [128, 128], F32)
            wsd = sb10("wsd", [1, DM], F32)
            sucol1 = sb10("sucol1", [128, 16], F32)
            sucol2 = sb10("sucol2", [128, 64], F32)
            d_s4c = self.dsem("s4c")
            self.dma(ph, "sync", d_s4c, sug[:, :], w["L1_ffn_SU_ug"].ap().rearrange("(a b) -> a b", b=128))
            self.dma(ph, "sync", d_s4c, sus[:, :], w["L1_ffn_SU_dp"].ap().rearrange("(a b) -> a b", b=128))
            self.dma(ph, "sync", d_s4c, wss[:, :], w["L1_ffn_Wscale_ug"].ap().rearrange("(a b) -> a b", b=128))
            self.dma(ph, "sync", d_s4c, wsd[:, :], w["L1_ffn_Wscale_dp"].ap().unsqueeze(0))
            self.wait(ph, "tensor", d_s4c, d_s4c.v)
            tv_list = []
            for (stg, width, bank) in ((sug, 16, 0), (sus, 64, 1), (wss, 128, 2)):
                tmv = mm.bump(1)
                ph.add("tensor", lambda e, stg=stg, width=width, bank=bank, m=mm.h: e.transpose(
                    self.PS[0:128, bank * 512: bank * 512 + width],
                    stg[:, :], self.ident_f32[0:stg.shape[0], 0:stg.shape[0]]).then_inc(m, 1))
            self.wait(ph, "vector", mm, mm.v)
            t7 = self.vchain(ph, "vector", [
                lambda e: e.tensor_copy(sucol1[:, :], self.PS[:, 0:16]),
                lambda e: e.tensor_copy(sucol2[:, :], self.PS[:, 512:512 + 64]),
                lambda e: e.tensor_scalar(wsu_col[:, :], self.PS[:, 1024:1024 + 128], 64.0, None, ALU.mult),
                lambda e: e.tensor_scalar(wsd[0:1, :], wsd[0:1, :], 64.0, None, ALU.mult),
            ])
            self.wait(ph, "tensor", sv, t7)
            for gg in range(4):
                upd = (mm.h, 1) if gg == 3 else None
                if upd:
                    mm.bump(1)
                ph.add("tensor", lambda e, gg=gg, upd=upd: e.matmul(
                    self.bank(4 + gg), self.onesr_f32[0:1, :],
                    wsd[0:1, gg * 512:(gg + 1) * 512], start=True, stop=True).then_maybe_inc(upd))
            self.wait(ph, "vector", mm, mm.v)
            t8 = self.vchain(ph, "vector", [
                lambda e: e.tensor_copy(wsdpb[:, :], self.PS[:, 4 * 512:8 * 512]),
            ])
            hs_ops = []
            for dc in range(NK):
                hs_ops.append(lambda e, dc=dc: e.tensor_scalar(
                    HSU1[:, dc * 128:(dc + 1) * 128], self.h128[:, :],
                    sucol1[:, dc:dc + 1], None, ALU.mult))
            for fc in range(NFC):
                hs_ops.append(lambda e, fc=fc: e.tensor_scalar(
                    HSU2[:, fc * 128:(fc + 1) * 128], self.h128[:, :],
                    sucol2[:, fc:fc + 1], None, ALU.mult))
            self.wait(ph, "vector", sv, t8)
            for i, op in enumerate(hs_ops):
                if i == len(hs_ops) - 1:
                    tHSU = sv.bump(1)
                    ph.add("vector", lambda e, op=op, m=sv.h: op(e).then_inc(m, 1))
                else:
                    ph.add("vector", lambda e, op=op: op(e))
            wstg = sb10("wstg", [128, 2 * INTER], BF16)
            d_wT = [self.dsem("wT0"), self.dsem("wT1")]
            d_wTs = [self.dsem("wTs0"), self.dsem("wTs1")]
            self.wait(ph, "sync", d_wbf, t_up_cast)
            last_store = {0: None, 1: None}
            jobs = [(self.wupbf, self.wupT, kc, INTER, 0) for kc in range(NK)]
            jobs += [(self.wgatebf, self.wgateT, kc, INTER, 1) for kc in range(NK)]
            jobs += [(self.wdownbf, self.wdownT, fc, DM, 2) for fc in range(NFC)]
            t_wup_jobs = None
            for j, (src_, dst_, kc, width, grp) in enumerate(jobs):
                par = j % 2
                if grp == 1 and kc == 0:
                    self.wait(ph, "sync", d_wbf, t_gate_cast)
                if grp == 2 and kc == 0:
                    self.wait(ph, "gpsimd", d_wT[0], t_wup_reads[0])
                    self.wait(ph, "gpsimd", d_wT[1], t_wup_reads[1])
                    t_dn_cast = self.dma(ph, "gpsimd", d_wbf, self.wdownbf, w["L1_ffn_W_down"].ap())
                    self.wait(ph, "sync", d_wbf, t_dn_cast)
                if last_store[par] is not None:
                    self.wait(ph, "sync", d_wTs[par], last_store[par])
                t1 = self.dmaT(ph, "sync", d_wT[par], wstg[:, par * INTER: par * INTER + width],
                               src_[:, kc * 128:(kc + 1) * 128])
                self.wait(ph, "sync", d_wT[par], t1)
                if grp == 0 and kc >= NK - 2:
                    if t_wup_jobs is None:
                        t_wup_jobs = {}
                    t_wup_jobs[par] = t1
                last_store[par] = self.dma(ph, "sync", d_wTs[par],
                                           dst_[kc * 128:(kc + 1) * 128, :],
                                           wstg[:, par * INTER: par * INTER + width])
                if grp == 0 and kc == NK - 1:
                    t_wup_reads = {0: d_wT[0].v, 1: d_wT[1].v}
            ph.emit()
            sc10.close()
            if self.phase_done():
                return

            ph = Phase(nc)
            sc11 = contextlib.ExitStack()
            sb11 = lambda name, shape, dt: sc11.enter_context(nc.sbuf_tensor(name, shape, dt))
            B1 = sb11("s4B1", [128, NK * S], BF16)
            A1 = sb11("s4A1", [128, NK * S], BF16)
            d_xfm4 = self.dsem("xfm4")
            for dc in range(NK):
                self.dmaT(ph, "sync", d_xfm4, B1[:, dc * S:(dc + 1) * S],
                          self.x3bf.ap()[:, dc * 128:(dc + 1) * 128])
            self.wait(ph, "tensor", d_xfm4, d_xfm4.v)
            self.wait(ph, "tensor", sv, tHSU)
            ev_at = {}
            for dc in range(NK):
                for g in range(4):
                    i = dc * 4 + g
                    b = i % 2
                    if i >= 2:
                        self.wait(ph, "tensor", ss, ev_at[i - 2])
                    tmv = mm.bump(1)
                    ph.add("tensor", lambda e, b=b, dc=dc, g=g, m=mm.h: e.matmul(
                        self.bank(b), HSU1[:, dc * 128:(dc + 1) * 128],
                        B1[:, dc * S + g * 512: dc * S + (g + 1) * 512],
                        start=True, stop=True).then_inc(m, 1))
                    self.wait(ph, "scalar", mm, tmv)
                    tevs = ss.bump(1)
                    ph.add("scalar", lambda e, b=b, dc=dc, g=g, m=ss.h: e.activation(
                        A1[:, dc * S + g * 512: dc * S + (g + 1) * 512],
                        self.bank(b), AF.Identity, scale=C1).then_inc(m, 1))
                    ev_at[i] = tevs
            self.wait(ph, "vector", ss, ss.v)
            bufs = [A1, B1]
            tbf = None
            for s in range(4):
                blk = 1 << s
                src_t = bufs[s % 2]
                dst_t = bufs[(s + 1) % 2]
                msz = blk * S
                na = NK // (2 * blk)
                lowS = src_t[:, :].rearrange("p (a b m) -> p a b m", b=2, m=msz)[:, :, 0, :]
                highS = src_t[:, :].rearrange("p (a b m) -> p a b m", b=2, m=msz)[:, :, 1, :]
                lowD = dst_t[:, :].rearrange("p (a b m) -> p a b m", b=2, m=msz)[:, :, 0, :]
                highD = dst_t[:, :].rearrange("p (a b m) -> p a b m", b=2, m=msz)[:, :, 1, :]
                tbf = self.vchain(ph, "vector", [
                    lambda e, lowD=lowD, lowS=lowS, highS=highS: e.tensor_tensor(lowD, lowS, highS, ALU.add),
                    lambda e, highD=highD, lowS=lowS, highS=highS: e.tensor_tensor(highD, lowS, highS, ALU.subtract),
                ])
            self.wait(ph, "sync", sv, tbf)
            d_xt = self.dsem("xt")
            for dc in range(NK):
                self.dma(ph, "sync", d_xt, self.xtd[dc], A1[:, dc * S:(dc + 1) * S])
            self.wait(ph, "sync", d_xt, d_xt.v)
            ph.emit()
            sc11.close()
            if self.phase_done():
                return

            FG = 2
            for p in range(NP):
                ph = Phase(nc)
                scp = contextlib.ExitStack()
                sbp = lambda name, shape, dt: scp.enter_context(nc.sbuf_tensor(name, shape, dt))
                z1 = sbp("z1_%d" % p, [128, NFC * TP], BF16)
                xtp = sbp("xtp%d" % p, [128, NK * TP], BF16)
                wug = sbp("wug%d" % p, [128, 2 * 2 * NK * FG * 128], BF16)
                gbuf = sbp("gbuf%d" % p, [128, 2 * TP], BF16)
                ubuf2 = sbp("ubuf2%d" % p, [128, 2 * TP], BF16)
                hbuf = sbp("hbuf%d" % p, [128, 2 * TP], BF16)
                d_xtp = self.dsem("xtp")
                self.dma(ph, "sync", d_xtp, xtp[:, :].rearrange("p (k t) -> p k t", t=TP),
                         self.xtd[:, :, p * TP:(p + 1) * TP].rearrange("k q t -> q k t"))
                d_ug = [self.dsem("ug_0"), self.dsem("ug_1")]
                self.wait(ph, "tensor", d_xtp, d_xtp.v)
                WS = 2 * NK * FG * 128
                UGW = NK * FG * 128
                nfg = NFC // FG
                hz_done = {}
                ug_mm = {}
                for fg in range(nfg):
                    par = fg % 2
                    if fg >= 2:
                        self.wait(ph, "sync", mm, ug_mm[fg - 2])
                    tld = self.dma(ph, "sync", d_ug[par],
                                   wug[:, par * WS: par * WS + UGW].rearrange("p (k f) -> p k f", f=FG * 128),
                                   self.wupT.rearrange("(k q) f -> q k f", q=128)[:, :, fg * FG * 128:(fg + 1) * FG * 128])
                    tld2 = self.dma(ph, "sync", d_ug[par],
                                    wug[:, par * WS + UGW: par * WS + 2 * UGW].rearrange("p (k f) -> p k f", f=FG * 128),
                                    self.wgateT.rearrange("(k q) f -> q k f", q=128)[:, :, fg * FG * 128:(fg + 1) * FG * 128])
                    self.wait(ph, "tensor", d_ug[par], tld2)
                    for sub in range(FG):
                        fc = fg * FG + sub
                        ib = fc % 2
                        if fc >= 2:
                            self.wait(ph, "tensor", sv, hz_done[fc - 2][0])
                            self.wait(ph, "tensor", ss, hz_done[fc - 2][1])
                        for which in range(2):
                            bnk = which * 2 + ib
                            for kc in range(NK):
                                upd = (mm.h, 1) if kc == NK - 1 else None
                                if upd:
                                    mm.bump(1)
                                ph.add("tensor", lambda e, bnk=bnk, kc=kc, which=which, par=par, sub=sub, upd=upd: e.matmul(
                                    self.bank(bnk),
                                    wug[:, par * WS + which * UGW + kc * FG * 128 + sub * 128:
                                        par * WS + which * UGW + kc * FG * 128 + (sub + 1) * 128],
                                    xtp[:, kc * TP:(kc + 1) * TP],
                                    start=(kc == 0), stop=(kc == NK - 1)).then_maybe_inc(upd))
                        t_upgate = mm.v
                        self.wait(ph, "scalar", mm, t_upgate)
                        hp = fc % 2
                        tung = self.vchain(ph, "scalar", [
                            lambda e, ib=ib, hp=hp, fc=fc: e.activation(
                                ubuf2[:, hp * TP:(hp + 1) * TP], self.bank(ib),
                                AF.Identity, scale=wsu_col[:, fc:fc + 1]),
                            lambda e, ib=ib, hp=hp, fc=fc: e.activation(
                                gbuf[:, hp * TP:(hp + 1) * TP], self.bank(2 + ib),
                                AF.Silu, scale=wsu_col[:, 64 + fc:64 + fc + 1]),
                        ])
                        self.wait(ph, "vector", ss, tung)
                        th = sv.bump(1)
                        ph.add("vector", lambda e, hp=hp, m=sv.h: e.tensor_tensor(
                            hbuf[:, hp * TP:(hp + 1) * TP], gbuf[:, hp * TP:(hp + 1) * TP],
                            ubuf2[:, hp * TP:(hp + 1) * TP], ALU.mult).then_inc(m, 1))
                        self.wait(ph, "tensor", sv, th)
                        tmz = mm.bump(1)
                        ph.add("tensor", lambda e, fc=fc, hp=hp, ib=ib, m=mm.h: e.matmul(
                            self.bank(4 + ib), HSU2[:, fc * 128:(fc + 1) * 128],
                            hbuf[:, hp * TP:(hp + 1) * TP],
                            start=True, stop=True).then_inc(m, 1))
                        self.wait(ph, "scalar", mm, tmz)
                        tz = ss.bump(1)
                        ph.add("scalar", lambda e, fc=fc, ib=ib, m=ss.h: e.activation(
                            z1[:, fc * TP:(fc + 1) * TP], self.bank(4 + ib),
                            AF.Identity, scale=C2).then_inc(m, 1))
                        hz_done[fc] = (th, tz)
                    ug_mm[fg] = mm.v
                self.wait(ph, "vector", ss, ss.v)
                tbf2 = None
                for s in range(6):
                    blk = 1 << s
                    msz = blk * TP
                    low = z1[:, :].rearrange("p (a b m) -> p a b m", b=2, m=msz)[:, :, 0, :]
                    high = z1[:, :].rearrange("p (a b m) -> p a b m", b=2, m=msz)[:, :, 1, :]
                    tbf2 = self.vchain(ph, "vector", [
                        lambda e, low=low, high=high: e.tensor_tensor(high, low, high, ALU.subtract),
                        lambda e, low=low, high=high: e.scalar_tensor_tensor(low, low, 2.0, high, ALU.mult, ALU.subtract),
                    ])
                scp2 = contextlib.ExitStack()
                sbp2 = lambda name, shape, dt: scp2.enter_context(nc.sbuf_tensor(name, shape, dt))
                yacc = sbp2("yacc_%d" % p, [128, 4 * DM], F32)
                x3ft = sbp2("x3ft_%d" % p, [128, DM], F32)
                y4 = sbp2("y4_%d" % p, [128, DM], F32)
                sq4 = xtp[:, 0:2 * DM].bitcast(F32)
                stat4 = sbp2("stat4_%d" % p, [128, 8], F32)
                d_wd = self.dsem("wd")
                d_x3f = [self.dsem("x3f_0"), self.dsem("x3f_1")]
                d_out = self.dsem("outs")
                self.wait(ph, "tensor", sv, tbf2)
                ev_dn = {}
                NW = 256
                for n5 in range(DM // NW):
                    if n5 == 0:
                        self.wait(ph, "sync", mm, ug_mm[nfg - 1])
                    else:
                        self.wait(ph, "sync", mm, dn_mm)
                    tldw = self.dma(ph, "sync", d_wd,
                                    wug[:, 0:NFC * NW].rearrange("p (k f) -> p k f", f=NW),
                                    self.wdownT.rearrange("(k q) f -> q k f", q=128)[:, :, n5 * NW:(n5 + 1) * NW])
                    self.wait(ph, "tensor", d_wd, tldw)
                    for st in range(4):
                        bnk = 4 + st
                        if n5 >= 1:
                            self.wait(ph, "tensor", sv, ev_dn[(n5 - 1, st)])
                        for fc in range(NFC):
                            upd = (mm.h, 1) if fc == NFC - 1 else None
                            if upd:
                                mm.bump(1)
                            ph.add("tensor", lambda e, bnk=bnk, fc=fc, st=st, upd=upd: e.matmul(
                                self.bank(bnk, NW), z1[:, fc * TP + st * 128: fc * TP + (st + 1) * 128],
                                wug[:, fc * NW:(fc + 1) * NW],
                                start=(fc == 0), stop=(fc == NFC - 1)).then_maybe_inc(upd))
                        dn_mm = mm.v
                        self.wait(ph, "vector", mm, dn_mm)
                        tevd = sv.bump(1)
                        ph.add("vector", lambda e, st=st, n5=n5, m=sv.h: e.tensor_tensor(
                            yacc[:, st * DM + n5 * NW: st * DM + (n5 + 1) * NW],
                            self.PS[:, (4 + st) * 512:(4 + st) * 512 + NW],
                            wsdpb[:, n5 * NW:(n5 + 1) * NW], ALU.mult).then_inc(m, 1))
                        ev_dn[(n5, st)] = tevd
                ln_done4 = {}
                for st in range(4):
                    ts = p * 4 + st
                    if st >= 1:
                        self.wait(ph, "scalar", sv, ln_done4[st - 1])
                    tx3 = self.dma(ph, "scalar", d_x3f[0], x3ft[:, :],
                                   self.x3f.ap()[ts * 128:(ts + 1) * 128, :])
                    self.wait(ph, "vector", d_x3f[0], tx3)
                    self.wait(ph, "vector", sv, ev_dn[(DM // NW - 1, st)])
                    sv.bump(1)
                    ph.add("vector", lambda e, m=sv.h: e.memset(stat4[:, 0:2], 0.0).then_inc(m, 1))
                    self.wait(ph, "vector", sv, sv.v)
                    tev = sv.bump(1)
                    ph.add("vector", lambda e, st=st, m=sv.h: e.scalar_tensor_tensor(
                        y4[:, :], yacc[:, st * DM:(st + 1) * DM], 1.0,
                        x3ft[:, :], ALU.mult, ALU.add,
                        accum_out=stat4[:, 0:1]).then_inc(m, 1))
                    self.wait(ph, "scalar", sv, tev)
                    tsq = ss.bump(1)
                    ph.add("scalar", lambda e, m=ss.h: e.activation(
                        sq4[:, 0:DM], y4[:, :], AF.Square, accum_out=stat4[:, 1:2]).then_inc(m, 1))
                    self.wait(ph, "vector", ss, tsq)
                    tln = self.ln_tail(ph, y4[:, :], stat4, y4[:, :])
                    ln_done4[st] = tln
                    self.wait(ph, "sync", sv, tln)
                    tso = self.dma(ph, "sync", d_out, self.out.ap()[ts * 128:(ts + 1) * 128, :], y4[:, :])
                    self.wait(ph, "vector", d_out, tso)
                    self.wait(ph, "scalar", d_out, tso)
                if p == NP - 1:
                    self.wait(ph, "sync", d_out, d_out.v)
                ph.emit()
                scp2.close()
                scp.close()

    def finish_partial(self, src_dram):
        nc = self.nc
        ld, st = self.dsem("finl"), self.dsem("fins")
        with contextlib.ExitStack() as sc:
            buf = sc.enter_context(nc.sbuf_tensor("fin", [128, DM], F32))
            ph = Phase(nc)
            for ts in range(NT):
                t1 = self.dma(ph, "sync", ld, buf[:, :], src_dram.ap()[ts * 128:(ts + 1) * 128, :])
                self.wait(ph, "sync", ld, t1)
                t2 = self.dma(ph, "sync", st, self.out.ap()[ts * 128:(ts + 1) * 128, :], buf[:, :])
                self.wait(ph, "sync", st, t2)
            ph.emit()

    def build(self):
        self.stage1()
        if self.max_phase < 99:
            self.stack.close()
            return self.nc
        if self.stages >= 2:
            self.stage2()
        if self.stages >= 3:
            self.stage3()
        if self.stages >= 4:
            self.stage4()
        else:
            src = {1: self.x1f, 2: self.x2f, 3: self.x3f}[self.stages]
            self.finish_partial(src)
        self.stack.close()
        return self.nc


def build_program(stages=4, debug_taps=(), max_phase=99):
    return Builder(stages=stages, debug_taps=debug_taps, max_phase=max_phase).build()


def make_in_maps(inputs, consts=None):
    consts = consts or host_constants()
    x = np.ascontiguousarray(inputs["L0_attn_x"])
    in_maps = []
    for core in range(8):
        m = {"x0": np.ascontiguousarray(x[core])}
        for name in WEIGHT_SPECS:
            m[name] = np.ascontiguousarray(inputs[name])
        for name in CONST_SPECS:
            m[name] = consts[name]
        in_maps.append(m)
    return in_maps


def kernel(**inputs):
    nc = build_program(stages=4)
    in_maps = make_in_maps(inputs)
    res = run_bass_kernel_spmd(nc, in_maps, core_ids=list(range(8)))
    return np.stack([r["out"] for r in res.results], axis=0)


# version 05
# speedup vs baseline: 1.1085x; 1.1085x over previous
import contextlib

import numpy as np
import ml_dtypes

import concourse.bass as bass
import concourse.mybir as mybir
from concourse.bass_utils import run_bass_kernel_spmd

F32 = mybir.dt.float32
BF16 = mybir.dt.bfloat16
AF = mybir.ActivationFunctionType
ALU = mybir.AluOpType
AX = mybir.AxisListType

S = 2048
DM = 2048
H = 16
HD = 128
NK = 16
NT = 16
NE = 8
INTER = 8192
NFC = 64
ALIBI_SHIFT = 14.0
INV_SQRT_HD = np.float32(1.0 / np.sqrt(np.float32(HD)))
LN_EPS = np.float32(1e-5)
VW = H * 129

ENGINES = ("tensor", "vector", "scalar", "sync", "gpsimd")
NO_AV_PIPELINE = False


class Sem:
    def __init__(self, nc, stack, name):
        self.h = stack.enter_context(nc.semaphore(name))
        self.v = 0

    def bump(self, k):
        self.v += k
        return self.v


class Phase:

    def __init__(self, nc):
        self.nc = nc
        self.ops = {e: [] for e in ENGINES}

    def add(self, eng, fn):
        self.ops[eng].append(fn)

    def emit(self):
        with self.nc.Block() as blk:
            for name in ENGINES:
                fns = self.ops[name]
                if fns:
                    getattr(blk, name)(lambda eng, fns=fns: [f(eng) for f in fns])
        self.ops = {e: [] for e in ENGINES}


def host_constants():
    c = {}
    c["ident_bf"] = np.eye(128, dtype=ml_dtypes.bfloat16)
    c["ident_f32"] = np.eye(128, dtype=np.float32)
    ratio = np.float32(2.0 ** (-8.0 / H))
    slopes = ratio ** np.arange(1, H + 1, dtype=np.float32)
    al = np.zeros((128, H * NT), dtype=np.float32)
    p = np.arange(128, dtype=np.float32)
    for h in range(H):
        for d in range(NT):
            al[:, h * NT + d] = slopes[h] * (p - 128.0 * d - 64.0) - 6.0
    c["alibi"] = al
    p = np.arange(128)[:, None]
    f = np.arange(128)[None, :]
    c["trimask"] = (p <= f).astype(ml_dtypes.bfloat16)
    hm = np.array([[1.0]])
    while hm.shape[0] < 128:
        hm = np.block([[hm, hm], [hm, -hm]])
    c["h128"] = hm.astype(ml_dtypes.bfloat16)
    return c


CONST_SPECS = {
    "ident_bf": ([128, 128], BF16),
    "ident_f32": ([128, 128], F32),
    "alibi": ([128, H * NT], F32),
    "trimask": ([128, 128], BF16),
    "h128": ([128, 128], BF16),
}

WEIGHT_SPECS = {
    "L0_attn_Wq": ([DM, DM], F32), "L0_attn_bq": ([DM], F32),
    "L0_attn_Wk": ([DM, DM], F32), "L0_attn_bk": ([DM], F32),
    "L0_attn_Wv": ([DM, DM], F32), "L0_attn_bv": ([DM], F32),
    "L0_attn_Wo": ([DM, DM], F32), "L0_attn_bo": ([DM], F32),
    "L0_ffn_switch_w": ([NE, S * DM], F32), "L0_ffn_switch_b": ([NE], F32),
    "L0_ffn_w1": ([NE, 128, 128], F32), "L0_ffn_b1": ([NE, 128], F32),
    "L0_ffn_w2": ([NE, 128, 128], F32), "L0_ffn_b2": ([NE, 128], F32),
    "L0_ffn_w3": ([DM, DM], F32), "L0_ffn_b3": ([DM], F32),
    "L1_attn_Wqk": ([DM, DM], F32), "L1_attn_bqk": ([DM], F32),
    "L1_attn_Wv": ([DM, DM], F32), "L1_attn_bv": ([DM], F32),
    "L1_attn_Wu": ([DM, DM], F32), "L1_attn_bu": ([DM], F32),
    "L1_attn_Wo": ([DM, DM], F32), "L1_attn_bo": ([DM], F32),
    "L1_attn_ln_g": ([DM], F32), "L1_attn_ln_b": ([DM], F32),
    "L1_ffn_SU_ug": ([DM], F32), "L1_ffn_SU_dp": ([INTER], F32),
    "L1_ffn_Wscale_ug": ([2 * INTER], F32), "L1_ffn_Wscale_dp": ([DM], F32),
    "L1_ffn_W_up": ([INTER, DM], F32), "L1_ffn_W_gate": ([INTER, DM], F32),
    "L1_ffn_W_down": ([DM, INTER], F32),
}


class Builder:
    def __init__(self, stages=4, debug_taps=(), max_phase=99):
        self.max_phase = max_phase
        self._phase_no = 0
        self.stages = stages
        self.debug_taps = set(debug_taps)
        self.nc = bass.Bass("TRN2", target_bir_lowering=False, detect_race_conditions=False)
        self.stack = contextlib.ExitStack()
        nc = self.nc
        self.x0 = nc.dram_tensor("x0", [S, DM], F32, kind="ExternalInput")
        self.out = nc.dram_tensor("out", [S, DM], F32, kind="ExternalOutput")
        self.w = {n: nc.dram_tensor(n, sh, dt, kind="ExternalInput")
                  for n, (sh, dt) in WEIGHT_SPECS.items()}
        self.c = {n: nc.dram_tensor(n, sh, dt, kind="ExternalInput")
                  for n, (sh, dt) in CONST_SPECS.items()}
        kind = lambda n: "ExternalOutput" if n in self.debug_taps else "Internal"
        dr = nc.dram_tensor
        dbg = bool(self.debug_taps)
        if dbg:
            self.x0bf = dr("x0bf", [S, DM], BF16, kind=kind("x0bf"))
            self.x1f = dr("x1f", [S, DM], F32, kind=kind("x1f"))
            self.x1bf = dr("x1bf", [S, DM], BF16, kind=kind("x1bf"))
            self.x2f = dr("x2f", [S, DM], F32, kind=kind("x2f"))
            self.x2bf = dr("x2bf", [S, DM], BF16, kind=kind("x2bf"))
            self.x3f = dr("x3f", [S, DM], F32, kind=kind("x3f"))
            self.x3bf = dr("x3bf", [S, DM], BF16, kind=kind("x3bf"))
        else:
            xres = dr("xres", [S, DM], F32, kind="Internal")
            xbf = dr("xbf", [S, DM], BF16, kind="Internal")
            self.x0bf = xbf
            self.x1f = self.x2f = self.x3f = xres
            self.x1bf = self.x2bf = self.x3bf = xbf
        self.ctxd = dr("ctxd", [H, S, HD], BF16, kind=kind("ctxd"))
        if "vtmd" in self.debug_taps:
            self.vtmd = dr("vtmd", [128, NT * VW], BF16, kind="ExternalOutput")
        if "qkd" in self.debug_taps:
            self.qkd = dr("qkd", [2, 128, 2 * S], BF16, kind="ExternalOutput")
        if "expd" in self.debug_taps:
            self.expd = dr("expd", [16, 128, 128], BF16, kind="ExternalOutput")
        if "accq" in self.debug_taps:
            self.accq = dr("accq", [2, NT, 128, 129], F32, kind="ExternalOutput")
        if "ysd" in self.debug_taps:
            self.ysd = dr("ysd", [NT, 128, DM], F32, kind="ExternalOutput")
            self.std = dr("std", [NT, 128, 8], F32, kind="ExternalOutput")
        if "accd" in self.debug_taps:
            self.accd = dr("accd", [128, 129], F32, kind="ExternalOutput")
        self.ctxdF = dr("ctxdF", [H, HD, S], BF16, kind=kind("ctxdF"))
        al = nc.alloc_sbuf_tensor
        self.ident_bf = al("identbf", [128, 128], BF16)
        self.ident_f32 = al("identf32", [128, 128], F32)
        self.alibi = al("alibisb", [128, H * NT], F32)
        self.trimask = al("trimasksb", [128, 128], BF16)
        self.h128 = al("h128sb", [128, 128], BF16)
        self.ones1_bf = al("ones1bf", [1, 512], BF16)
        self.ones1_f32 = al("ones1f32", [128, 1], F32)
        self.onesr_f32 = al("onesrf32", [1, 128], F32)
        self.PS = nc.alloc_psum_tensor("PS", [128, 4096], F32)
        self.wstg_bg = al("wstgbg", [128, INTER // 2], BF16)
        self.sem = {}
        for name in ("mm", "sv", "ss", "sg"):
            self.sem[name] = Sem(nc, self.stack, "s_" + name)
        self._consts_loaded = False
        self._bg4 = None

    def phase_done(self):
        self._phase_no += 1
        return self._phase_no >= self.max_phase

    def bank(self, i, width=512):
        assert 0 <= i < 8
        return self.PS[:, i * 512:i * 512 + width]

    def dma(self, ph, eng, sem, dst, src, **kw):
        tgt = sem.bump(16)
        ph.add(eng, lambda e, d=dst, s=src, m=sem.h, kw=kw:
               e.dma_start(d, s, **kw).then_inc(m, 16))
        return tgt

    def dmaT(self, ph, eng, sem, dst, src):
        tgt = sem.bump(16)
        ph.add(eng, lambda e, d=dst, s=src, m=sem.h:
               e.dma_start_transpose(d, s).then_inc(m, 16))
        return tgt

    def wait(self, ph, eng, sem, val):
        ph.add(eng, lambda e, m=sem.h, v=val: e.wait_ge(m, v))

    def mmdone(self, ph):
        mm = self.sem["mm"]
        t = mm.bump(1)
        ph.add("tensor", lambda e, m=mm.h: e.nop().then_inc(m, 1))
        return t

    def prep_bg4(self):
        nc = self.nc
        dr = nc.dram_tensor
        N1 = dr("wN1", [INTER * DM], BF16, kind="Internal")
        N2 = dr("wN2", [INTER * DM], BF16, kind="Internal")
        T1 = dr("wT1f", [INTER * DM], BF16, kind="Internal")
        T2 = dr("wT2f", [INTER * DM], BF16, kind="Internal")
        self.wupbf = N1.ap().rearrange("(a b) -> a b", b=DM)
        self.wgatebf = N2.ap().rearrange("(a b) -> a b", b=DM)
        self.wdownbf = N1.ap().rearrange("(a b) -> a b", b=INTER)
        self.wupT = T1.ap().rearrange("(a b) -> a b", b=INTER)
        self.wgateT = T2.ap().rearrange("(a b) -> a b", b=INTER)
        self.wdownT = N2.ap().rearrange("(a b) -> a b", b=DM)
        d_wbf = self.dsem("wbf")
        d_wT = self.dsem("wTl")
        d_wTs = self.dsem("wTsl")
        wstg = self.wstg_bg
        jobs = []
        RC = INTER // 16

        def cast_chunk(wname, dst, r0, rows):
            def f(ph):
                self.dma(ph, "gpsimd", d_wbf, dst[r0:r0 + rows, :],
                         self.w[wname].ap()[r0:r0 + rows, :])
            return f

        def tjob(srcv, dstv, kc, width, wait_cast_to):
            def f(ph):
                self.wait(ph, "sync", d_wbf, wait_cast_to)
                for h0 in range(0, width, INTER // 2):
                    hw = min(INTER // 2, width - h0)
                    if d_wTs.v > 0:
                        self.wait(ph, "sync", d_wTs, d_wTs.v)
                    t1 = self.dmaT(ph, "sync", d_wT, wstg[:, 0:hw],
                                   srcv[h0:h0 + hw, kc * 128:(kc + 1) * 128])
                    self.wait(ph, "sync", d_wT, t1)
                    self.dma(ph, "sync", d_wTs,
                             dstv[kc * 128:(kc + 1) * 128, h0:h0 + hw],
                             wstg[:, 0:hw])
            return f

        for i in range(16):
            jobs.append(cast_chunk("L1_ffn_W_up", self.wupbf, i * RC, RC))
        for i in range(16):
            jobs.append(cast_chunk("L1_ffn_W_gate", self.wgatebf, i * RC, RC))
        for kc in range(NK):
            jobs.append(tjob(self.wupbf, self.wupT, kc, INTER, 16 * 16))
        for kc in range(NK):
            jobs.append(tjob(self.wgatebf, self.wgateT, kc, INTER, 32 * 16))

        def dn_cast_chunk(r0):
            def f(ph):
                self.wait(ph, "gpsimd", d_wTs, d_wTs.v)
                self.dma(ph, "gpsimd", d_wbf, self.wdownbf[r0:r0 + 256, :],
                         self.w["L1_ffn_W_down"].ap()[r0:r0 + 256, :])
            return f

        for i in range(8):
            jobs.append(dn_cast_chunk(i * 256))
        for fc in range(NFC):
            jobs.append(tjob(self.wdownbf, self.wdownT, fc, DM, 40 * 16))
        self._bg4 = jobs
        self._bg4_sems = (d_wbf, d_wT, d_wTs)

    def pump_bg4(self, ph, n):
        if not self._bg4:
            return
        for _ in range(min(n, len(self._bg4))):
            self._bg4.pop(0)(ph)

    def drain_bg4(self, ph):
        self.pump_bg4(ph, 10 ** 9)

    def dsem(self, name):
        if name not in self.sem:
            self.sem[name] = Sem(self.nc, self.stack, "d_" + name)
        return self.sem[name]

    def load_consts(self, ph):
        dc = self.dsem("cst")
        for name, dst in (("ident_bf", self.ident_bf), ("ident_f32", self.ident_f32),
                          ("alibi", self.alibi), ("trimask", self.trimask),
                          ("h128", self.h128)):
            self.dma(ph, "sync", dc, dst[:, :], self.c[name].ap())
        ph.add("gpsimd", lambda e: e.memset(self.ones1_bf[:, :], 1.0))
        ph.add("gpsimd", lambda e: e.memset(self.ones1_f32[:, :], 1.0))
        ph.add("gpsimd", lambda e: e.memset(self.onesr_f32[:, :], 1.0))
        return dc

    def bias_row(self, ph, sc, name, wname, dsem_name="brow"):
        n = WEIGHT_SPECS[wname][0][0]
        buf = sc.enter_context(self.nc.sbuf_tensor(name, [1, n], BF16))
        ds = self.dsem(dsem_name)
        t = self.dma(ph, "gpsimd", ds, buf[:, :], self.w[wname].ap().unsqueeze(0))
        return buf, ds, t

    def bias_cols(self, ph, sc, name, wname, dc):
        nc = self.nc
        mm, sv = self.sem["mm"], self.sem["sv"]
        stg = sc.enter_context(nc.sbuf_tensor(name + "_stg", [16, 128], F32))
        col = sc.enter_context(nc.sbuf_tensor(name, [128, 16], F32))
        self.dma(ph, "sync", dc, stg[:, :],
                 self.w[wname].ap().rearrange("(a b) -> a b", b=128))
        self._pending_cols.append((stg, col))
        return col

    def flush_bias_cols(self, ph, dc):
        mm, sv = self.sem["mm"], self.sem["sv"]
        self.wait(ph, "tensor", dc, dc.v)
        out = 0
        for stg, col in self._pending_cols:
            tm = mm.bump(1)
            ph.add("tensor", lambda e, stg=stg, m=mm.h: e.transpose(
                self.bank(7, 16), stg[:, :], self.ident_f32[0:16, 0:16]).then_inc(m, 1))
            self.wait(ph, "vector", mm, tm)
            tv = sv.bump(1)
            ph.add("vector", lambda e, col=col, m=sv.h: e.tensor_copy(
                col[:, :], self.bank(7, 16)).then_inc(m, 1))
            self.wait(ph, "tensor", sv, tv)
            out = tv
        self._pending_cols = []
        return out

    def vchain(self, ph, eng, fns):
        sem = {"vector": self.sem["sv"], "scalar": self.sem["ss"], "tensor": self.sem["mm"],
               "gpsimd": self.sem["sg"]}[eng]
        t = None
        for i, fn in enumerate(fns):
            if t is not None:
                self.wait(ph, eng, sem, t)
            t = sem.bump(1)
            ph.add(eng, lambda e, fn=fn, m=sem.h: fn(e).then_inc(m, 1))
        return t

    def ln_tail(self, ph, y_sb, stat, out_dst):
        sv, ss = self.sem["sv"], self.sem["ss"]
        inv_n = 1.0 / DM
        A = ALU
        t1 = self.vchain(ph, "vector", [
            lambda e: e.tensor_scalar(stat[:, 2:3], stat[:, 0:1], inv_n, None, A.mult),
            lambda e: e.tensor_scalar(stat[:, 3:4], stat[:, 1:2], inv_n, None, A.mult),
            lambda e: e.tensor_tensor(stat[:, 4:5], stat[:, 2:3], stat[:, 2:3], A.mult),
            lambda e: e.tensor_tensor(stat[:, 4:5], stat[:, 3:4], stat[:, 4:5], A.subtract),
        ])
        self.wait(ph, "scalar", sv, t1)
        ta2 = ss.bump(1)
        ph.add("scalar", lambda e, m=ss.h: e.activation(
            stat[:, 5:6], stat[:, 4:5], AF.Sqrt, bias=np.float32(LN_EPS)).then_inc(m, 1))
        self.wait(ph, "vector", ss, ta2)
        tv = self.vchain(ph, "vector", [
            lambda e: e.reciprocal(stat[:, 6:7], stat[:, 5:6]),
            lambda e: e.tensor_tensor(stat[:, 7:8], stat[:, 6:7], stat[:, 6:7], A.mult),
            lambda e: e.tensor_scalar(stat[:, 4:5], stat[:, 4:5], float(LN_EPS), 0.5, A.add, A.mult),
            lambda e: e.tensor_tensor(stat[:, 7:8], stat[:, 4:5], stat[:, 7:8], A.mult),
            lambda e: e.tensor_scalar(stat[:, 7:8], stat[:, 7:8], -1.0, 1.5, A.mult, A.add),
            lambda e: e.tensor_tensor(stat[:, 6:7], stat[:, 6:7], stat[:, 7:8], A.mult),
            lambda e: e.tensor_scalar(out_dst, y_sb, stat[:, 2:3], stat[:, 6:7], A.subtract, A.mult),
        ])
        return tv

    def stage1(self):
        nc = self.nc
        mm, sv, ss = self.sem["mm"], self.sem["sv"], self.sem["ss"]
        self._pending_cols = []

        with contextlib.ExitStack() as sc:
            sb = lambda name, shape, dt: sc.enter_context(nc.sbuf_tensor(name, shape, dt))
            xfm = sb("xfm", [128, NK * S], BF16)
            vtm = sb("vtm", [128, NT * VW], BF16)

            ph = Phase(nc)
            dc = self.load_consts(ph)
            dg = self.dsem("g1")
            t_x0bf = self.dma(ph, "gpsimd", dg, self.x0bf.ap(), self.x0.ap())
            brow, d_brow, _ = self.bias_row(ph, sc, "brow_v", "L0_attn_bv")
            bqcol = self.bias_cols(ph, sc, "bqcol", "L0_attn_bq", dc)
            bkcol = self.bias_cols(ph, sc, "bkcol", "L0_attn_bk", dc)
            t_cols = self.flush_bias_cols(ph, dc)
            dxf = self.dsem("xfm")
            self.wait(ph, "sync", dg, t_x0bf)
            for dcc in range(NK):
                self.dmaT(ph, "sync", dxf, xfm[:, dcc * S:(dcc + 1) * S],
                          self.x0bf.ap()[:, dcc * 128:(dcc + 1) * 128])
            for ts in range(NT):
                vt = vtm[:, ts * VW:(ts + 1) * VW]
                ph.add("gpsimd", lambda e, vt=vt: e.memset(
                    vt.rearrange("p (h c) -> p h c", c=129)[:, :, 128:129], 1.0))
            ph.emit()
            if self.phase_done():
                return

            sc1 = contextlib.ExitStack()
            ph = Phase(nc)
            wvs = sc1.enter_context(nc.sbuf_tensor("wvs", [128, 2 * NK * 512], BF16))
            self.wait(ph, "tensor", dxf, dxf.v)
            self.wait(ph, "tensor", dc, dc.v)
            self.wait(ph, "tensor", d_brow, d_brow.v)
            dwv = [self.dsem("wv0"), self.dsem("wv1")]
            evac_at = {}
            mm_g = {}
            for g in range(4):
                par = g % 2
                buf = wvs[:, par * NK * 512:(par + 1) * NK * 512]
                if g >= 2:
                    self.wait(ph, "gpsimd", mm, mm_g[g - 2])
                twv = self.dma(
                    ph, "gpsimd", dwv[par], buf.rearrange("p (kc n) -> p kc n", n=512),
                    self.w["L0_attn_Wv"].ap().rearrange("(kc p) n -> p kc n", p=128)[:, :, g * 512:(g + 1) * 512])
                self.wait(ph, "tensor", dwv[par], twv)
                for ts in range(NT):
                    b = (g * NT + ts) % 2
                    if b in evac_at:
                        self.wait(ph, "tensor", sv, evac_at[b])
                    for kc in range(NK):
                        ph.add("tensor", lambda e, b=b, kc=kc, ts=ts, buf=buf: e.matmul(
                            self.bank(b),
                            xfm[:, kc * S + ts * 128: kc * S + ts * 128 + 128],
                            buf[:, kc * 512:(kc + 1) * 512],
                            start=(kc == 0), stop=False))
                    tmv = mm.bump(1)
                    ph.add("tensor", lambda e, b=b, g=g, m=mm.h: e.matmul(
                        self.bank(b), self.ones1_bf[0:1, 0:128],
                        brow[0:1, g * 512:(g + 1) * 512], start=False, stop=True).then_inc(m, 1))
                    self.wait(ph, "vector", mm, tmv)
                    vt = vtm[:, ts * VW:(ts + 1) * VW]
                    dst = vt.rearrange("p (h c) -> p h c", c=129)[:, 4 * g:4 * g + 4, 0:128]
                    tev = sv.bump(1)
                    ph.add("vector", lambda e, b=b, dst=dst, m=sv.h: e.tensor_copy(
                        dst, self.bank(b).rearrange("p (h c) -> p h c", c=128)).then_inc(m, 1))
                    evac_at[b] = tev
                mm_g[g] = mm.v
            ph.emit()
            sc1.close()
            if self.phase_done():
                return

            ph = Phase(nc)
            wqh = sc.enter_context(nc.sbuf_tensor("wqh", [128, 2 * NK * 128], BF16))
            wkh = sc.enter_context(nc.sbuf_tensor("wkh", [128, 2 * NK * 128], BF16))
            qh = sc.enter_context(nc.sbuf_tensor("qh", [128, 2 * S], BF16))
            kh = sc.enter_context(nc.sbuf_tensor("kh", [128, 2 * S], BF16))
            expb = sc.enter_context(nc.sbuf_tensor("expb", [128, 3 * 128], BF16))
            rcol = sc.enter_context(nc.sbuf_tensor("rcol", [128, 2], F32))
            ctx_sb = sc.enter_context(nc.sbuf_tensor("ctx_sb", [128, S], BF16))
            self.wait(ph, "scalar", dc, dc.v)
            self.wait(ph, "vector", dc, dc.v)
            if "accq" in self.debug_taps:
                self._dbg_accs = sc.enter_context(nc.sbuf_tensor("accs", [128, 129], F32))
            dwq = [self.dsem("wq0"), self.dsem("wq1")]
            dwk = [self.dsem("wk0"), self.dsem("wk1")]
            dcx = self.dsem("ctxs")
            mm_headqk = {}
            ctx_store = {}
            ctx_ev = {}
            evq_at = {}
            for h in range(H):
                par = h % 2
                if h >= 2:
                    self.wait(ph, "gpsimd", mm, mm_headqk[h - 2])
                if h >= 1:
                    self.pump_bg4(ph, 2)
                tq = self.dma(ph, "gpsimd", dwq[par],
                              wqh[:, par * NK * 128:(par + 1) * NK * 128].rearrange("p (kc n) -> p kc n", n=128),
                              self.w["L0_attn_Wq"].ap().rearrange("(kc p) n -> p kc n", p=128)[:, :, h * 128:(h + 1) * 128])
                tk = self.dma(ph, "gpsimd", dwk[par],
                              wkh[:, par * NK * 128:(par + 1) * NK * 128].rearrange("p (kc n) -> p kc n", n=128),
                              self.w["L0_attn_Wk"].ap().rearrange("(kc p) n -> p kc n", p=128)[:, :, h * 128:(h + 1) * 128])
                if h == 0:
                    self.wait(ph, "scalar", sv, t_cols)
                for wsl, dstt, dws, tw, bi, bcolx in ((wqh, qh, dwq[par], tq, 0, bqcol),
                                                      (wkh, kh, dwk[par], tk, 1, bkcol)):
                    self.wait(ph, "tensor", dws, tw)
                    for g in range(4):
                        if bi in evq_at:
                            self.wait(ph, "tensor", ss, evq_at[bi])
                        tmv = mm.bump(1)
                        for kc in range(NK):
                            ph.add("tensor", lambda e, b=bi, kc=kc, g=g, wsl=wsl, par=par, m=mm.h: e.matmul(
                                self.bank(b),
                                wsl[:, par * NK * 128 + kc * 128: par * NK * 128 + (kc + 1) * 128],
                                xfm[:, kc * S + g * 512: kc * S + (g + 1) * 512],
                                start=(kc == 0), stop=(kc == NK - 1)).then_maybe_inc((m, 1) if kc == NK - 1 else None))
                        self.wait(ph, "scalar", mm, tmv)
                        tev = ss.bump(1)
                        ph.add("scalar", lambda e, b=bi, g=g, dstt=dstt, par=par, bc=bcolx[:, h:h + 1], m=ss.h: e.activation(
                            dstt[:, par * S + g * 512: par * S + (g + 1) * 512],
                            self.bank(b), AF.Identity, bias=bc).then_inc(m, 1))
                        evq_at[bi] = tev
                self.wait(ph, "tensor", ss, ss.v)
                mm_headqk[h] = mm.v
                if h >= 1:
                    self.wait(ph, "scalar", dcx, ctx_store[h - 1])
                for qt in range(NT):
                    njt = qt + 1
                    cb = 2 + qt % 2
                    if cb in ctx_ev:
                        self.wait(ph, "tensor", ss, ctx_ev[cb])
                    av_pend = None
                    for jt in range(njt):
                        b = jt % 2
                        rot = jt % 3
                        tmv = mm.bump(1)
                        ph.add("tensor", lambda e, b=b, jt=jt, qt=qt, par=par, m=mm.h: e.matmul(
                            self.bank(b, 128),
                            kh[:, par * S + jt * 128: par * S + (jt + 1) * 128],
                            qh[:, par * S + qt * 128: par * S + (qt + 1) * 128],
                            start=True, stop=True).then_inc(m, 1))
                        self.wait(ph, "scalar", mm, tmv)
                        tex = ss.bump(1)
                        ph.add("scalar", lambda e, b=b, rot=rot, h=h, d=qt - jt, m=ss.h: e.activation(
                            expb[:, rot * 128:(rot + 1) * 128], self.bank(b, 128), AF.Exp,
                            scale=INV_SQRT_HD,
                            bias=self.alibi[:, h * NT + d: h * NT + d + 1]).then_inc(m, 1))
                        exdep = (ss, tex)
                        if jt == qt:
                            self.wait(ph, "vector", ss, tex)
                            tmx = sv.bump(1)
                            ph.add("vector", lambda e, rot=rot, m=sv.h: e.tensor_mul(
                                expb[:, rot * 128:(rot + 1) * 128],
                                expb[:, rot * 128:(rot + 1) * 128],
                                self.trimask[:, 0:128]).then_inc(m, 1))
                            exdep = (sv, tmx)
                        if av_pend is not None:
                            self._emit_av(ph, av_pend, h, expb, vtm)
                        av_pend = (jt, rot, exdep, qt, njt)
                        if NO_AV_PIPELINE:
                            if jt == njt - 1:
                                tmv = mm.bump(1)
                            self._emit_av(ph, av_pend, h, expb, vtm, inc=(jt == njt - 1))
                            av_pend = None
                    if not NO_AV_PIPELINE:
                        tmv = mm.bump(1)
                        self._emit_av(ph, av_pend, h, expb, vtm, inc=True)
                    self.wait(ph, "vector", mm, tmv)
                    if h in (0, 15) and "accq" in self.debug_taps:
                        hi = 0 if h == 0 else 1
                        taq = sv.bump(1)
                        ph.add("vector", lambda e, cb=cb, m=sv.h: e.tensor_copy(
                            self._dbg_accs[:, :], self.PS[:, cb * 512: cb * 512 + 129]).then_inc(m, 1))
                        self.wait(ph, "sync", sv, taq)
                        tdq = self.dma(ph, "sync", self.dsem("accq"), self.accq.ap()[hi, qt, :, :], self._dbg_accs[:, :])
                        self.wait(ph, "vector", self.dsem("accq"), tdq)
                    if cb in ctx_ev:
                        self.wait(ph, "vector", ss, ctx_ev[cb])
                    rsl = qt % 2
                    tvv = sv.bump(1)
                    ph.add("vector", lambda e, cb=cb, rsl=rsl, m=sv.h: e.reciprocal(
                        rcol[:, rsl:rsl + 1],
                        self.PS[:, cb * 512 + 128: cb * 512 + 129]).then_inc(m, 1))
                    self.wait(ph, "scalar", sv, tvv)
                    tev = ss.bump(1)
                    ph.add("scalar", lambda e, cb=cb, qt=qt, rsl=rsl, m=ss.h: e.activation(
                        ctx_sb[:, qt * 128:(qt + 1) * 128],
                        self.PS[:, cb * 512: cb * 512 + 128],
                        AF.Identity, scale=rcol[:, rsl:rsl + 1]).then_inc(m, 1))
                    ctx_ev[cb] = tev
                self.wait(ph, "sync", ss, ss.v)
                ctx_store[h] = self.dma(ph, "sync", dcx,
                                        self.ctxd.ap()[h].rearrange("(b p) c -> p b c", p=128),
                                        ctx_sb[:, :].rearrange("p (b c) -> p b c", c=128))
            ph.emit()
            t_ctx_total = dcx.v
            if self.phase_done():
                return

        with contextlib.ExitStack() as sc:
            sb = lambda name, shape, dt: sc.enter_context(nc.sbuf_tensor(name, shape, dt))
            ph = Phase(nc)
            ctxf = sb("ctxf", [128, 2 * S], BF16)
            wos = sb("wos", [128, 4 * NK * 512], BF16)
            x0t = sb("x0t", [128, 2 * DM], F32)
            y_sb = sb("y_sb", [128, DM], F32)
            sq_sb = sb("sq_sb", [128, DM], F32)
            x1t = sb("x1t", [128, DM], F32)
            stat = sb("stat", [128, 8], F32)
            browo, d_bro, _ = self.bias_row(ph, sc, "brow_o2", "L0_attn_bo", "bro")
            dcx = self.dsem("ctxs")
            dfT = [self.dsem("cfT0"), self.dsem("cfT1")]
            dcF = [self.dsem("cF0"), self.dsem("cF1")]
            dct = [self.dsem("cts0"), self.dsem("cts1")]
            dwo = self.dsem("wo")
            dx0 = [self.dsem("x00"), self.dsem("x01")]
            dx1 = self.dsem("x1s")
            dx1b = self.dsem("x1b")
            self.wait(ph, "sync", dcx, t_ctx_total)
            tF = {}
            for h in range(H):
                par = h % 2
                buf = ctxf[:, par * S:(par + 1) * S]
                if h >= 2:
                    self.wait(ph, "sync", dcF[par], tF[h - 2])
                t1 = self.dmaT(ph, "sync", dfT[par], buf, self.ctxd.ap()[h])
                self.wait(ph, "sync", dfT[par], t1)
                tF[h] = self.dma(ph, "sync", dcF[par], self.ctxdF.ap()[h], buf)
            for g in range(4):
                bufw = wos[:, g * NK * 512:(g + 1) * NK * 512]
                self.dma(ph, "gpsimd", dwo,
                         bufw.rearrange("p (kc n) -> p kc n", n=512),
                         self.w["L0_attn_Wo"].ap().rearrange("(kc p) n -> p kc n", p=128)[:, :, g * 512:(g + 1) * 512])
            self.wait(ph, "sync", dcF[0], dcF[0].v)
            self.wait(ph, "sync", dcF[1], dcF[1].v)
            mm_ts = {}
            ln_done = {}
            for ts in range(NT):
                par = ts % 2
                buf = ctxf[:, par * S:(par + 1) * S]
                self.pump_bg4(ph, 1)
                if ts >= 2:
                    self.wait(ph, "sync", mm, mm_ts[ts - 2])
                tld = self.dma(ph, "sync", dct[par],
                               buf.rearrange("p (h q) -> p h q", q=128),
                               self.ctxdF.ap()[:, :, ts * 128:(ts + 1) * 128].rearrange("h p q -> p h q"))
                if ts >= 2:
                    self.wait(ph, "scalar", sv, ln_done[ts - 2])
                tx0 = self.dma(ph, "scalar", dx0[par], x0t[:, par * DM:(par + 1) * DM],
                               self.x0.ap()[ts * 128:(ts + 1) * 128, :])
                self.wait(ph, "tensor", dct[par], tld)
                if ts == 0:
                    self.wait(ph, "tensor", d_bro, d_bro.v)
                    self.wait(ph, "tensor", dwo, dwo.v)
                for g in range(4):
                    bk = 4 + g
                    for hc in range(H):
                        ph.add("tensor", lambda e, bk=bk, hc=hc, g=g, buf=buf: e.matmul(
                            self.bank(bk), buf[:, hc * 128:(hc + 1) * 128],
                            wos[:, g * NK * 512 + hc * 512: g * NK * 512 + (hc + 1) * 512],
                            start=(hc == 0), stop=False))
                    upd = (mm.h, 1) if g == 3 else None
                    ph.add("tensor", lambda e, bk=bk, g=g, upd=upd: e.matmul(
                        self.bank(bk), self.ones1_bf[0:1, 0:128],
                        browo[0:1, g * 512:(g + 1) * 512], start=False, stop=True).then_maybe_inc(upd))
                mm_ts[ts] = mm.bump(1)
                self.wait(ph, "vector", mm, mm_ts[ts])
                self.wait(ph, "vector", dx0[par], tx0)
                sv.bump(1)
                ph.add("vector", lambda e, m=sv.h: e.memset(stat[:, 0:2], 0.0).then_inc(m, 1))
                self.wait(ph, "vector", sv, sv.v)
                tev = sv.bump(1)
                ph.add("vector", lambda e, par=par, m=sv.h: e.scalar_tensor_tensor(
                    y_sb[:, :], self.PS[:, 4 * 512:8 * 512], 1.0,
                    x0t[:, par * DM:(par + 1) * DM], ALU.mult, ALU.add,
                    accum_out=stat[:, 0:1]).then_inc(m, 1))
                self.wait(ph, "tensor", sv, tev)
                self.wait(ph, "scalar", sv, tev)
                if "ysd" in self.debug_taps:
                    tdy = self.dma(ph, "scalar", self.dsem("ysd"), self.ysd.ap()[ts, :, :], y_sb[:, :])
                    self.wait(ph, "vector", self.dsem("ysd"), tdy)
                tsq = ss.bump(1)
                ph.add("scalar", lambda e, m=ss.h: e.activation(
                    sq_sb[:, :], y_sb[:, :], AF.Square,
                    accum_out=stat[:, 1:2]).then_inc(m, 1))
                self.wait(ph, "vector", ss, tsq)
                tln = self.ln_tail(ph, y_sb[:, :], stat, x1t[:, :])
                ln_done[ts] = tln
                if "ysd" in self.debug_taps:
                    self.wait(ph, "scalar", sv, tln)
                    tdd = self.dma(ph, "scalar", self.dsem("std"), self.std.ap()[ts, :, :], stat[:, :])
                    self.wait(ph, "vector", self.dsem("std"), tdd)
                self.wait(ph, "sync", sv, tln)
                self.wait(ph, "gpsimd", sv, tln)
                t_f32 = self.dma(ph, "sync", dx1, self.x1f.ap()[ts * 128:(ts + 1) * 128, :], x1t[:, :])
                t_bf = self.dma(ph, "gpsimd", dx1b, self.x1bf.ap()[ts * 128:(ts + 1) * 128, :], x1t[:, :])
                self.wait(ph, "vector", dx1, t_f32)
                self.wait(ph, "vector", dx1b, t_bf)
            ph.emit()
            if self.phase_done():
                return

    def _emit_av(self, ph, pend, h, expb, vtm, inc=False):
        jt, rot, exdep, qt, njt = pend
        mm = self.sem["mm"]
        cb = 2 + qt % 2
        self.wait(ph, "tensor", exdep[0], exdep[1])
        upd = (mm.h, 1) if inc else None
        ph.add("tensor", lambda e, rot=rot, jt=jt, h=h, cb=cb, upd=upd: e.matmul(
            self.PS[:, cb * 512: cb * 512 + 129],
            expb[:, rot * 128:(rot + 1) * 128],
            vtm[:, jt * VW + h * 129: jt * VW + (h + 1) * 129],
            start=(jt == 0), stop=(jt == njt - 1)).then_maybe_inc(upd))

    def stage2(self):
        nc = self.nc
        mm, sv, ss, sg = self.sem["mm"], self.sem["sv"], self.sem["ss"], self.sem["sg"]
        w = self.w
        with contextlib.ExitStack() as sc:
            sb = lambda name, shape, dt: sc.enter_context(nc.sbuf_tensor(name, shape, dt))
            A = sb("s2A", [128, NK * S], BF16)
            gates_bc = sb("gates_bc", [128, NE], F32)
            b1T = sb("b1T", [128, NE], F32)
            b2eff = sb("b2eff", [128, 1], F32)
            w1T = sb("w1T", [128, NE * 128], BF16)
            w2T = sb("w2T", [128, NE * 128], BF16)

            ph = Phase(nc)
            sc4 = contextlib.ExitStack()
            sb4 = lambda name, shape, dt: sc4.enter_context(nc.sbuf_tensor(name, shape, dt))
            swb = sb4("swb", [128, 4 * DM], F32)
            x1tl = sb4("x1tl", [128, 2 * DM], BF16)
            junk = sb4("junk", [128, 2 * DM], BF16)
            racc = sb4("racc", [128, 2 * NE * NT], F32)
            acc8 = sb4("acc8", [128, NE], F32)
            rsc = sb4("rsc", [8, 64], F32)
            b2sb = sb4("b2sb", [8, 128], F32)
            b1sb = sb4("b1sb", [8, 128], F32)
            swbias = sb4("swbias", [1, 8], F32)
            w1s = sb4("w1s", [128, NE * 128], BF16)
            w2s = sb4("w2s", [128, NE * 128], BF16)

            self.w3bf = nc.dram_tensor("w3bf", [DM, DM], BF16, kind="Internal")
            d_w3bf = self.dsem("w3bf")
            self.dma(ph, "gpsimd", d_w3bf, self.w3bf.ap(), w["L0_ffn_w3"].ap())
            d_s2c = self.dsem("s2c")
            self.dma(ph, "sync", d_s2c, swbias[:, :], w["L0_ffn_switch_b"].ap().unsqueeze(0))
            self.dma(ph, "sync", d_s2c, b1sb[:, :], w["L0_ffn_b1"].ap())
            self.dma(ph, "sync", d_s2c, b2sb[:, :], w["L0_ffn_b2"].ap())
            d_w12 = self.dsem("w12")
            self.dma(ph, "gpsimd", d_w12, w1s[:, :].rearrange("p (e f) -> p e f", f=128),
                     w["L0_ffn_w1"].ap().rearrange("e f d -> f e d"))
            self.dma(ph, "gpsimd", d_w12, w2s[:, :].rearrange("p (e f) -> p e f", f=128),
                     w["L0_ffn_w2"].ap().rearrange("e d f -> d e f"))
            d_xfm2 = self.dsem("xfm2")
            for dc in range(NK):
                self.dmaT(ph, "sync", d_xfm2, A[:, dc * S:(dc + 1) * S],
                          self.x1bf.ap()[:, dc * 128:(dc + 1) * 128])
            d_sw = [self.dsem("sw0"), self.dsem("sw1"), self.dsem("sw2"), self.dsem("sw3")]
            d_x1r = [self.dsem("x1r0"), self.dsem("x1r1")]
            sv.bump(1)
            ph.add("vector", lambda e, m=sv.h: e.memset(racc[:, 0:NE * NT], 0.0).then_inc(m, 1))
            t_r0 = sv.v
            stt_done = {}
            gtt_done = {}
            for ts in range(NT):
                par = ts % 2
                if ts >= 2:
                    self.wait(ph, "sync", sv, stt_done[(ts - 2, NE - 1)])
                tx1 = self.dma(ph, "sync", d_x1r[par], x1tl[:, par * DM:(par + 1) * DM],
                               self.x1bf.ap()[ts * 128:(ts + 1) * 128, :])
                for e in range(NE):
                    i = ts * NE + e
                    veng = "vector"
                    esem = sv
                    spar = i % 4
                    qeng = "sync" if (i % 2 == 0) else "scalar"
                    if i >= 4:
                        pi = i - 4
                        self.wait(ph, qeng, sv, stt_done[(pi // NE, pi % NE)])
                    tsw = self.dma(ph, qeng, d_sw[spar], swb[:, spar * DM:(spar + 1) * DM],
                                   w["L0_ffn_switch_w"].ap().rearrange("e (t p f) -> e t p f", p=128, f=DM)[e, ts])
                    self.wait(ph, veng, d_sw[spar], tsw)
                    if e in (0, 5):
                        self.wait(ph, veng, d_x1r[par], tx1)
                    if i == 0:
                        self.wait(ph, "vector", sv, t_r0)
                    col = e * NT + ts
                    tst = esem.bump(1)
                    jslice = junk[:, 0:DM]
                    ph.add(veng, lambda e2, par=par, spar=spar, col=col, m=esem.h, jslice=jslice: e2.scalar_tensor_tensor(
                        jslice, x1tl[:, par * DM:(par + 1) * DM], 1.0,
                        swb[:, spar * DM:(spar + 1) * DM], ALU.mult, ALU.mult,
                        accum_out=racc[:, col:col + 1]).then_inc(m, 1))
                    stt_done[(ts, e)] = tst
            t1 = self.vchain(ph, "vector", [
                lambda e: e.tensor_reduce(acc8[:, :],
                                          racc[:, 0:NE * NT].rearrange("p (e t) -> p e t", t=NT),
                                          AX.X, ALU.add)])
            self.wait(ph, "tensor", sv, t1)
            tmv = mm.bump(1)
            ph.add("tensor", lambda e, m=mm.h: e.matmul(
                self.PS[0:1, 7 * 512:7 * 512 + 8], self.ones1_f32[:, 0:1], acc8[:, :],
                start=True, stop=True).then_inc(m, 1))
            self.wait(ph, "vector", mm, tmv)
            self.wait(ph, "vector", d_s2c, d_s2c.v)
            R = rsc
            t2 = self.vchain(ph, "vector", [
                lambda e: e.tensor_tensor(R[0:1, 0:8], self.PS[0:1, 7 * 512:7 * 512 + 8], swbias[0:1, :], ALU.add),
                lambda e: e.reduce_max(R[0:1, 8:9], R[0:1, 0:8], AX.X),
                lambda e: e.tensor_scalar(R[0:1, 9:10], R[0:1, 8:9], -1.0, None, ALU.mult),
            ])
            self.wait(ph, "scalar", sv, t2)
            t3 = ss.bump(1)
            ph.add("scalar", lambda e, m=ss.h: e.activation(
                R[0:1, 10:18], R[0:1, 0:8], AF.Exp, bias=R[0:1, 9:10]).then_inc(m, 1))
            self.wait(ph, "vector", ss, t3)
            t4 = self.vchain(ph, "vector", [
                lambda e: e.reduce_sum(R[0:1, 18:19], R[0:1, 10:18], AX.X),
                lambda e: e.reciprocal(R[0:1, 19:20], R[0:1, 18:19]),
                lambda e: e.tensor_scalar(R[0:1, 20:28], R[0:1, 10:18], R[0:1, 19:20], None, ALU.mult),
                lambda e: e.reduce_max(R[0:1, 28:29], R[0:1, 20:28], AX.X),
                lambda e: e.tensor_single_scalar(R[0:1, 29:37], R[0:1, 20:28], R[0:1, 28:29], ALU.is_ge),
                lambda e: e.scalar_tensor_tensor(R[0:1, 29:37], R[0:1, 29:37], -1e30, R[0:1, 20:28], ALU.mult, ALU.add),
                lambda e: e.reduce_max(R[0:1, 37:38], R[0:1, 29:37], AX.X),
                lambda e: e.tensor_single_scalar(R[0:1, 38:46], R[0:1, 20:28], R[0:1, 37:38], ALU.is_ge),
                lambda e: e.tensor_tensor(R[0:1, 46:54], R[0:1, 20:28], R[0:1, 38:46], ALU.mult),
            ])
            self.wait(ph, "tensor", sv, t4)
            tmv = mm.bump(1)
            ph.add("tensor", lambda e, m=mm.h: e.matmul(
                self.PS[:, 6 * 512:6 * 512 + 8], self.onesr_f32[0:1, :],
                R[0:1, 46:54], start=True, stop=True).then_inc(m, 1))
            tmv2 = mm.bump(1)
            ph.add("tensor", lambda e, m=mm.h: e.transpose(
                self.PS[0:8, 7 * 512 + 16:7 * 512 + 17], R[0:1, 46:54],
                self.ident_f32[0:1, 0:1]).then_inc(m, 1))
            self.wait(ph, "vector", mm, tmv2)
            t5 = self.vchain(ph, "vector", [
                lambda e: e.tensor_copy(gates_bc[:, :], self.PS[:, 6 * 512:6 * 512 + 8]),
                lambda e: e.tensor_copy(R[0:8, 54:55], self.PS[0:8, 7 * 512 + 16:7 * 512 + 17]),
            ])
            self.wait(ph, "tensor", sv, t5)
            self.wait(ph, "tensor", d_s2c, d_s2c.v)
            tmv3 = mm.bump(1)
            ph.add("tensor", lambda e, m=mm.h: e.matmul(
                self.PS[:, 5 * 512:5 * 512 + 1], b2sb[:, :], R[0:8, 54:55],
                start=True, stop=True).then_inc(m, 1))
            tmv4 = mm.bump(1)
            ph.add("tensor", lambda e, m=mm.h: e.transpose(
                self.PS[:, 4 * 512:4 * 512 + 8], b1sb[:, :],
                self.ident_f32[0:8, 0:8]).then_inc(m, 1))
            self.wait(ph, "vector", mm, tmv4)
            t6 = self.vchain(ph, "vector", [
                lambda e: e.tensor_copy(b2eff[:, :], self.PS[:, 5 * 512:5 * 512 + 1]),
                lambda e: e.tensor_copy(b1T[:, :], self.PS[:, 4 * 512:4 * 512 + 8]),
            ])
            self.wait(ph, "tensor", d_w12, d_w12.v)
            for e_ in range(NE):
                for wsrc, wdst in ((w1s, w1T), (w2s, w2T)):
                    tmvT = mm.bump(1)
                    ph.add("tensor", lambda e, e_=e_, wsrc=wsrc, m=mm.h: e.transpose(
                        self.PS[:, 3 * 512:3 * 512 + 64].bitcast(BF16),
                        wsrc[:, e_ * 128:(e_ + 1) * 128], self.ident_bf[:, :]).then_inc(m, 1))
                    self.wait(ph, "vector", mm, tmvT)
                    tvT = sv.bump(1)
                    ph.add("vector", lambda e, e_=e_, wdst=wdst, m=sv.h: e.tensor_copy(
                        wdst[:, e_ * 128:(e_ + 1) * 128],
                        self.PS[:, 3 * 512:3 * 512 + 64].bitcast(BF16)).then_inc(m, 1))
                    self.wait(ph, "tensor", sv, tvT)
            self.wait(ph, "vector", sv, t6)
            ph.emit()
            sc4.close()
            if self.phase_done():
                return sc.close()

            ph = Phase(nc)
            sc5 = contextlib.ExitStack()
            sb5 = lambda name, shape, dt: sc5.enter_context(nc.sbuf_tensor(name, shape, dt))
            hb = sb5("hb", [128, 4 * 512], BF16)
            hg = sb5("hg", [128, 4 * 512], BF16)
            B = sb5("s2B", [128, NK * S], BF16)
            x1ft = sb5("x1ft", [128, 2 * DM], F32)
            y_sb = sb5("y2_sb", [128, DM], F32)
            sq_sb = sb5("sq2_sb", [128, DM], F32)
            x2t = sb5("x2t", [128, DM], F32)
            stat = sb5("stat2", [128, 8], F32)
            brow3, d_br3, _ = self.bias_row(ph, sc5, "brow_b3", "L0_ffn_b3", "br3")
            self.wait(ph, "tensor", d_xfm2, d_xfm2.v)
            gel_done = {}
            gmul_done = {}
            omm_done = {}
            pre_done = {}
            for h in range(H):
                self.pump_bg4(ph, 1)
                for g in range(4):
                    gi = h * 4 + g
                    ob = 2 + gi % 2
                    if gi >= 2:
                        self.wait(ph, "tensor", sv, pre_done[gi - 2])
                    HB = (0, 1, 4, 5)
                    for e_ in range(NE):
                        i = gi * NE + e_
                        hbk = i % 4
                        if i >= 4:
                            self.wait(ph, "tensor", ss, gel_done[i - 4])
                        tmv = mm.bump(1)
                        ph.add("tensor", lambda e, hbk=hbk, e_=e_, h=h, g=g, m=mm.h: e.matmul(
                            self.bank(HB[hbk]), w1T[:, e_ * 128:(e_ + 1) * 128],
                            A[:, h * S + g * 512: h * S + (g + 1) * 512],
                            start=True, stop=True).then_inc(m, 1))
                        self.wait(ph, "scalar", mm, tmv)
                        if i >= 4:
                            self.wait(ph, "scalar", sv, gmul_done[i - 4])
                        tgl = ss.bump(1)
                        ph.add("scalar", lambda e, hbk=hbk, e_=e_, m=ss.h: e.activation(
                            hb[:, hbk * 512:(hbk + 1) * 512], self.bank(HB[hbk]), AF.Gelu,
                            bias=b1T[:, e_:e_ + 1]).then_inc(m, 1))
                        gel_done[i] = tgl
                        self.wait(ph, "vector", ss, tgl)
                        if i >= 4:
                            self.wait(ph, "vector", mm, omm_done.get(i - 4, 0))
                        tgm = sv.bump(1)
                        ph.add("vector", lambda e, hbk=hbk, e_=e_, m=sv.h: e.tensor_scalar(
                            hg[:, hbk * 512:(hbk + 1) * 512], hb[:, hbk * 512:(hbk + 1) * 512],
                            gates_bc[:, e_:e_ + 1], None, ALU.mult).then_inc(m, 1))
                        gmul_done[i] = tgm
                        self.wait(ph, "tensor", sv, tgm)
                        tom = mm.bump(1)
                        ph.add("tensor", lambda e, ob=ob, e_=e_, hbk=hbk, m=mm.h: e.matmul(
                            self.bank(ob), w2T[:, e_ * 128:(e_ + 1) * 128],
                            hg[:, hbk * 512:(hbk + 1) * 512],
                            start=(e_ == 0), stop=(e_ == NE - 1)).then_inc(m, 1))
                        omm_done[i] = tom
                    self.wait(ph, "vector", mm, omm_done[gi * NE + NE - 1])
                    tpre = sv.bump(1)
                    ph.add("vector", lambda e, ob=ob, h=h, g=g, m=sv.h: e.scalar_tensor_tensor(
                        A[:, h * S + g * 512: h * S + (g + 1) * 512], self.bank(ob),
                        b2eff[:, 0:1], A[:, h * S + g * 512: h * S + (g + 1) * 512],
                        ALU.add, ALU.add).then_inc(m, 1))
                    pre_done[gi] = tpre
            t_pre_all = sv.v
            d_w3T = self.dsem("w3T")
            self.wait(ph, "sync", d_w3bf, d_w3bf.v)
            for dc in range(NK):
                self.dmaT(ph, "sync", d_w3T, B[:, dc * S:(dc + 1) * S],
                          self.w3bf.ap()[:, dc * 128:(dc + 1) * 128])
            d_x1f = [self.dsem("x1f0"), self.dsem("x1f1")]
            dx2 = self.dsem("x2s")
            dx2b = self.dsem("x2b")
            self.wait(ph, "tensor", d_w3T, d_w3T.v)
            self.wait(ph, "tensor", sv, t_pre_all)
            self.wait(ph, "tensor", d_br3, d_br3.v)
            mm_ts = {}
            ln_done = {}
            for ts in range(NT):
                par = ts % 2
                self.pump_bg4(ph, 2)
                if ts >= 2:
                    self.wait(ph, "scalar", sv, ln_done[ts - 2])
                tx1 = self.dma(ph, "scalar", d_x1f[par], x1ft[:, par * DM:(par + 1) * DM],
                               self.x1f.ap()[ts * 128:(ts + 1) * 128, :])
                for g in range(4):
                    bk = 4 + g
                    for dc in range(NK):
                        ph.add("tensor", lambda e, bk=bk, dc=dc, g=g, ts=ts: e.matmul(
                            self.bank(bk), A[:, dc * S + ts * 128: dc * S + ts * 128 + 128],
                            B[:, dc * S + g * 512: dc * S + (g + 1) * 512],
                            start=(dc == 0), stop=False))
                    upd = (mm.h, 1) if g == 3 else None
                    ph.add("tensor", lambda e, bk=bk, g=g, upd=upd: e.matmul(
                        self.bank(bk), self.ones1_bf[0:1, 0:128],
                        brow3[0:1, g * 512:(g + 1) * 512], start=False, stop=True).then_maybe_inc(upd))
                mm_ts[ts] = mm.bump(1)
                self.wait(ph, "vector", mm, mm_ts[ts])
                self.wait(ph, "vector", d_x1f[par], tx1)
                sv.bump(1)
                ph.add("vector", lambda e, m=sv.h: e.memset(stat[:, 0:2], 0.0).then_inc(m, 1))
                self.wait(ph, "vector", sv, sv.v)
                tev = sv.bump(1)
                ph.add("vector", lambda e, par=par, m=sv.h: e.scalar_tensor_tensor(
                    y_sb[:, :], self.PS[:, 4 * 512:8 * 512], 1.0,
                    x1ft[:, par * DM:(par + 1) * DM], ALU.mult, ALU.add,
                    accum_out=stat[:, 0:1]).then_inc(m, 1))
                self.wait(ph, "tensor", sv, tev)
                self.wait(ph, "scalar", sv, tev)
                tsq = ss.bump(1)
                ph.add("scalar", lambda e, m=ss.h: e.activation(
                    sq_sb[:, :], y_sb[:, :], AF.Square, accum_out=stat[:, 1:2]).then_inc(m, 1))
                self.wait(ph, "vector", ss, tsq)
                tln = self.ln_tail(ph, y_sb[:, :], stat, x2t[:, :])
                ln_done[ts] = tln
                self.wait(ph, "sync", sv, tln)
                self.wait(ph, "gpsimd", sv, tln)
                t_f32 = self.dma(ph, "sync", dx2, self.x2f.ap()[ts * 128:(ts + 1) * 128, :], x2t[:, :])
                t_bf = self.dma(ph, "gpsimd", dx2b, self.x2bf.ap()[ts * 128:(ts + 1) * 128, :], x2t[:, :])
                self.wait(ph, "vector", dx2, t_f32)
                self.wait(ph, "vector", dx2b, t_bf)
            ph.emit()
            sc5.close()
            if self.phase_done():
                return

    def stage3(self):
        nc = self.nc
        mm, sv, ss = self.sem["mm"], self.sem["sv"], self.sem["ss"]
        w = self.w
        kind = lambda n: "ExternalOutput" if n in self.debug_taps else "Internal"
        if self.debug_taps:
            self.kbf = nc.dram_tensor("kbf", [S, DM], BF16, kind=kind("kbf")).ap()
            self.mixbf = nc.dram_tensor("mixbf", [S, DM], BF16, kind=kind("mixbf")).ap()
            self.ubf = nc.dram_tensor("ubf", [S, DM], BF16, kind=kind("ubf")).ap()
        else:
            self.kbf = self.ctxd.ap().rearrange("h s c -> (h s c)").rearrange("(a b) -> a b", b=DM)
            self.mixbf = self.ctxdF.ap().rearrange("h c s -> (h c s)").rearrange("(a b) -> a b", b=DM)
            self.ubf = self.w3bf.ap()
        with contextlib.ExitStack() as sc:
            sb = lambda name, shape, dt: sc.enter_context(nc.sbuf_tensor(name, shape, dt))
            A = sb("s3A", [128, NK * S], BF16)
            B = sb("s3B", [128, NK * S], BF16)
            kv_sb = sb("kv_sb", [128, H * 128], BF16)
            sc6 = contextlib.ExitStack()
            sb6 = lambda name, shape, dt: sc6.enter_context(nc.sbuf_tensor(name, shape, dt))
            sc67 = contextlib.ExitStack()
            sb67 = lambda name, shape, dt: sc67.enter_context(nc.sbuf_tensor(name, shape, dt))
            wsl = sb67("wsl3", [128, 2 * NK * 512], BF16)

            def proj_pass(ph, wname, brow, out_emit, d_par, bias_row_ready):
                done = {}
                mm_g = {}
                mm_entry = mm.v
                for g in range(4):
                    par = g % 2
                    buf = wsl[:, par * NK * 512:(par + 1) * NK * 512]
                    self.pump_bg4(ph, 3)
                    if g >= 2:
                        self.wait(ph, "gpsimd", mm, mm_g[g - 2])
                    elif mm_entry > 0:
                        self.wait(ph, "gpsimd", mm, mm_entry)
                    twv = self.dma(ph, "gpsimd", d_par[par],
                                   buf.rearrange("p (kc n) -> p kc n", n=512),
                                   w[wname].ap().rearrange("(kc p) n -> p kc n", p=128)[:, :, g * 512:(g + 1) * 512])
                    self.wait(ph, "tensor", d_par[par], twv)
                    if g == 0:
                        self.wait(ph, "tensor", bias_row_ready[0], bias_row_ready[1])
                    for ts in range(NT):
                        i = g * NT + ts
                        b = i % 2
                        if i >= 2:
                            sem, val = done[i - 2]
                            self.wait(ph, "tensor", sem, val)
                        for kc in range(NK):
                            ph.add("tensor", lambda e, b=b, kc=kc, ts=ts, buf=buf: e.matmul(
                                self.bank(b), A[:, kc * S + ts * 128: kc * S + ts * 128 + 128],
                                buf[:, kc * 512:(kc + 1) * 512], start=(kc == 0), stop=False))
                        tmv = mm.bump(1)
                        ph.add("tensor", lambda e, b=b, g=g, brow=brow, m=mm.h: e.matmul(
                            self.bank(b), self.ones1_bf[0:1, 0:128],
                            brow[0:1, g * 512:(g + 1) * 512], start=False, stop=True).then_inc(m, 1))
                        done[i] = out_emit(ph, g, ts, b, tmv, i)
                    mm_g[g] = mm.v
                return done

            ph = Phase(nc)
            elu1 = sb67("elu1", [128, 512], F32)
            elu2 = sb67("elu2", [128, 512], F32)
            elu3 = sb67("elu3", [128, 512], F32)
            browk, d_brk, _ = self.bias_row(ph, sc67, "brow_qk", "L1_attn_bqk", "brk")
            d_xfm3 = self.dsem("xfm3")
            for dc in range(NK):
                self.dmaT(ph, "sync", d_xfm3, A[:, dc * S:(dc + 1) * S],
                          self.x2bf.ap()[:, dc * 128:(dc + 1) * 128])
            d_wqk = [self.dsem("wqk0"), self.dsem("wqk1")]
            d_kbf = self.dsem("kbfs")
            self.wait(ph, "tensor", d_xfm3, d_xfm3.v)

            kspill = {}
            def elu_emit(ph, g, ts, b, tmv, i):
                self.wait(ph, "vector", mm, tmv)
                if i >= 1:
                    self.wait(ph, "vector", ss, ss.v)
                tv1 = self.vchain(ph, "vector", [
                    lambda e, b=b: e.tensor_scalar(elu1[:, :], self.bank(b), 0.0, None, ALU.min),
                    lambda e, b=b: e.tensor_relu(elu3[:, :], self.bank(b)),
                ])
                self.wait(ph, "scalar", sv, tv1)
                tsx = ss.bump(1)
                ph.add("scalar", lambda e, m=ss.h: e.activation(
                    elu2[:, :], elu1[:, :], AF.Exp).then_inc(m, 1))
                self.wait(ph, "vector", ss, tsx)
                tst = sv.bump(1)
                ph.add("vector", lambda e, ts=ts, g=g, m=sv.h: e.scalar_tensor_tensor(
                    B[:, ts * DM + g * 512: ts * DM + (g + 1) * 512],
                    elu2[:, :], -1.0,
                    elu3[:, :], ALU.add, ALU.add).then_inc(m, 1))
                kspill.setdefault(ts, []).append(tst)
                return (sv, tst)

            proj_pass(ph, "L1_attn_Wqk", browk, elu_emit, d_wqk, (d_brk, d_brk.v))
            for ts in range(NT):
                self.wait(ph, "sync", sv, max(kspill[ts]))
                self.dma(ph, "sync", d_kbf, self.kbf[ts * 128:(ts + 1) * 128, :],
                         B[:, ts * DM:(ts + 1) * DM])
            ph.emit()
            if self.phase_done():
                sc6.close()
                return

            ph = Phase(nc)
            vbuf = sb67("vbuf", [128, 2 * 512], BF16)
            browv, d_brv, _ = self.bias_row(ph, sc67, "brow_v3", "L1_attn_bv", "brv")
            browu, d_bru, _ = self.bias_row(ph, sc67, "brow_u3", "L1_attn_bu", "bru")
            d_wv3 = [self.dsem("wv30"), self.dsem("wv31")]
            d_wu3 = [self.dsem("wu30"), self.dsem("wu31")]
            kv_last = {}

            def v_emit(ph, g, ts, b, tmv, i):
                vp = i % 2
                self.wait(ph, "scalar", mm, tmv)
                tsl = ss.bump(1)
                ph.add("scalar", lambda e, b=b, vp=vp, m=ss.h: e.activation(
                    vbuf[:, vp * 512:(vp + 1) * 512], self.bank(b), AF.Silu).then_inc(m, 1))
                self.wait(ph, "tensor", ss, tsl)
                kvbk = 2 + g
                for hh in range(4):
                    h = g * 4 + hh
                    upd = (mm.h, 1) if hh == 3 else None
                    ph.add("tensor", lambda e, h=h, hh=hh, ts=ts, vp=vp, kvbk=kvbk, upd=upd: e.matmul(
                        self.PS[:, kvbk * 512 + hh * 128: kvbk * 512 + (hh + 1) * 128],
                        B[:, ts * DM + h * 128: ts * DM + (h + 1) * 128],
                        vbuf[:, vp * 512 + hh * 128: vp * 512 + (hh + 1) * 128],
                        start=(ts == 0 and hh == 0),
                        stop=(ts == NT - 1 and hh == 3)).then_maybe_inc(upd))
                tkv = mm.bump(1)
                kv_last[(g, ts)] = tkv
                return (ss, tsl)

            proj_pass(ph, "L1_attn_Wv", browv, v_emit, d_wv3, (d_brv, d_brv.v))
            for g in range(4):
                self.wait(ph, "vector", mm, kv_last[(g, NT - 1)])
                tve = sv.bump(1)
                ph.add("vector", lambda e, g=g, m=sv.h: e.tensor_copy(
                    kv_sb[:, g * 512:(g + 1) * 512],
                    self.PS[:, (2 + g) * 512:(3 + g) * 512]).then_inc(m, 1))
            t_kv_done = sv.v

            d_ub = self.dsem("ubs")
            ubuf = sb67("ubuf", [128, 2 * 512], BF16)
            uspill = {}

            def u_emit(ph, g, ts, b, tmv, i):
                up = i % 2
                self.wait(ph, "scalar", mm, tmv)
                if i >= 2:
                    self.wait(ph, "scalar", d_ub, uspill[i - 2])
                tsl = ss.bump(1)
                ph.add("scalar", lambda e, b=b, up=up, m=ss.h: e.activation(
                    ubuf[:, up * 512:(up + 1) * 512], self.bank(b), AF.Silu).then_inc(m, 1))
                self.wait(ph, "sync", ss, tsl)
                tu = self.dma(ph, "sync", d_ub,
                              self.ubf[ts * 128:(ts + 1) * 128, g * 512:(g + 1) * 512],
                              ubuf[:, up * 512:(up + 1) * 512])
                uspill[i] = tu
                return (ss, tsl)

            proj_pass(ph, "L1_attn_Wu", browu, u_emit, d_wu3, (d_bru, d_bru.v))
            t_u_done = d_ub.v
            ph.emit()
            sc67.close()
            if self.phase_done():
                sc6.close()
                return

            ph = Phase(nc)
            y3_sb = sb6("y3_sb", [128, DM], F32)
            sq3 = sb6("sq3", [128, DM], F32)
            stat3 = sb6("stat3", [128, 8], F32)
            sc8 = contextlib.ExitStack()
            sb8 = lambda name, shape, dt: sc8.enter_context(nc.sbuf_tensor(name, shape, dt))
            yn_sb = sb8("yn_sb", [128, DM], F32)
            mixt = sb8("mixt", [128, DM], BF16)
            ut = sb8("ut", [128, 2 * DM], BF16)
            d_ut = [self.dsem("ut0"), self.dsem("ut1")]
            lrow = sb8("lrow", [1, 2 * DM], F32)
            Af32 = A[:, 0:8192].bitcast(F32)
            gbc = Af32[:, 0:DM]
            bbc = Af32[:, DM:2 * DM]
            d_qfm = self.dsem("qfm")
            d_lnr = self.dsem("lnr")
            self.dma(ph, "sync", d_lnr, lrow[:, 0:DM], w["L1_attn_ln_g"].ap().unsqueeze(0))
            self.dma(ph, "sync", d_lnr, lrow[:, DM:2 * DM], w["L1_attn_ln_b"].ap().unsqueeze(0))
            self.wait(ph, "sync", sv, t_kv_done)
            for dc in range(NK):
                self.dmaT(ph, "sync", d_qfm, B[:, dc * S:(dc + 1) * S],
                          self.kbf[:, dc * 128:(dc + 1) * 128])
            self.wait(ph, "tensor", d_lnr, d_lnr.v)
            for which in range(2):
                for gg in range(4):
                    bk = which * 4 + gg
                    upd = (mm.h, 1) if (which == 1 and gg == 3) else None
                    if upd:
                        mm.bump(1)
                    ph.add("tensor", lambda e, bk=bk, which=which, gg=gg, upd=upd: e.matmul(
                        self.bank(bk), self.onesr_f32[0:1, :],
                        lrow[0:1, which * DM + gg * 512: which * DM + (gg + 1) * 512],
                        start=True, stop=True).then_maybe_inc(upd))
            self.wait(ph, "vector", mm, mm.v)
            tgb = self.vchain(ph, "vector", [
                lambda e: e.tensor_copy(gbc, self.PS[:, 0:2048]),
                lambda e: e.tensor_copy(bbc, self.PS[:, 2048:4096]),
            ])
            brow_o3, d_bro3, _ = self.bias_row(ph, sc8, "brow_o3", "L1_attn_bo", "bro3")
            d_mix = self.dsem("mixs")
            self.wait(ph, "tensor", d_qfm, d_qfm.v)
            self.wait(ph, "tensor", sv, tgb)
            self.wait(ph, "tensor", sv, t_kv_done)
            mix_done = {}
            for ts in range(NT):
                self.pump_bg4(ph, 2)
                if ts >= 1:
                    self.wait(ph, "tensor", sv, mix_done[ts - 1])
                for h in range(H):
                    bk = 4 + h // 4
                    upd = (mm.h, 1) if h == H - 1 else None
                    if h == H - 1:
                        mm.bump(1)
                    ph.add("tensor", lambda e, h=h, bk=bk, ts=ts, upd=upd: e.matmul(
                        self.PS[:, bk * 512 + (h % 4) * 128: bk * 512 + (h % 4 + 1) * 128],
                        B[:, h * S + ts * 128: h * S + ts * 128 + 128],
                        kv_sb[:, h * 128:(h + 1) * 128],
                        start=True, stop=True).then_maybe_inc(upd))
                t_out = mm.v
                self.wait(ph, "vector", mm, t_out)
                sv.bump(1)
                ph.add("vector", lambda e, m=sv.h: e.memset(stat3[:, 0:2], 0.0).then_inc(m, 1))
                self.wait(ph, "vector", sv, sv.v)
                tev = sv.bump(1)
                ph.add("vector", lambda e, m=sv.h: e.tensor_scalar(
                    y3_sb[:, :], self.PS[:, 4 * 512:8 * 512], 1.0, 0.0, ALU.mult,
                    ALU.add, accum_out=stat3[:, 0:1]).then_inc(m, 1))
                self.wait(ph, "scalar", sv, tev)
                tsq = ss.bump(1)
                ph.add("scalar", lambda e, m=ss.h: e.activation(
                    sq3[:, :], y3_sb[:, :], AF.Square, accum_out=stat3[:, 1:2]).then_inc(m, 1))
                self.wait(ph, "vector", ss, tsq)
                if ts == 0:
                    self.wait(ph, "vector", sv, tgb)
                tln = self.ln_tail(ph, y3_sb[:, :], stat3, yn_sb[:, :])
                upar = ts % 2
                if ts == 0:
                    self.wait(ph, "gpsimd", d_ub, t_u_done)
                if ts >= 2:
                    self.wait(ph, "gpsimd", sv, mix_done[ts - 2])
                tuld = self.dma(ph, "gpsimd", d_ut[upar], ut[:, upar * DM:(upar + 1) * DM],
                                self.ubf[ts * 128:(ts + 1) * 128, :])
                self.wait(ph, "vector", sv, tln)
                self.wait(ph, "vector", d_ut[upar], tuld)
                tmx = self.vchain(ph, "vector", [
                    lambda e: e.tensor_tensor(yn_sb[:, :], yn_sb[:, :], gbc, ALU.mult),
                    lambda e: e.tensor_tensor(yn_sb[:, :], yn_sb[:, :], bbc, ALU.add),
                    lambda e, upar=upar: e.tensor_tensor(mixt[:, :], yn_sb[:, :],
                                                         ut[:, upar * DM:(upar + 1) * DM], ALU.mult),
                ])
                mix_done[ts] = tmx
                self.wait(ph, "sync", sv, tmx)
                tms = self.dma(ph, "sync", d_mix, self.mixbf[ts * 128:(ts + 1) * 128, :], mixt[:, :])
                self.wait(ph, "vector", d_mix, tms)
            ph.emit()
            sc8.close()
            if self.phase_done():
                sc6.close()
                return

            ph = Phase(nc)
            x2ft = sb6("x2ft", [128, 2 * DM], F32)
            x3t = sb6("x3t", [128, DM], F32)
            brow_o9, d_bro9, _ = self.bias_row(ph, sc6, "brow_o9", "L1_attn_bo", "bro9")
            d_mfm = self.dsem("mfm")
            for dc in range(NK):
                self.dmaT(ph, "sync", d_mfm, B[:, dc * S:(dc + 1) * S],
                          self.mixbf[:, dc * 128:(dc + 1) * 128])
            d_wo3 = [self.dsem("wo30"), self.dsem("wo31")]
            d_x2f = [self.dsem("x2f0"), self.dsem("x2f1")]
            dx3 = self.dsem("x3s")
            dx3b = self.dsem("x3b")
            self.wait(ph, "tensor", d_mfm, d_mfm.v)
            self.wait(ph, "tensor", d_bro9, d_bro9.v)
            ln_done9 = {}
            mm_ts9 = {}
            wof = A
            d_wof = self.dsem("wof")
            for g in range(4):
                self.dma(ph, "gpsimd", d_wof,
                         wof[:, g * NK * 512:(g + 1) * NK * 512].rearrange("p (kc n) -> p kc n", n=512),
                         w["L1_attn_Wo"].ap().rearrange("(kc p) n -> p kc n", p=128)[:, :, g * 512:(g + 1) * 512])
            self.wait(ph, "tensor", d_wof, d_wof.v)
            for ts in range(NT):
                par = ts % 2
                self.pump_bg4(ph, 3)
                if ts >= 2:
                    self.wait(ph, "scalar", sv, ln_done9[ts - 2])
                tx2 = self.dma(ph, "scalar", d_x2f[par], x2ft[:, par * DM:(par + 1) * DM],
                               self.x2f.ap()[ts * 128:(ts + 1) * 128, :])
                for g in range(4):
                    bk = 4 + g
                    for kc in range(NK):
                        ph.add("tensor", lambda e, bk=bk, kc=kc, g=g, ts=ts: e.matmul(
                            self.bank(bk), B[:, kc * S + ts * 128: kc * S + ts * 128 + 128],
                            wof[:, g * NK * 512 + kc * 512: g * NK * 512 + (kc + 1) * 512],
                            start=(kc == 0), stop=False))
                    upd = (mm.h, 1) if g == 3 else None
                    ph.add("tensor", lambda e, bk=bk, g=g, upd=upd: e.matmul(
                        self.bank(bk), self.ones1_bf[0:1, 0:128],
                        brow_o9[0:1, g * 512:(g + 1) * 512], start=False, stop=True).then_maybe_inc(upd))
                mm_ts9[ts] = mm.bump(1)
                self.wait(ph, "vector", mm, mm_ts9[ts])
                self.wait(ph, "vector", d_x2f[par], tx2)
                sv.bump(1)
                ph.add("vector", lambda e, m=sv.h: e.memset(stat3[:, 0:2], 0.0).then_inc(m, 1))
                self.wait(ph, "vector", sv, sv.v)
                tev = sv.bump(1)
                ph.add("vector", lambda e, par=par, m=sv.h: e.scalar_tensor_tensor(
                    y3_sb[:, :], self.PS[:, 4 * 512:8 * 512], 1.0,
                    x2ft[:, par * DM:(par + 1) * DM], ALU.mult, ALU.add,
                    accum_out=stat3[:, 0:1]).then_inc(m, 1))
                self.wait(ph, "tensor", sv, tev)
                self.wait(ph, "scalar", sv, tev)
                tsq = ss.bump(1)
                ph.add("scalar", lambda e, m=ss.h: e.activation(
                    sq3[:, :], y3_sb[:, :], AF.Square, accum_out=stat3[:, 1:2]).then_inc(m, 1))
                self.wait(ph, "vector", ss, tsq)
                tln = self.ln_tail(ph, y3_sb[:, :], stat3, x3t[:, :])
                ln_done9[ts] = tln
                self.wait(ph, "sync", sv, tln)
                self.wait(ph, "gpsimd", sv, tln)
                t_f32 = self.dma(ph, "sync", dx3, self.x3f.ap()[ts * 128:(ts + 1) * 128, :], x3t[:, :])
                t_bf = self.dma(ph, "gpsimd", dx3b, self.x3bf.ap()[ts * 128:(ts + 1) * 128, :], x3t[:, :])
                self.wait(ph, "vector", dx3, t_f32)
                self.wait(ph, "vector", dx3b, t_bf)
            ph.emit()
            sc6.close()
            if self.phase_done():
                return

    def stage4(self):
        nc = self.nc
        mm, sv, ss = self.sem["mm"], self.sem["sv"], self.sem["ss"]
        w = self.w
        NP = 4
        TP = S // NP
        C1 = np.float32(1.0 / (np.sqrt(2048.0) * 64.0))
        C2 = np.float32(1.0 / (np.sqrt(8192.0) * 64.0))
        dr = nc.dram_tensor
        kindd = lambda n: "ExternalOutput" if n in self.debug_taps else "Internal"
        if self.debug_taps:
            self.xtd = dr("xtd", [NK, 128, S], BF16, kind=kindd("xtd")).ap()
        else:
            self.xtd = self.ctxd.ap().rearrange("h s c -> (h s c)").rearrange("(k p s) -> k p s", p=128, s=S)

        with contextlib.ExitStack() as sc:
            sb = lambda name, shape, dt: sc.enter_context(nc.sbuf_tensor(name, shape, dt))
            HSU2 = sb("HSU2", [128, NFC * 128], BF16)
            HSU1 = sb("HSU1", [128, NK * 128], BF16)
            wsu_col = sb("wsu_col", [128, 128], F32)
            wsdpb = sb("wsdpb", [128, DM], F32)

            ph = Phase(nc)
            sc10 = contextlib.ExitStack()
            sb10 = lambda name, shape, dt: sc10.enter_context(nc.sbuf_tensor(name, shape, dt))
            self.drain_bg4(ph)
            d_wbf, d_wTl, d_wTsl = self._bg4_sems
            t_wT_ready = d_wTsl.v
            sus = sb10("sus", [64, 128], F32)
            sug = sb10("sug", [16, 128], F32)
            wss = sb10("wss", [128, 128], F32)
            wsd = sb10("wsd", [1, DM], F32)
            sucol1 = sb10("sucol1", [128, 16], F32)
            sucol2 = sb10("sucol2", [128, 64], F32)
            d_s4c = self.dsem("s4c")
            self.dma(ph, "sync", d_s4c, sug[:, :], w["L1_ffn_SU_ug"].ap().rearrange("(a b) -> a b", b=128))
            self.dma(ph, "sync", d_s4c, sus[:, :], w["L1_ffn_SU_dp"].ap().rearrange("(a b) -> a b", b=128))
            self.dma(ph, "sync", d_s4c, wss[:, :], w["L1_ffn_Wscale_ug"].ap().rearrange("(a b) -> a b", b=128))
            self.dma(ph, "sync", d_s4c, wsd[:, :], w["L1_ffn_Wscale_dp"].ap().unsqueeze(0))
            self.wait(ph, "tensor", d_s4c, d_s4c.v)
            tv_list = []
            for (stg, width, bank) in ((sug, 16, 0), (sus, 64, 1), (wss, 128, 2)):
                tmv = mm.bump(1)
                ph.add("tensor", lambda e, stg=stg, width=width, bank=bank, m=mm.h: e.transpose(
                    self.PS[0:128, bank * 512: bank * 512 + width],
                    stg[:, :], self.ident_f32[0:stg.shape[0], 0:stg.shape[0]]).then_inc(m, 1))
            self.wait(ph, "vector", mm, mm.v)
            t7 = self.vchain(ph, "vector", [
                lambda e: e.tensor_copy(sucol1[:, :], self.PS[:, 0:16]),
                lambda e: e.tensor_copy(sucol2[:, :], self.PS[:, 512:512 + 64]),
                lambda e: e.tensor_scalar(wsu_col[:, :], self.PS[:, 1024:1024 + 128], 64.0, None, ALU.mult),
                lambda e: e.tensor_scalar(wsd[0:1, :], wsd[0:1, :], 64.0, None, ALU.mult),
            ])
            self.wait(ph, "tensor", sv, t7)
            for gg in range(4):
                upd = (mm.h, 1) if gg == 3 else None
                if upd:
                    mm.bump(1)
                ph.add("tensor", lambda e, gg=gg, upd=upd: e.matmul(
                    self.bank(4 + gg), self.onesr_f32[0:1, :],
                    wsd[0:1, gg * 512:(gg + 1) * 512], start=True, stop=True).then_maybe_inc(upd))
            self.wait(ph, "vector", mm, mm.v)
            t8 = self.vchain(ph, "vector", [
                lambda e: e.tensor_copy(wsdpb[:, :], self.PS[:, 4 * 512:8 * 512]),
            ])
            hs_ops = []
            for dc in range(NK):
                hs_ops.append(lambda e, dc=dc: e.tensor_scalar(
                    HSU1[:, dc * 128:(dc + 1) * 128], self.h128[:, :],
                    sucol1[:, dc:dc + 1], None, ALU.mult))
            for fc in range(NFC):
                hs_ops.append(lambda e, fc=fc: e.tensor_scalar(
                    HSU2[:, fc * 128:(fc + 1) * 128], self.h128[:, :],
                    sucol2[:, fc:fc + 1], None, ALU.mult))
            self.wait(ph, "vector", sv, t8)
            for i, op in enumerate(hs_ops):
                if i == len(hs_ops) - 1:
                    tHSU = sv.bump(1)
                    ph.add("vector", lambda e, op=op, m=sv.h: op(e).then_inc(m, 1))
                else:
                    ph.add("vector", lambda e, op=op: op(e))
            ph.emit()
            sc10.close()
            if self.phase_done():
                return

            ph = Phase(nc)
            sc11 = contextlib.ExitStack()
            sb11 = lambda name, shape, dt: sc11.enter_context(nc.sbuf_tensor(name, shape, dt))
            B1 = sb11("s4B1", [128, NK * S], BF16)
            A1 = sb11("s4A1", [128, NK * S], BF16)
            d_xfm4 = self.dsem("xfm4")
            for dc in range(NK):
                self.dmaT(ph, "sync", d_xfm4, B1[:, dc * S:(dc + 1) * S],
                          self.x3bf.ap()[:, dc * 128:(dc + 1) * 128])
            self.wait(ph, "tensor", d_xfm4, d_xfm4.v)
            self.wait(ph, "tensor", sv, tHSU)
            ev_at = {}
            for dc in range(NK):
                for g in range(4):
                    i = dc * 4 + g
                    b = i % 2
                    if i >= 2:
                        self.wait(ph, "tensor", ss, ev_at[i - 2])
                    tmv = mm.bump(1)
                    ph.add("tensor", lambda e, b=b, dc=dc, g=g, m=mm.h: e.matmul(
                        self.bank(b), HSU1[:, dc * 128:(dc + 1) * 128],
                        B1[:, dc * S + g * 512: dc * S + (g + 1) * 512],
                        start=True, stop=True).then_inc(m, 1))
                    self.wait(ph, "scalar", mm, tmv)
                    tevs = ss.bump(1)
                    ph.add("scalar", lambda e, b=b, dc=dc, g=g, m=ss.h: e.activation(
                        A1[:, dc * S + g * 512: dc * S + (g + 1) * 512],
                        self.bank(b), AF.Identity, scale=C1).then_inc(m, 1))
                    ev_at[i] = tevs
            self.wait(ph, "vector", ss, ss.v)
            bufs = [A1, B1]
            tbf = None
            for s in range(4):
                blk = 1 << s
                src_t = bufs[s % 2]
                dst_t = bufs[(s + 1) % 2]
                msz = blk * S
                na = NK // (2 * blk)
                lowS = src_t[:, :].rearrange("p (a b m) -> p a b m", b=2, m=msz)[:, :, 0, :]
                highS = src_t[:, :].rearrange("p (a b m) -> p a b m", b=2, m=msz)[:, :, 1, :]
                lowD = dst_t[:, :].rearrange("p (a b m) -> p a b m", b=2, m=msz)[:, :, 0, :]
                highD = dst_t[:, :].rearrange("p (a b m) -> p a b m", b=2, m=msz)[:, :, 1, :]
                tbf = self.vchain(ph, "vector", [
                    lambda e, lowD=lowD, lowS=lowS, highS=highS: e.tensor_tensor(lowD, lowS, highS, ALU.add),
                    lambda e, highD=highD, lowS=lowS, highS=highS: e.tensor_tensor(highD, lowS, highS, ALU.subtract),
                ])
            self.wait(ph, "sync", sv, tbf)
            d_xt = self.dsem("xt")
            for dc in range(NK):
                self.dma(ph, "sync", d_xt, self.xtd[dc], A1[:, dc * S:(dc + 1) * S])
            self.wait(ph, "sync", d_xt, d_xt.v)
            ph.emit()
            sc11.close()
            if self.phase_done():
                return

            FG = 2
            for p in range(NP):
                ph = Phase(nc)
                scp = contextlib.ExitStack()
                sbp = lambda name, shape, dt: scp.enter_context(nc.sbuf_tensor(name, shape, dt))
                z1 = sbp("z1_%d" % p, [128, NFC * TP], BF16)
                xtp = sbp("xtp%d" % p, [128, NK * TP], BF16)
                wug = sbp("wug%d" % p, [128, 2 * 2 * NK * FG * 128], BF16)
                gbuf = sbp("gbuf%d" % p, [128, 2 * TP], BF16)
                ubuf2 = sbp("ubuf2%d" % p, [128, 2 * TP], BF16)
                hbuf = sbp("hbuf%d" % p, [128, 2 * TP], BF16)
                d_xtp = self.dsem("xtp")
                self.dma(ph, "sync", d_xtp, xtp[:, :].rearrange("p (k t) -> p k t", t=TP),
                         self.xtd[:, :, p * TP:(p + 1) * TP].rearrange("k q t -> q k t"))
                d_ug = [self.dsem("ug_0"), self.dsem("ug_1")]
                if p == 0:
                    self.wait(ph, "sync", d_wTsl, t_wT_ready)
                self.wait(ph, "tensor", d_xtp, d_xtp.v)
                WS = 2 * NK * FG * 128
                UGW = NK * FG * 128
                nfg = NFC // FG
                hz_done = {}
                ug_mm = {}
                for fg in range(nfg):
                    par = fg % 2
                    if fg >= 2:
                        self.wait(ph, "sync", mm, ug_mm[fg - 2])
                    tld = self.dma(ph, "sync", d_ug[par],
                                   wug[:, par * WS: par * WS + UGW].rearrange("p (k f) -> p k f", f=FG * 128),
                                   self.wupT.rearrange("(k q) f -> q k f", q=128)[:, :, fg * FG * 128:(fg + 1) * FG * 128])
                    tld2 = self.dma(ph, "sync", d_ug[par],
                                    wug[:, par * WS + UGW: par * WS + 2 * UGW].rearrange("p (k f) -> p k f", f=FG * 128),
                                    self.wgateT.rearrange("(k q) f -> q k f", q=128)[:, :, fg * FG * 128:(fg + 1) * FG * 128])
                    self.wait(ph, "tensor", d_ug[par], tld2)
                    for sub in range(FG):
                        fc = fg * FG + sub
                        ib = fc % 2
                        if fc >= 2:
                            self.wait(ph, "tensor", sv, hz_done[fc - 2][0])
                            self.wait(ph, "tensor", ss, hz_done[fc - 2][1])
                        for which in range(2):
                            bnk = which * 2 + ib
                            for kc in range(NK):
                                upd = (mm.h, 1) if kc == NK - 1 else None
                                if upd:
                                    mm.bump(1)
                                ph.add("tensor", lambda e, bnk=bnk, kc=kc, which=which, par=par, sub=sub, upd=upd: e.matmul(
                                    self.bank(bnk),
                                    wug[:, par * WS + which * UGW + kc * FG * 128 + sub * 128:
                                        par * WS + which * UGW + kc * FG * 128 + (sub + 1) * 128],
                                    xtp[:, kc * TP:(kc + 1) * TP],
                                    start=(kc == 0), stop=(kc == NK - 1)).then_maybe_inc(upd))
                        t_upgate = mm.v
                        self.wait(ph, "scalar", mm, t_upgate)
                        hp = fc % 2
                        tung = self.vchain(ph, "scalar", [
                            lambda e, ib=ib, hp=hp, fc=fc: e.activation(
                                ubuf2[:, hp * TP:(hp + 1) * TP], self.bank(ib),
                                AF.Identity, scale=wsu_col[:, fc:fc + 1]),
                            lambda e, ib=ib, hp=hp, fc=fc: e.activation(
                                gbuf[:, hp * TP:(hp + 1) * TP], self.bank(2 + ib),
                                AF.Silu, scale=wsu_col[:, 64 + fc:64 + fc + 1]),
                        ])
                        self.wait(ph, "vector", ss, tung)
                        th = sv.bump(1)
                        ph.add("vector", lambda e, hp=hp, m=sv.h: e.tensor_tensor(
                            hbuf[:, hp * TP:(hp + 1) * TP], gbuf[:, hp * TP:(hp + 1) * TP],
                            ubuf2[:, hp * TP:(hp + 1) * TP], ALU.mult).then_inc(m, 1))
                        self.wait(ph, "tensor", sv, th)
                        tmz = mm.bump(1)
                        ph.add("tensor", lambda e, fc=fc, hp=hp, ib=ib, m=mm.h: e.matmul(
                            self.bank(4 + ib), HSU2[:, fc * 128:(fc + 1) * 128],
                            hbuf[:, hp * TP:(hp + 1) * TP],
                            start=True, stop=True).then_inc(m, 1))
                        self.wait(ph, "scalar", mm, tmz)
                        tz = ss.bump(1)
                        ph.add("scalar", lambda e, fc=fc, ib=ib, m=ss.h: e.activation(
                            z1[:, fc * TP:(fc + 1) * TP], self.bank(4 + ib),
                            AF.Identity, scale=C2).then_inc(m, 1))
                        hz_done[fc] = (th, tz)
                    ug_mm[fg] = mm.v
                self.wait(ph, "vector", ss, ss.v)
                tbf2 = None
                for s in range(6):
                    blk = 1 << s
                    msz = blk * TP
                    low = z1[:, :].rearrange("p (a b m) -> p a b m", b=2, m=msz)[:, :, 0, :]
                    high = z1[:, :].rearrange("p (a b m) -> p a b m", b=2, m=msz)[:, :, 1, :]
                    tbf2 = self.vchain(ph, "vector", [
                        lambda e, low=low, high=high: e.tensor_tensor(high, low, high, ALU.subtract),
                        lambda e, low=low, high=high: e.scalar_tensor_tensor(low, low, 2.0, high, ALU.mult, ALU.subtract),
                    ])
                scp2 = contextlib.ExitStack()
                sbp2 = lambda name, shape, dt: scp2.enter_context(nc.sbuf_tensor(name, shape, dt))
                yacc = sbp2("yacc_%d" % p, [128, 4 * DM], F32)
                x3ft = sbp2("x3ft_%d" % p, [128, DM], F32)
                y4 = sbp2("y4_%d" % p, [128, DM], F32)
                sq4 = xtp[:, 0:2 * DM].bitcast(F32)
                stat4 = sbp2("stat4_%d" % p, [128, 8], F32)
                d_wd = self.dsem("wd")
                d_x3f = [self.dsem("x3f_0"), self.dsem("x3f_1")]
                d_out = self.dsem("outs")
                self.wait(ph, "tensor", sv, tbf2)
                ev_dn = {}
                NW = 256
                for n5 in range(DM // NW):
                    if n5 == 0:
                        self.wait(ph, "sync", mm, ug_mm[nfg - 1])
                    else:
                        self.wait(ph, "sync", mm, dn_mm)
                    tldw = self.dma(ph, "sync", d_wd,
                                    wug[:, 0:NFC * NW].rearrange("p (k f) -> p k f", f=NW),
                                    self.wdownT.rearrange("(k q) f -> q k f", q=128)[:, :, n5 * NW:(n5 + 1) * NW])
                    self.wait(ph, "tensor", d_wd, tldw)
                    for st in range(4):
                        bnk = 4 + st
                        if n5 >= 1:
                            self.wait(ph, "tensor", sv, ev_dn[(n5 - 1, st)])
                        for fc in range(NFC):
                            upd = (mm.h, 1) if fc == NFC - 1 else None
                            if upd:
                                mm.bump(1)
                            ph.add("tensor", lambda e, bnk=bnk, fc=fc, st=st, upd=upd: e.matmul(
                                self.bank(bnk, NW), z1[:, fc * TP + st * 128: fc * TP + (st + 1) * 128],
                                wug[:, fc * NW:(fc + 1) * NW],
                                start=(fc == 0), stop=(fc == NFC - 1)).then_maybe_inc(upd))
                        dn_mm = mm.v
                        self.wait(ph, "vector", mm, dn_mm)
                        tevd = sv.bump(1)
                        ph.add("vector", lambda e, st=st, n5=n5, m=sv.h: e.tensor_tensor(
                            yacc[:, st * DM + n5 * NW: st * DM + (n5 + 1) * NW],
                            self.PS[:, (4 + st) * 512:(4 + st) * 512 + NW],
                            wsdpb[:, n5 * NW:(n5 + 1) * NW], ALU.mult).then_inc(m, 1))
                        ev_dn[(n5, st)] = tevd
                ln_done4 = {}
                for st in range(4):
                    ts = p * 4 + st
                    if st >= 1:
                        self.wait(ph, "scalar", sv, ln_done4[st - 1])
                    tx3 = self.dma(ph, "scalar", d_x3f[0], x3ft[:, :],
                                   self.x3f.ap()[ts * 128:(ts + 1) * 128, :])
                    self.wait(ph, "vector", d_x3f[0], tx3)
                    self.wait(ph, "vector", sv, ev_dn[(DM // NW - 1, st)])
                    sv.bump(1)
                    ph.add("vector", lambda e, m=sv.h: e.memset(stat4[:, 0:2], 0.0).then_inc(m, 1))
                    self.wait(ph, "vector", sv, sv.v)
                    tev = sv.bump(1)
                    ph.add("vector", lambda e, st=st, m=sv.h: e.scalar_tensor_tensor(
                        y4[:, :], yacc[:, st * DM:(st + 1) * DM], 1.0,
                        x3ft[:, :], ALU.mult, ALU.add,
                        accum_out=stat4[:, 0:1]).then_inc(m, 1))
                    self.wait(ph, "scalar", sv, tev)
                    tsq = ss.bump(1)
                    ph.add("scalar", lambda e, m=ss.h: e.activation(
                        sq4[:, 0:DM], y4[:, :], AF.Square, accum_out=stat4[:, 1:2]).then_inc(m, 1))
                    self.wait(ph, "vector", ss, tsq)
                    tln = self.ln_tail(ph, y4[:, :], stat4, y4[:, :])
                    ln_done4[st] = tln
                    self.wait(ph, "sync", sv, tln)
                    tso = self.dma(ph, "sync", d_out, self.out.ap()[ts * 128:(ts + 1) * 128, :], y4[:, :])
                    self.wait(ph, "vector", d_out, tso)
                    self.wait(ph, "scalar", d_out, tso)
                if p == NP - 1:
                    self.wait(ph, "sync", d_out, d_out.v)
                ph.emit()
                scp2.close()
                scp.close()

    def finish_partial(self, src_dram):
        nc = self.nc
        ld, st = self.dsem("finl"), self.dsem("fins")
        with contextlib.ExitStack() as sc:
            buf = sc.enter_context(nc.sbuf_tensor("fin", [128, DM], F32))
            ph = Phase(nc)
            for ts in range(NT):
                t1 = self.dma(ph, "sync", ld, buf[:, :], src_dram.ap()[ts * 128:(ts + 1) * 128, :])
                self.wait(ph, "sync", ld, t1)
                t2 = self.dma(ph, "sync", st, self.out.ap()[ts * 128:(ts + 1) * 128, :], buf[:, :])
                self.wait(ph, "sync", st, t2)
            ph.emit()

    def build(self):
        if self.stages >= 4:
            self.prep_bg4()
        self.stage1()
        if self.max_phase < 99:
            self.stack.close()
            return self.nc
        if self.stages >= 2:
            self.stage2()
        if self.stages >= 3:
            self.stage3()
        if self.stages >= 4:
            self.stage4()
        else:
            src = {1: self.x1f, 2: self.x2f, 3: self.x3f}[self.stages]
            self.finish_partial(src)
        self.stack.close()
        return self.nc


def build_program(stages=4, debug_taps=(), max_phase=99):
    return Builder(stages=stages, debug_taps=debug_taps, max_phase=max_phase).build()


def make_in_maps(inputs, consts=None):
    consts = consts or host_constants()
    x = np.ascontiguousarray(inputs["L0_attn_x"])
    in_maps = []
    for core in range(8):
        m = {"x0": np.ascontiguousarray(x[core])}
        for name in WEIGHT_SPECS:
            m[name] = np.ascontiguousarray(inputs[name])
        for name in CONST_SPECS:
            m[name] = consts[name]
        in_maps.append(m)
    return in_maps


def kernel(**inputs):
    nc = build_program(stages=4)
    in_maps = make_in_maps(inputs)
    res = run_bass_kernel_spmd(nc, in_maps, core_ids=list(range(8)))
    return np.stack([r["out"] for r in res.results], axis=0)


# version 06
# speedup vs baseline: 1.1208x; 1.0111x over previous
import contextlib

import numpy as np
import ml_dtypes

import concourse.bass as bass
import concourse.mybir as mybir
from concourse.bass_utils import run_bass_kernel_spmd

F32 = mybir.dt.float32
BF16 = mybir.dt.bfloat16
AF = mybir.ActivationFunctionType
ALU = mybir.AluOpType
AX = mybir.AxisListType

S = 2048
DM = 2048
H = 16
HD = 128
NK = 16
NT = 16
NE = 8
INTER = 8192
NFC = 64
ALIBI_SHIFT = 14.0
INV_SQRT_HD = np.float32(1.0 / np.sqrt(np.float32(HD)))
LN_EPS = np.float32(1e-5)
VW = H * 129

ENGINES = ("tensor", "vector", "scalar", "sync", "gpsimd")
NO_AV_PIPELINE = False


class Sem:
    def __init__(self, nc, stack, name):
        self.h = stack.enter_context(nc.semaphore(name))
        self.v = 0

    def bump(self, k):
        self.v += k
        return self.v


class Phase:

    def __init__(self, nc):
        self.nc = nc
        self.ops = {e: [] for e in ENGINES}

    def add(self, eng, fn):
        self.ops[eng].append(fn)

    def emit(self):
        with self.nc.Block() as blk:
            for name in ENGINES:
                fns = self.ops[name]
                if fns:
                    getattr(blk, name)(lambda eng, fns=fns: [f(eng) for f in fns])
        self.ops = {e: [] for e in ENGINES}


def host_constants():
    c = {}
    c["ident_bf"] = np.eye(128, dtype=ml_dtypes.bfloat16)
    c["ident_f32"] = np.eye(128, dtype=np.float32)
    ratio = np.float32(2.0 ** (-8.0 / H))
    slopes = ratio ** np.arange(1, H + 1, dtype=np.float32)
    al = np.zeros((128, H * NT), dtype=np.float32)
    p = np.arange(128, dtype=np.float32)
    for h in range(H):
        for d in range(NT):
            al[:, h * NT + d] = slopes[h] * (p - 128.0 * d - 64.0) - 6.0
    c["alibi"] = al
    p = np.arange(128)[:, None]
    f = np.arange(128)[None, :]
    c["trimask"] = (p <= f).astype(ml_dtypes.bfloat16)
    hm = np.array([[1.0]])
    while hm.shape[0] < 128:
        hm = np.block([[hm, hm], [hm, -hm]])
    c["h128"] = hm.astype(ml_dtypes.bfloat16)
    return c


CONST_SPECS = {
    "ident_bf": ([128, 128], BF16),
    "ident_f32": ([128, 128], F32),
    "alibi": ([128, H * NT], F32),
    "trimask": ([128, 128], BF16),
    "h128": ([128, 128], BF16),
}

WEIGHT_SPECS = {
    "L0_attn_Wq": ([DM, DM], F32), "L0_attn_bq": ([DM], F32),
    "L0_attn_Wk": ([DM, DM], F32), "L0_attn_bk": ([DM], F32),
    "L0_attn_Wv": ([DM, DM], F32), "L0_attn_bv": ([DM], F32),
    "L0_attn_Wo": ([DM, DM], F32), "L0_attn_bo": ([DM], F32),
    "L0_ffn_switch_w": ([NE, S * DM], F32), "L0_ffn_switch_b": ([NE], F32),
    "L0_ffn_w1": ([NE, 128, 128], F32), "L0_ffn_b1": ([NE, 128], F32),
    "L0_ffn_w2": ([NE, 128, 128], F32), "L0_ffn_b2": ([NE, 128], F32),
    "L0_ffn_w3": ([DM, DM], F32), "L0_ffn_b3": ([DM], F32),
    "L1_attn_Wqk": ([DM, DM], F32), "L1_attn_bqk": ([DM], F32),
    "L1_attn_Wv": ([DM, DM], F32), "L1_attn_bv": ([DM], F32),
    "L1_attn_Wu": ([DM, DM], F32), "L1_attn_bu": ([DM], F32),
    "L1_attn_Wo": ([DM, DM], F32), "L1_attn_bo": ([DM], F32),
    "L1_attn_ln_g": ([DM], F32), "L1_attn_ln_b": ([DM], F32),
    "L1_ffn_SU_ug": ([DM], F32), "L1_ffn_SU_dp": ([INTER], F32),
    "L1_ffn_Wscale_ug": ([2 * INTER], F32), "L1_ffn_Wscale_dp": ([DM], F32),
    "L1_ffn_W_up": ([INTER, DM], F32), "L1_ffn_W_gate": ([INTER, DM], F32),
    "L1_ffn_W_down": ([DM, INTER], F32),
}


class Builder:
    def __init__(self, stages=4, debug_taps=(), max_phase=99):
        self.max_phase = max_phase
        self._phase_no = 0
        self.stages = stages
        self.debug_taps = set(debug_taps)
        self.nc = bass.Bass("TRN2", target_bir_lowering=False, detect_race_conditions=False)
        self.stack = contextlib.ExitStack()
        nc = self.nc
        self.x0 = nc.dram_tensor("x0", [S, DM], F32, kind="ExternalInput")
        self.out = nc.dram_tensor("out", [S, DM], F32, kind="ExternalOutput")
        self.w = {n: nc.dram_tensor(n, sh, dt, kind="ExternalInput")
                  for n, (sh, dt) in WEIGHT_SPECS.items()}
        self.c = {n: nc.dram_tensor(n, sh, dt, kind="ExternalInput")
                  for n, (sh, dt) in CONST_SPECS.items()}
        kind = lambda n: "ExternalOutput" if n in self.debug_taps else "Internal"
        dr = nc.dram_tensor
        dbg = bool(self.debug_taps)
        if dbg:
            self.x0bf = dr("x0bf", [S, DM], BF16, kind=kind("x0bf"))
            self.x1f = dr("x1f", [S, DM], F32, kind=kind("x1f"))
            self.x1bf = dr("x1bf", [S, DM], BF16, kind=kind("x1bf"))
            self.x2f = dr("x2f", [S, DM], F32, kind=kind("x2f"))
            self.x2bf = dr("x2bf", [S, DM], BF16, kind=kind("x2bf"))
            self.x3f = dr("x3f", [S, DM], F32, kind=kind("x3f"))
            self.x3bf = dr("x3bf", [S, DM], BF16, kind=kind("x3bf"))
        else:
            xres = dr("xres", [S, DM], F32, kind="Internal")
            xbf = dr("xbf", [S, DM], BF16, kind="Internal")
            self.x0bf = xbf
            self.x1f = self.x2f = self.x3f = xres
            self.x1bf = self.x2bf = self.x3bf = xbf
        self.ctxd = dr("ctxd", [H, S, HD], BF16, kind=kind("ctxd"))
        if "vtmd" in self.debug_taps:
            self.vtmd = dr("vtmd", [128, NT * VW], BF16, kind="ExternalOutput")
        if "qkd" in self.debug_taps:
            self.qkd = dr("qkd", [2, 128, 2 * S], BF16, kind="ExternalOutput")
        if "expd" in self.debug_taps:
            self.expd = dr("expd", [16, 128, 128], BF16, kind="ExternalOutput")
        if "accq" in self.debug_taps:
            self.accq = dr("accq", [2, NT, 128, 129], F32, kind="ExternalOutput")
        if "ysd" in self.debug_taps:
            self.ysd = dr("ysd", [NT, 128, DM], F32, kind="ExternalOutput")
            self.std = dr("std", [NT, 128, 8], F32, kind="ExternalOutput")
        if "accd" in self.debug_taps:
            self.accd = dr("accd", [128, 129], F32, kind="ExternalOutput")
        self.ctxdF = dr("ctxdF", [H, HD, S], BF16, kind=kind("ctxdF"))
        al = nc.alloc_sbuf_tensor
        self.ident_bf = al("identbf", [128, 128], BF16)
        self.ident_f32 = al("identf32", [128, 128], F32)
        self.alibi = al("alibisb", [128, H * NT], F32)
        self.trimask = al("trimasksb", [128, 128], BF16)
        self.h128 = al("h128sb", [128, 128], BF16)
        self.ones1_bf = al("ones1bf", [1, 512], BF16)
        self.ones1_f32 = al("ones1f32", [128, 1], F32)
        self.onesr_f32 = al("onesrf32", [1, 128], F32)
        self.PS = nc.alloc_psum_tensor("PS", [128, 4096], F32)
        self.wstg_bg = al("wstgbg", [128, INTER // 2], BF16)
        self.sem = {}
        for name in ("mm", "sv", "ss", "sg"):
            self.sem[name] = Sem(nc, self.stack, "s_" + name)
        self._consts_loaded = False
        self._bg4 = None

    def phase_done(self):
        self._phase_no += 1
        return self._phase_no >= self.max_phase

    def bank(self, i, width=512):
        assert 0 <= i < 8
        return self.PS[:, i * 512:i * 512 + width]

    def dma(self, ph, eng, sem, dst, src, **kw):
        tgt = sem.bump(16)
        ph.add(eng, lambda e, d=dst, s=src, m=sem.h, kw=kw:
               e.dma_start(d, s, **kw).then_inc(m, 16))
        return tgt

    def dmaT(self, ph, eng, sem, dst, src):
        tgt = sem.bump(16)
        ph.add(eng, lambda e, d=dst, s=src, m=sem.h:
               e.dma_start_transpose(d, s).then_inc(m, 16))
        return tgt

    def wait(self, ph, eng, sem, val):
        ph.add(eng, lambda e, m=sem.h, v=val: e.wait_ge(m, v))

    def mmdone(self, ph):
        mm = self.sem["mm"]
        t = mm.bump(1)
        ph.add("tensor", lambda e, m=mm.h: e.nop().then_inc(m, 1))
        return t

    def prep_bg4(self):
        nc = self.nc
        dr = nc.dram_tensor
        N1 = dr("wN1", [INTER * DM], BF16, kind="Internal")
        N2 = dr("wN2", [INTER * DM], BF16, kind="Internal")
        T1 = dr("wT1f", [INTER * DM], BF16, kind="Internal")
        T2 = dr("wT2f", [INTER * DM], BF16, kind="Internal")
        self.wupbf = N1.ap().rearrange("(a b) -> a b", b=DM)
        self.wgatebf = N2.ap().rearrange("(a b) -> a b", b=DM)
        self.wdownbf = N1.ap().rearrange("(a b) -> a b", b=INTER)
        self.wupT = T1.ap().rearrange("(a b) -> a b", b=INTER)
        self.wgateT = T2.ap().rearrange("(a b) -> a b", b=INTER)
        self.wdownT = N2.ap().rearrange("(a b) -> a b", b=DM)
        d_wbf = self.dsem("wbf")
        d_wT = self.dsem("wTl")
        d_wTs = self.dsem("wTsl")
        wstg = self.wstg_bg
        jobs = []
        RC = INTER // 16

        def cast_chunk(wname, dst, r0, rows):
            def f(ph):
                self.dma(ph, "gpsimd", d_wbf, dst[r0:r0 + rows, :],
                         self.w[wname].ap()[r0:r0 + rows, :])
            return f

        def tjob(srcv, dstv, kc, width, wait_cast_to):
            def f(ph):
                self.wait(ph, "sync", d_wbf, wait_cast_to)
                for h0 in range(0, width, INTER // 2):
                    hw = min(INTER // 2, width - h0)
                    if d_wTs.v > 0:
                        self.wait(ph, "sync", d_wTs, d_wTs.v)
                    t1 = self.dmaT(ph, "sync", d_wT, wstg[:, 0:hw],
                                   srcv[h0:h0 + hw, kc * 128:(kc + 1) * 128])
                    self.wait(ph, "sync", d_wT, t1)
                    self.dma(ph, "sync", d_wTs,
                             dstv[kc * 128:(kc + 1) * 128, h0:h0 + hw],
                             wstg[:, 0:hw])
            return f

        for i in range(16):
            jobs.append(cast_chunk("L1_ffn_W_up", self.wupbf, i * RC, RC))
        for i in range(16):
            jobs.append(cast_chunk("L1_ffn_W_gate", self.wgatebf, i * RC, RC))
        for kc in range(NK):
            jobs.append(tjob(self.wupbf, self.wupT, kc, INTER, 16 * 16))
        for kc in range(NK):
            jobs.append(tjob(self.wgatebf, self.wgateT, kc, INTER, 32 * 16))

        def dn_cast_chunk(r0):
            def f(ph):
                self.wait(ph, "gpsimd", d_wTs, d_wTs.v)
                self.dma(ph, "gpsimd", d_wbf, self.wdownbf[r0:r0 + 256, :],
                         self.w["L1_ffn_W_down"].ap()[r0:r0 + 256, :])
            return f

        for i in range(8):
            jobs.append(dn_cast_chunk(i * 256))
        for fc in range(NFC):
            jobs.append(tjob(self.wdownbf, self.wdownT, fc, DM, 40 * 16))
        self._bg4 = jobs
        self._bg4_sems = (d_wbf, d_wT, d_wTs)

    def pump_bg4(self, ph, n):
        if not self._bg4:
            return
        for _ in range(min(n, len(self._bg4))):
            self._bg4.pop(0)(ph)

    def drain_bg4(self, ph):
        self.pump_bg4(ph, 10 ** 9)

    def dsem(self, name):
        if name not in self.sem:
            self.sem[name] = Sem(self.nc, self.stack, "d_" + name)
        return self.sem[name]

    def load_consts(self, ph):
        dc = self.dsem("cst")
        for name, dst in (("ident_bf", self.ident_bf), ("ident_f32", self.ident_f32),
                          ("alibi", self.alibi), ("trimask", self.trimask),
                          ("h128", self.h128)):
            self.dma(ph, "sync", dc, dst[:, :], self.c[name].ap())
        ph.add("gpsimd", lambda e: e.memset(self.ones1_bf[:, :], 1.0))
        ph.add("gpsimd", lambda e: e.memset(self.ones1_f32[:, :], 1.0))
        ph.add("gpsimd", lambda e: e.memset(self.onesr_f32[:, :], 1.0))
        return dc

    def bias_row(self, ph, sc, name, wname, dsem_name="brow"):
        n = WEIGHT_SPECS[wname][0][0]
        buf = sc.enter_context(self.nc.sbuf_tensor(name, [1, n], BF16))
        ds = self.dsem(dsem_name)
        t = self.dma(ph, "gpsimd", ds, buf[:, :], self.w[wname].ap().unsqueeze(0))
        return buf, ds, t

    def bias_cols(self, ph, sc, name, wname, dc):
        nc = self.nc
        mm, sv = self.sem["mm"], self.sem["sv"]
        stg = sc.enter_context(nc.sbuf_tensor(name + "_stg", [16, 128], F32))
        col = sc.enter_context(nc.sbuf_tensor(name, [128, 16], F32))
        self.dma(ph, "sync", dc, stg[:, :],
                 self.w[wname].ap().rearrange("(a b) -> a b", b=128))
        self._pending_cols.append((stg, col))
        return col

    def flush_bias_cols(self, ph, dc):
        mm, sv = self.sem["mm"], self.sem["sv"]
        self.wait(ph, "tensor", dc, dc.v)
        out = 0
        for stg, col in self._pending_cols:
            tm = mm.bump(1)
            ph.add("tensor", lambda e, stg=stg, m=mm.h: e.transpose(
                self.bank(7, 16), stg[:, :], self.ident_f32[0:16, 0:16]).then_inc(m, 1))
            self.wait(ph, "vector", mm, tm)
            tv = sv.bump(1)
            ph.add("vector", lambda e, col=col, m=sv.h: e.tensor_copy(
                col[:, :], self.bank(7, 16)).then_inc(m, 1))
            self.wait(ph, "tensor", sv, tv)
            out = tv
        self._pending_cols = []
        return out

    def vchain(self, ph, eng, fns):
        sem = {"vector": self.sem["sv"], "scalar": self.sem["ss"], "tensor": self.sem["mm"],
               "gpsimd": self.sem["sg"]}[eng]
        t = None
        for i, fn in enumerate(fns):
            if t is not None:
                self.wait(ph, eng, sem, t)
            t = sem.bump(1)
            ph.add(eng, lambda e, fn=fn, m=sem.h: fn(e).then_inc(m, 1))
        return t

    def ln_tail(self, ph, y_sb, stat, out_dst):
        sv, ss = self.sem["sv"], self.sem["ss"]
        inv_n = 1.0 / DM
        A = ALU
        t1 = self.vchain(ph, "vector", [
            lambda e: e.tensor_scalar(stat[:, 2:3], stat[:, 0:1], inv_n, None, A.mult),
            lambda e: e.tensor_scalar(stat[:, 3:4], stat[:, 1:2], inv_n, None, A.mult),
            lambda e: e.tensor_tensor(stat[:, 4:5], stat[:, 2:3], stat[:, 2:3], A.mult),
            lambda e: e.tensor_tensor(stat[:, 4:5], stat[:, 3:4], stat[:, 4:5], A.subtract),
        ])
        self.wait(ph, "scalar", sv, t1)
        ta2 = ss.bump(1)
        ph.add("scalar", lambda e, m=ss.h: e.activation(
            stat[:, 5:6], stat[:, 4:5], AF.Sqrt, bias=np.float32(LN_EPS)).then_inc(m, 1))
        self.wait(ph, "vector", ss, ta2)
        tv = self.vchain(ph, "vector", [
            lambda e: e.reciprocal(stat[:, 6:7], stat[:, 5:6]),
            lambda e: e.tensor_tensor(stat[:, 7:8], stat[:, 6:7], stat[:, 6:7], A.mult),
            lambda e: e.tensor_scalar(stat[:, 4:5], stat[:, 4:5], float(LN_EPS), 0.5, A.add, A.mult),
            lambda e: e.tensor_tensor(stat[:, 7:8], stat[:, 4:5], stat[:, 7:8], A.mult),
            lambda e: e.tensor_scalar(stat[:, 7:8], stat[:, 7:8], -1.0, 1.5, A.mult, A.add),
            lambda e: e.tensor_tensor(stat[:, 6:7], stat[:, 6:7], stat[:, 7:8], A.mult),
            lambda e: e.tensor_scalar(out_dst, y_sb, stat[:, 2:3], stat[:, 6:7], A.subtract, A.mult),
        ])
        return tv

    def stage1(self):
        nc = self.nc
        mm, sv, ss = self.sem["mm"], self.sem["sv"], self.sem["ss"]
        self._pending_cols = []

        with contextlib.ExitStack() as sc:
            sb = lambda name, shape, dt: sc.enter_context(nc.sbuf_tensor(name, shape, dt))
            xfm = sb("xfm", [128, NK * S], BF16)
            vtm = sb("vtm", [128, NT * VW], BF16)

            ph = Phase(nc)
            dc = self.load_consts(ph)
            dg = self.dsem("g1")
            t_x0bf = self.dma(ph, "gpsimd", dg, self.x0bf.ap(), self.x0.ap())
            brow, d_brow, _ = self.bias_row(ph, sc, "brow_v", "L0_attn_bv")
            bqcol = self.bias_cols(ph, sc, "bqcol", "L0_attn_bq", dc)
            bkcol = self.bias_cols(ph, sc, "bkcol", "L0_attn_bk", dc)
            t_cols = self.flush_bias_cols(ph, dc)
            dxf = self.dsem("xfm")
            self.wait(ph, "sync", dg, t_x0bf)
            for dcc in range(NK):
                self.dmaT(ph, "sync", dxf, xfm[:, dcc * S:(dcc + 1) * S],
                          self.x0bf.ap()[:, dcc * 128:(dcc + 1) * 128])
            for ts in range(NT):
                vt = vtm[:, ts * VW:(ts + 1) * VW]
                ph.add("gpsimd", lambda e, vt=vt: e.memset(
                    vt.rearrange("p (h c) -> p h c", c=129)[:, :, 128:129], 1.0))
            ph.emit()
            if self.phase_done():
                return

            sc1 = contextlib.ExitStack()
            ph = Phase(nc)
            wvs = sc1.enter_context(nc.sbuf_tensor("wvs", [128, 2 * NK * 512], BF16))
            self.wait(ph, "tensor", dxf, dxf.v)
            self.wait(ph, "tensor", dc, dc.v)
            self.wait(ph, "tensor", d_brow, d_brow.v)
            dwv = [self.dsem("wv0"), self.dsem("wv1")]
            evac_at = {}
            mm_g = {}
            for g in range(4):
                par = g % 2
                buf = wvs[:, par * NK * 512:(par + 1) * NK * 512]
                if g >= 2:
                    self.wait(ph, "gpsimd", mm, mm_g[g - 2])
                twv = self.dma(
                    ph, "gpsimd", dwv[par], buf.rearrange("p (kc n) -> p kc n", n=512),
                    self.w["L0_attn_Wv"].ap().rearrange("(kc p) n -> p kc n", p=128)[:, :, g * 512:(g + 1) * 512])
                self.wait(ph, "tensor", dwv[par], twv)
                for ts in range(NT):
                    b = (g * NT + ts) % 2
                    if b in evac_at:
                        self.wait(ph, "tensor", sv, evac_at[b])
                    for kc in range(NK):
                        ph.add("tensor", lambda e, b=b, kc=kc, ts=ts, buf=buf: e.matmul(
                            self.bank(b),
                            xfm[:, kc * S + ts * 128: kc * S + ts * 128 + 128],
                            buf[:, kc * 512:(kc + 1) * 512],
                            start=(kc == 0), stop=False))
                    tmv = mm.bump(1)
                    ph.add("tensor", lambda e, b=b, g=g, m=mm.h: e.matmul(
                        self.bank(b), self.ones1_bf[0:1, 0:128],
                        brow[0:1, g * 512:(g + 1) * 512], start=False, stop=True).then_inc(m, 1))
                    self.wait(ph, "vector", mm, tmv)
                    vt = vtm[:, ts * VW:(ts + 1) * VW]
                    dst = vt.rearrange("p (h c) -> p h c", c=129)[:, 4 * g:4 * g + 4, 0:128]
                    tev = sv.bump(1)
                    ph.add("vector", lambda e, b=b, dst=dst, m=sv.h: e.tensor_copy(
                        dst, self.bank(b).rearrange("p (h c) -> p h c", c=128)).then_inc(m, 1))
                    evac_at[b] = tev
                mm_g[g] = mm.v
            ph.emit()
            sc1.close()
            if self.phase_done():
                return

            ph = Phase(nc)
            wqh = sc.enter_context(nc.sbuf_tensor("wqh", [128, 2 * NK * 128], BF16))
            wkh = sc.enter_context(nc.sbuf_tensor("wkh", [128, 2 * NK * 128], BF16))
            qh = sc.enter_context(nc.sbuf_tensor("qh", [128, 2 * S], BF16))
            kh = sc.enter_context(nc.sbuf_tensor("kh", [128, 2 * S], BF16))
            expb = sc.enter_context(nc.sbuf_tensor("expb", [128, 3 * 128], BF16))
            rcol = sc.enter_context(nc.sbuf_tensor("rcol", [128, 2], F32))
            ctx_sb = sc.enter_context(nc.sbuf_tensor("ctx_sb", [128, S], BF16))
            self.wait(ph, "scalar", dc, dc.v)
            self.wait(ph, "vector", dc, dc.v)
            if "accq" in self.debug_taps:
                self._dbg_accs = sc.enter_context(nc.sbuf_tensor("accs", [128, 129], F32))
            dwq = [self.dsem("wq0"), self.dsem("wq1")]
            dwk = [self.dsem("wk0"), self.dsem("wk1")]
            dcx = self.dsem("ctxs")
            mm_headqk = {}
            ctx_store = {}
            ctx_ev = {}
            evq_at = {}
            for h in range(H):
                par = h % 2
                if h >= 2:
                    self.wait(ph, "gpsimd", mm, mm_headqk[h - 2])
                if h >= 1:
                    self.pump_bg4(ph, 2)
                tq = self.dma(ph, "gpsimd", dwq[par],
                              wqh[:, par * NK * 128:(par + 1) * NK * 128].rearrange("p (kc n) -> p kc n", n=128),
                              self.w["L0_attn_Wq"].ap().rearrange("(kc p) n -> p kc n", p=128)[:, :, h * 128:(h + 1) * 128])
                tk = self.dma(ph, "gpsimd", dwk[par],
                              wkh[:, par * NK * 128:(par + 1) * NK * 128].rearrange("p (kc n) -> p kc n", n=128),
                              self.w["L0_attn_Wk"].ap().rearrange("(kc p) n -> p kc n", p=128)[:, :, h * 128:(h + 1) * 128])
                if h == 0:
                    self.wait(ph, "scalar", sv, t_cols)
                for wsl, dstt, dws, tw, bi, bcolx in ((wqh, qh, dwq[par], tq, 0, bqcol),
                                                      (wkh, kh, dwk[par], tk, 1, bkcol)):
                    self.wait(ph, "tensor", dws, tw)
                    for g in range(4):
                        if bi in evq_at:
                            self.wait(ph, "tensor", ss, evq_at[bi])
                        tmv = mm.bump(1)
                        for kc in range(NK):
                            ph.add("tensor", lambda e, b=bi, kc=kc, g=g, wsl=wsl, par=par, m=mm.h: e.matmul(
                                self.bank(b),
                                wsl[:, par * NK * 128 + kc * 128: par * NK * 128 + (kc + 1) * 128],
                                xfm[:, kc * S + g * 512: kc * S + (g + 1) * 512],
                                start=(kc == 0), stop=(kc == NK - 1)).then_maybe_inc((m, 1) if kc == NK - 1 else None))
                        self.wait(ph, "scalar", mm, tmv)
                        tev = ss.bump(1)
                        ph.add("scalar", lambda e, b=bi, g=g, dstt=dstt, par=par, bc=bcolx[:, h:h + 1], m=ss.h: e.activation(
                            dstt[:, par * S + g * 512: par * S + (g + 1) * 512],
                            self.bank(b), AF.Identity, bias=bc).then_inc(m, 1))
                        evq_at[bi] = tev
                self.wait(ph, "tensor", ss, ss.v)
                mm_headqk[h] = mm.v
                if h >= 1:
                    self.wait(ph, "scalar", dcx, ctx_store[h - 1])
                for qt in range(NT):
                    njt = qt + 1
                    cb = 2 + qt % 2
                    if cb in ctx_ev:
                        self.wait(ph, "tensor", ss, ctx_ev[cb])
                    av_pend = None
                    for jt in range(njt):
                        b = jt % 2
                        rot = jt % 3
                        tmv = mm.bump(1)
                        ph.add("tensor", lambda e, b=b, jt=jt, qt=qt, par=par, m=mm.h: e.matmul(
                            self.bank(b, 128),
                            kh[:, par * S + jt * 128: par * S + (jt + 1) * 128],
                            qh[:, par * S + qt * 128: par * S + (qt + 1) * 128],
                            start=True, stop=True).then_inc(m, 1))
                        self.wait(ph, "scalar", mm, tmv)
                        tex = ss.bump(1)
                        ph.add("scalar", lambda e, b=b, rot=rot, h=h, d=qt - jt, m=ss.h: e.activation(
                            expb[:, rot * 128:(rot + 1) * 128], self.bank(b, 128), AF.Exp,
                            scale=INV_SQRT_HD,
                            bias=self.alibi[:, h * NT + d: h * NT + d + 1]).then_inc(m, 1))
                        exdep = (ss, tex)
                        if jt == qt:
                            self.wait(ph, "vector", ss, tex)
                            tmx = sv.bump(1)
                            ph.add("vector", lambda e, rot=rot, m=sv.h: e.tensor_mul(
                                expb[:, rot * 128:(rot + 1) * 128],
                                expb[:, rot * 128:(rot + 1) * 128],
                                self.trimask[:, 0:128]).then_inc(m, 1))
                            exdep = (sv, tmx)
                        if av_pend is not None:
                            self._emit_av(ph, av_pend, h, expb, vtm)
                        av_pend = (jt, rot, exdep, qt, njt)
                        if NO_AV_PIPELINE:
                            if jt == njt - 1:
                                tmv = mm.bump(1)
                            self._emit_av(ph, av_pend, h, expb, vtm, inc=(jt == njt - 1))
                            av_pend = None
                    if not NO_AV_PIPELINE:
                        tmv = mm.bump(1)
                        self._emit_av(ph, av_pend, h, expb, vtm, inc=True)
                    self.wait(ph, "vector", mm, tmv)
                    if h in (0, 15) and "accq" in self.debug_taps:
                        hi = 0 if h == 0 else 1
                        taq = sv.bump(1)
                        ph.add("vector", lambda e, cb=cb, m=sv.h: e.tensor_copy(
                            self._dbg_accs[:, :], self.PS[:, cb * 512: cb * 512 + 129]).then_inc(m, 1))
                        self.wait(ph, "sync", sv, taq)
                        tdq = self.dma(ph, "sync", self.dsem("accq"), self.accq.ap()[hi, qt, :, :], self._dbg_accs[:, :])
                        self.wait(ph, "vector", self.dsem("accq"), tdq)
                    if cb in ctx_ev:
                        self.wait(ph, "vector", ss, ctx_ev[cb])
                    rsl = qt % 2
                    tvv = sv.bump(1)
                    ph.add("vector", lambda e, cb=cb, rsl=rsl, m=sv.h: e.reciprocal(
                        rcol[:, rsl:rsl + 1],
                        self.PS[:, cb * 512 + 128: cb * 512 + 129]).then_inc(m, 1))
                    self.wait(ph, "scalar", sv, tvv)
                    tev = ss.bump(1)
                    ph.add("scalar", lambda e, cb=cb, qt=qt, rsl=rsl, m=ss.h: e.activation(
                        ctx_sb[:, qt * 128:(qt + 1) * 128],
                        self.PS[:, cb * 512: cb * 512 + 128],
                        AF.Identity, scale=rcol[:, rsl:rsl + 1]).then_inc(m, 1))
                    ctx_ev[cb] = tev
                self.wait(ph, "sync", ss, ss.v)
                ctx_store[h] = self.dma(ph, "sync", dcx,
                                        self.ctxd.ap()[h].rearrange("(b p) c -> p b c", p=128),
                                        ctx_sb[:, :].rearrange("p (b c) -> p b c", c=128))
            ph.emit()
            t_ctx_total = dcx.v
            if self.phase_done():
                return

        with contextlib.ExitStack() as sc:
            sb = lambda name, shape, dt: sc.enter_context(nc.sbuf_tensor(name, shape, dt))
            ph = Phase(nc)
            ctxf = sb("ctxf", [128, 4 * S], BF16)
            wos = sb("wos", [128, 4 * NK * 512], BF16)
            x0t = sb("x0t", [128, 2 * DM], F32)
            y_sb = sb("y_sb", [128, DM], F32)
            sq_sb = sb("sq_sb", [128, DM], F32)
            x1t = sb("x1t", [128, DM], F32)
            stat = sb("stat", [128, 8], F32)
            browo, d_bro, _ = self.bias_row(ph, sc, "brow_o2", "L0_attn_bo", "bro")
            dcx = self.dsem("ctxs")
            dfT = [self.dsem("cfT0"), self.dsem("cfT1")]
            dcF = [self.dsem("cF0"), self.dsem("cF1")]
            dct = [self.dsem("cts0"), self.dsem("cts1")]
            dwo = self.dsem("wo")
            dx0 = [self.dsem("x00"), self.dsem("x01")]
            dx1 = self.dsem("x1s")
            dx1b = self.dsem("x1b")
            self.wait(ph, "sync", dcx, t_ctx_total)
            tF = {}
            for h in range(H):
                par = h % 2
                buf = ctxf[:, par * S:(par + 1) * S]
                if h >= 2:
                    self.wait(ph, "sync", dcF[par], tF[h - 2])
                t1 = self.dmaT(ph, "sync", dfT[par], buf, self.ctxd.ap()[h])
                self.wait(ph, "sync", dfT[par], t1)
                tF[h] = self.dma(ph, "sync", dcF[par], self.ctxdF.ap()[h], buf)
            for g in range(4):
                bufw = wos[:, g * NK * 512:(g + 1) * NK * 512]
                self.dma(ph, "gpsimd", dwo,
                         bufw.rearrange("p (kc n) -> p kc n", n=512),
                         self.w["L0_attn_Wo"].ap().rearrange("(kc p) n -> p kc n", p=128)[:, :, g * 512:(g + 1) * 512])
            self.wait(ph, "sync", dcF[0], dcF[0].v)
            self.wait(ph, "sync", dcF[1], dcF[1].v)
            mm_ts = {}
            ln_done = {}
            for ts in range(NT):
                par = ts % 4
                buf = ctxf[:, par * S:(par + 1) * S]
                self.pump_bg4(ph, 1)
                if ts >= 4:
                    self.wait(ph, "sync", mm, mm_ts[ts - 4])
                tld = self.dma(ph, "sync", dct[par % 2],
                               buf.rearrange("p (h q) -> p h q", q=128),
                               self.ctxdF.ap()[:, :, ts * 128:(ts + 1) * 128].rearrange("h p q -> p h q"))
                if ts >= 2:
                    self.wait(ph, "scalar", sv, ln_done[ts - 2])
                tx0 = self.dma(ph, "scalar", dx0[par % 2], x0t[:, (par % 2) * DM:(par % 2 + 1) * DM],
                               self.x0.ap()[ts * 128:(ts + 1) * 128, :])
                self.wait(ph, "tensor", dct[par % 2], tld)
                if ts == 0:
                    self.wait(ph, "tensor", d_bro, d_bro.v)
                    self.wait(ph, "tensor", dwo, dwo.v)
                for g in range(4):
                    bk = 4 + g
                    for hc in range(H):
                        ph.add("tensor", lambda e, bk=bk, hc=hc, g=g, buf=buf: e.matmul(
                            self.bank(bk), buf[:, hc * 128:(hc + 1) * 128],
                            wos[:, g * NK * 512 + hc * 512: g * NK * 512 + (hc + 1) * 512],
                            start=(hc == 0), stop=False))
                    upd = (mm.h, 1) if g == 3 else None
                    ph.add("tensor", lambda e, bk=bk, g=g, upd=upd: e.matmul(
                        self.bank(bk), self.ones1_bf[0:1, 0:128],
                        browo[0:1, g * 512:(g + 1) * 512], start=False, stop=True).then_maybe_inc(upd))
                mm_ts[ts] = mm.bump(1)
                self.wait(ph, "vector", mm, mm_ts[ts])
                self.wait(ph, "vector", dx0[par % 2], tx0)
                sv.bump(1)
                ph.add("vector", lambda e, m=sv.h: e.memset(stat[:, 0:2], 0.0).then_inc(m, 1))
                self.wait(ph, "vector", sv, sv.v)
                tev = sv.bump(1)
                ph.add("vector", lambda e, par=par % 2, m=sv.h: e.scalar_tensor_tensor(
                    y_sb[:, :], self.PS[:, 4 * 512:8 * 512], 1.0,
                    x0t[:, par * DM:(par + 1) * DM], ALU.mult, ALU.add,
                    accum_out=stat[:, 0:1]).then_inc(m, 1))
                self.wait(ph, "tensor", sv, tev)
                self.wait(ph, "scalar", sv, tev)
                if "ysd" in self.debug_taps:
                    tdy = self.dma(ph, "scalar", self.dsem("ysd"), self.ysd.ap()[ts, :, :], y_sb[:, :])
                    self.wait(ph, "vector", self.dsem("ysd"), tdy)
                tsq = ss.bump(1)
                ph.add("scalar", lambda e, m=ss.h: e.activation(
                    sq_sb[:, :], y_sb[:, :], AF.Square,
                    accum_out=stat[:, 1:2]).then_inc(m, 1))
                self.wait(ph, "vector", ss, tsq)
                tln = self.ln_tail(ph, y_sb[:, :], stat, x1t[:, :])
                ln_done[ts] = tln
                if "ysd" in self.debug_taps:
                    self.wait(ph, "scalar", sv, tln)
                    tdd = self.dma(ph, "scalar", self.dsem("std"), self.std.ap()[ts, :, :], stat[:, :])
                    self.wait(ph, "vector", self.dsem("std"), tdd)
                self.wait(ph, "sync", sv, tln)
                self.wait(ph, "gpsimd", sv, tln)
                t_f32 = self.dma(ph, "sync", dx1, self.x1f.ap()[ts * 128:(ts + 1) * 128, :], x1t[:, :])
                t_bf = self.dma(ph, "gpsimd", dx1b, self.x1bf.ap()[ts * 128:(ts + 1) * 128, :], x1t[:, :])
                self.wait(ph, "vector", dx1, t_f32)
                self.wait(ph, "vector", dx1b, t_bf)
            ph.emit()
            if self.phase_done():
                return

    def _emit_av(self, ph, pend, h, expb, vtm, inc=False):
        jt, rot, exdep, qt, njt = pend
        mm = self.sem["mm"]
        cb = 2 + qt % 2
        self.wait(ph, "tensor", exdep[0], exdep[1])
        upd = (mm.h, 1) if inc else None
        ph.add("tensor", lambda e, rot=rot, jt=jt, h=h, cb=cb, upd=upd: e.matmul(
            self.PS[:, cb * 512: cb * 512 + 129],
            expb[:, rot * 128:(rot + 1) * 128],
            vtm[:, jt * VW + h * 129: jt * VW + (h + 1) * 129],
            start=(jt == 0), stop=(jt == njt - 1)).then_maybe_inc(upd))

    def stage2(self):
        nc = self.nc
        mm, sv, ss, sg = self.sem["mm"], self.sem["sv"], self.sem["ss"], self.sem["sg"]
        w = self.w
        with contextlib.ExitStack() as sc:
            sb = lambda name, shape, dt: sc.enter_context(nc.sbuf_tensor(name, shape, dt))
            A = sb("s2A", [128, NK * S], BF16)
            gates_bc = sb("gates_bc", [128, NE], F32)
            b1T = sb("b1T", [128, NE], F32)
            b2eff = sb("b2eff", [128, 1], F32)
            w1T = sb("w1T", [128, NE * 128], BF16)
            w2T = sb("w2T", [128, NE * 128], BF16)

            ph = Phase(nc)
            sc4 = contextlib.ExitStack()
            sb4 = lambda name, shape, dt: sc4.enter_context(nc.sbuf_tensor(name, shape, dt))
            swb = sb4("swb", [128, 4 * DM], F32)
            x1tl = sb4("x1tl", [128, 2 * DM], BF16)
            junk = sb4("junk", [128, 2 * DM], BF16)
            racc = sb4("racc", [128, 2 * NE * NT], F32)
            acc8 = sb4("acc8", [128, NE], F32)
            rsc = sb4("rsc", [8, 64], F32)
            b2sb = sb4("b2sb", [8, 128], F32)
            b1sb = sb4("b1sb", [8, 128], F32)
            swbias = sb4("swbias", [1, 8], F32)
            w1s = sb4("w1s", [128, NE * 128], BF16)
            w2s = sb4("w2s", [128, NE * 128], BF16)

            self.w3bf = nc.dram_tensor("w3bf", [DM, DM], BF16, kind="Internal")
            d_w3bf = self.dsem("w3bf")
            self.dma(ph, "gpsimd", d_w3bf, self.w3bf.ap(), w["L0_ffn_w3"].ap())
            d_s2c = self.dsem("s2c")
            self.dma(ph, "sync", d_s2c, swbias[:, :], w["L0_ffn_switch_b"].ap().unsqueeze(0))
            self.dma(ph, "sync", d_s2c, b1sb[:, :], w["L0_ffn_b1"].ap())
            self.dma(ph, "sync", d_s2c, b2sb[:, :], w["L0_ffn_b2"].ap())
            d_w12 = self.dsem("w12")
            self.dma(ph, "gpsimd", d_w12, w1s[:, :].rearrange("p (e f) -> p e f", f=128),
                     w["L0_ffn_w1"].ap().rearrange("e f d -> f e d"))
            self.dma(ph, "gpsimd", d_w12, w2s[:, :].rearrange("p (e f) -> p e f", f=128),
                     w["L0_ffn_w2"].ap().rearrange("e d f -> d e f"))
            d_xfm2 = self.dsem("xfm2")
            for dc in range(NK):
                self.dmaT(ph, "sync", d_xfm2, A[:, dc * S:(dc + 1) * S],
                          self.x1bf.ap()[:, dc * 128:(dc + 1) * 128])
            d_sw = [self.dsem("sw0"), self.dsem("sw1"), self.dsem("sw2"), self.dsem("sw3")]
            d_x1r = [self.dsem("x1r0"), self.dsem("x1r1")]
            sv.bump(1)
            ph.add("vector", lambda e, m=sv.h: e.memset(racc[:, 0:NE * NT], 0.0).then_inc(m, 1))
            t_r0 = sv.v
            stt_done = {}
            gtt_done = {}
            for ts in range(NT):
                par = ts % 2
                if ts >= 2:
                    self.wait(ph, "sync", sv, stt_done[(ts - 2, NE - 1)])
                tx1 = self.dma(ph, "sync", d_x1r[par], x1tl[:, par * DM:(par + 1) * DM],
                               self.x1bf.ap()[ts * 128:(ts + 1) * 128, :])
                for e in range(NE):
                    i = ts * NE + e
                    veng = "vector"
                    esem = sv
                    spar = i % 4
                    qeng = "sync" if (i % 2 == 0) else "scalar"
                    if i >= 4:
                        pi = i - 4
                        self.wait(ph, qeng, sv, stt_done[(pi // NE, pi % NE)])
                    tsw = self.dma(ph, qeng, d_sw[spar], swb[:, spar * DM:(spar + 1) * DM],
                                   w["L0_ffn_switch_w"].ap().rearrange("e (t p f) -> e t p f", p=128, f=DM)[e, ts])
                    self.wait(ph, veng, d_sw[spar], tsw)
                    if e in (0, 5):
                        self.wait(ph, veng, d_x1r[par], tx1)
                    if i == 0:
                        self.wait(ph, "vector", sv, t_r0)
                    col = e * NT + ts
                    tst = esem.bump(1)
                    jslice = junk[:, 0:DM]
                    ph.add(veng, lambda e2, par=par, spar=spar, col=col, m=esem.h, jslice=jslice: e2.scalar_tensor_tensor(
                        jslice, x1tl[:, par * DM:(par + 1) * DM], 1.0,
                        swb[:, spar * DM:(spar + 1) * DM], ALU.mult, ALU.mult,
                        accum_out=racc[:, col:col + 1]).then_inc(m, 1))
                    stt_done[(ts, e)] = tst
            t1 = self.vchain(ph, "vector", [
                lambda e: e.tensor_reduce(acc8[:, :],
                                          racc[:, 0:NE * NT].rearrange("p (e t) -> p e t", t=NT),
                                          AX.X, ALU.add)])
            self.wait(ph, "tensor", sv, t1)
            tmv = mm.bump(1)
            ph.add("tensor", lambda e, m=mm.h: e.matmul(
                self.PS[0:1, 7 * 512:7 * 512 + 8], self.ones1_f32[:, 0:1], acc8[:, :],
                start=True, stop=True).then_inc(m, 1))
            self.wait(ph, "vector", mm, tmv)
            self.wait(ph, "vector", d_s2c, d_s2c.v)
            R = rsc
            t2 = self.vchain(ph, "vector", [
                lambda e: e.tensor_tensor(R[0:1, 0:8], self.PS[0:1, 7 * 512:7 * 512 + 8], swbias[0:1, :], ALU.add),
                lambda e: e.reduce_max(R[0:1, 8:9], R[0:1, 0:8], AX.X),
                lambda e: e.tensor_scalar(R[0:1, 9:10], R[0:1, 8:9], -1.0, None, ALU.mult),
            ])
            self.wait(ph, "scalar", sv, t2)
            t3 = ss.bump(1)
            ph.add("scalar", lambda e, m=ss.h: e.activation(
                R[0:1, 10:18], R[0:1, 0:8], AF.Exp, bias=R[0:1, 9:10]).then_inc(m, 1))
            self.wait(ph, "vector", ss, t3)
            t4 = self.vchain(ph, "vector", [
                lambda e: e.reduce_sum(R[0:1, 18:19], R[0:1, 10:18], AX.X),
                lambda e: e.reciprocal(R[0:1, 19:20], R[0:1, 18:19]),
                lambda e: e.tensor_scalar(R[0:1, 20:28], R[0:1, 10:18], R[0:1, 19:20], None, ALU.mult),
                lambda e: e.reduce_max(R[0:1, 28:29], R[0:1, 20:28], AX.X),
                lambda e: e.tensor_single_scalar(R[0:1, 29:37], R[0:1, 20:28], R[0:1, 28:29], ALU.is_ge),
                lambda e: e.scalar_tensor_tensor(R[0:1, 29:37], R[0:1, 29:37], -1e30, R[0:1, 20:28], ALU.mult, ALU.add),
                lambda e: e.reduce_max(R[0:1, 37:38], R[0:1, 29:37], AX.X),
                lambda e: e.tensor_single_scalar(R[0:1, 38:46], R[0:1, 20:28], R[0:1, 37:38], ALU.is_ge),
                lambda e: e.tensor_tensor(R[0:1, 46:54], R[0:1, 20:28], R[0:1, 38:46], ALU.mult),
            ])
            self.wait(ph, "tensor", sv, t4)
            tmv = mm.bump(1)
            ph.add("tensor", lambda e, m=mm.h: e.matmul(
                self.PS[:, 6 * 512:6 * 512 + 8], self.onesr_f32[0:1, :],
                R[0:1, 46:54], start=True, stop=True).then_inc(m, 1))
            tmv2 = mm.bump(1)
            ph.add("tensor", lambda e, m=mm.h: e.transpose(
                self.PS[0:8, 7 * 512 + 16:7 * 512 + 17], R[0:1, 46:54],
                self.ident_f32[0:1, 0:1]).then_inc(m, 1))
            self.wait(ph, "vector", mm, tmv2)
            t5 = self.vchain(ph, "vector", [
                lambda e: e.tensor_copy(gates_bc[:, :], self.PS[:, 6 * 512:6 * 512 + 8]),
                lambda e: e.tensor_copy(R[0:8, 54:55], self.PS[0:8, 7 * 512 + 16:7 * 512 + 17]),
            ])
            self.wait(ph, "tensor", sv, t5)
            self.wait(ph, "tensor", d_s2c, d_s2c.v)
            tmv3 = mm.bump(1)
            ph.add("tensor", lambda e, m=mm.h: e.matmul(
                self.PS[:, 5 * 512:5 * 512 + 1], b2sb[:, :], R[0:8, 54:55],
                start=True, stop=True).then_inc(m, 1))
            tmv4 = mm.bump(1)
            ph.add("tensor", lambda e, m=mm.h: e.transpose(
                self.PS[:, 4 * 512:4 * 512 + 8], b1sb[:, :],
                self.ident_f32[0:8, 0:8]).then_inc(m, 1))
            self.wait(ph, "vector", mm, tmv4)
            t6 = self.vchain(ph, "vector", [
                lambda e: e.tensor_copy(b2eff[:, :], self.PS[:, 5 * 512:5 * 512 + 1]),
                lambda e: e.tensor_copy(b1T[:, :], self.PS[:, 4 * 512:4 * 512 + 8]),
            ])
            self.wait(ph, "tensor", d_w12, d_w12.v)
            for e_ in range(NE):
                for wsrc, wdst in ((w1s, w1T), (w2s, w2T)):
                    tmvT = mm.bump(1)
                    ph.add("tensor", lambda e, e_=e_, wsrc=wsrc, m=mm.h: e.transpose(
                        self.PS[:, 3 * 512:3 * 512 + 64].bitcast(BF16),
                        wsrc[:, e_ * 128:(e_ + 1) * 128], self.ident_bf[:, :]).then_inc(m, 1))
                    self.wait(ph, "vector", mm, tmvT)
                    tvT = sv.bump(1)
                    ph.add("vector", lambda e, e_=e_, wdst=wdst, m=sv.h: e.tensor_copy(
                        wdst[:, e_ * 128:(e_ + 1) * 128],
                        self.PS[:, 3 * 512:3 * 512 + 64].bitcast(BF16)).then_inc(m, 1))
                    self.wait(ph, "tensor", sv, tvT)
            self.wait(ph, "vector", sv, t6)
            ph.emit()
            sc4.close()
            if self.phase_done():
                return sc.close()

            ph = Phase(nc)
            sc5 = contextlib.ExitStack()
            sb5 = lambda name, shape, dt: sc5.enter_context(nc.sbuf_tensor(name, shape, dt))
            hb = sb5("hb", [128, 4 * 512], BF16)
            hg = sb5("hg", [128, 4 * 512], BF16)
            B = sb5("s2B", [128, NK * S], BF16)
            x1ft = sb5("x1ft", [128, 2 * DM], F32)
            y_sb = sb5("y2_sb", [128, DM], F32)
            sq_sb = sb5("sq2_sb", [128, DM], F32)
            x2t = sb5("x2t", [128, DM], F32)
            stat = sb5("stat2", [128, 8], F32)
            brow3, d_br3, _ = self.bias_row(ph, sc5, "brow_b3", "L0_ffn_b3", "br3")
            self.wait(ph, "tensor", d_xfm2, d_xfm2.v)
            gel_done = {}
            gmul_done = {}
            omm_done = {}
            pre_done = {}
            for h in range(H):
                self.pump_bg4(ph, 1)
                for g in range(4):
                    gi = h * 4 + g
                    ob = 2 + gi % 2
                    if gi >= 2:
                        self.wait(ph, "tensor", sv, pre_done[gi - 2])
                    HB = (0, 1, 4, 5)
                    for e_ in range(NE):
                        i = gi * NE + e_
                        hbk = i % 4
                        if i >= 4:
                            self.wait(ph, "tensor", ss, gel_done[i - 4])
                        tmv = mm.bump(1)
                        ph.add("tensor", lambda e, hbk=hbk, e_=e_, h=h, g=g, m=mm.h: e.matmul(
                            self.bank(HB[hbk]), w1T[:, e_ * 128:(e_ + 1) * 128],
                            A[:, h * S + g * 512: h * S + (g + 1) * 512],
                            start=True, stop=True).then_inc(m, 1))
                        self.wait(ph, "scalar", mm, tmv)
                        if i >= 4:
                            self.wait(ph, "scalar", sv, gmul_done[i - 4])
                        tgl = ss.bump(1)
                        ph.add("scalar", lambda e, hbk=hbk, e_=e_, m=ss.h: e.activation(
                            hb[:, hbk * 512:(hbk + 1) * 512], self.bank(HB[hbk]), AF.Gelu,
                            bias=b1T[:, e_:e_ + 1]).then_inc(m, 1))
                        gel_done[i] = tgl
                        self.wait(ph, "vector", ss, tgl)
                        if i >= 4:
                            self.wait(ph, "vector", mm, omm_done.get(i - 4, 0))
                        tgm = sv.bump(1)
                        ph.add("vector", lambda e, hbk=hbk, e_=e_, m=sv.h: e.tensor_scalar(
                            hg[:, hbk * 512:(hbk + 1) * 512], hb[:, hbk * 512:(hbk + 1) * 512],
                            gates_bc[:, e_:e_ + 1], None, ALU.mult).then_inc(m, 1))
                        gmul_done[i] = tgm
                        self.wait(ph, "tensor", sv, tgm)
                        tom = mm.bump(1)
                        ph.add("tensor", lambda e, ob=ob, e_=e_, hbk=hbk, m=mm.h: e.matmul(
                            self.bank(ob), w2T[:, e_ * 128:(e_ + 1) * 128],
                            hg[:, hbk * 512:(hbk + 1) * 512],
                            start=(e_ == 0), stop=(e_ == NE - 1)).then_inc(m, 1))
                        omm_done[i] = tom
                    self.wait(ph, "vector", mm, omm_done[gi * NE + NE - 1])
                    tpre = sv.bump(1)
                    ph.add("vector", lambda e, ob=ob, h=h, g=g, m=sv.h: e.scalar_tensor_tensor(
                        A[:, h * S + g * 512: h * S + (g + 1) * 512], self.bank(ob),
                        b2eff[:, 0:1], A[:, h * S + g * 512: h * S + (g + 1) * 512],
                        ALU.add, ALU.add).then_inc(m, 1))
                    pre_done[gi] = tpre
            t_pre_all = sv.v
            d_w3T = self.dsem("w3T")
            self.wait(ph, "sync", d_w3bf, d_w3bf.v)
            for dc in range(NK):
                self.dmaT(ph, "sync", d_w3T, B[:, dc * S:(dc + 1) * S],
                          self.w3bf.ap()[:, dc * 128:(dc + 1) * 128])
            d_x1f = [self.dsem("x1f0"), self.dsem("x1f1")]
            dx2 = self.dsem("x2s")
            dx2b = self.dsem("x2b")
            self.wait(ph, "tensor", d_w3T, d_w3T.v)
            self.wait(ph, "tensor", sv, t_pre_all)
            self.wait(ph, "tensor", d_br3, d_br3.v)
            mm_ts = {}
            ln_done = {}
            for ts in range(NT):
                par = ts % 2
                self.pump_bg4(ph, 2)
                if ts >= 2:
                    self.wait(ph, "scalar", sv, ln_done[ts - 2])
                tx1 = self.dma(ph, "scalar", d_x1f[par], x1ft[:, par * DM:(par + 1) * DM],
                               self.x1f.ap()[ts * 128:(ts + 1) * 128, :])
                for g in range(4):
                    bk = 4 + g
                    for dc in range(NK):
                        ph.add("tensor", lambda e, bk=bk, dc=dc, g=g, ts=ts: e.matmul(
                            self.bank(bk), A[:, dc * S + ts * 128: dc * S + ts * 128 + 128],
                            B[:, dc * S + g * 512: dc * S + (g + 1) * 512],
                            start=(dc == 0), stop=False))
                    upd = (mm.h, 1) if g == 3 else None
                    ph.add("tensor", lambda e, bk=bk, g=g, upd=upd: e.matmul(
                        self.bank(bk), self.ones1_bf[0:1, 0:128],
                        brow3[0:1, g * 512:(g + 1) * 512], start=False, stop=True).then_maybe_inc(upd))
                mm_ts[ts] = mm.bump(1)
                self.wait(ph, "vector", mm, mm_ts[ts])
                self.wait(ph, "vector", d_x1f[par], tx1)
                sv.bump(1)
                ph.add("vector", lambda e, m=sv.h: e.memset(stat[:, 0:2], 0.0).then_inc(m, 1))
                self.wait(ph, "vector", sv, sv.v)
                tev = sv.bump(1)
                ph.add("vector", lambda e, par=par, m=sv.h: e.scalar_tensor_tensor(
                    y_sb[:, :], self.PS[:, 4 * 512:8 * 512], 1.0,
                    x1ft[:, par * DM:(par + 1) * DM], ALU.mult, ALU.add,
                    accum_out=stat[:, 0:1]).then_inc(m, 1))
                self.wait(ph, "tensor", sv, tev)
                self.wait(ph, "scalar", sv, tev)
                tsq = ss.bump(1)
                ph.add("scalar", lambda e, m=ss.h: e.activation(
                    sq_sb[:, :], y_sb[:, :], AF.Square, accum_out=stat[:, 1:2]).then_inc(m, 1))
                self.wait(ph, "vector", ss, tsq)
                tln = self.ln_tail(ph, y_sb[:, :], stat, x2t[:, :])
                ln_done[ts] = tln
                self.wait(ph, "sync", sv, tln)
                self.wait(ph, "gpsimd", sv, tln)
                t_f32 = self.dma(ph, "sync", dx2, self.x2f.ap()[ts * 128:(ts + 1) * 128, :], x2t[:, :])
                t_bf = self.dma(ph, "gpsimd", dx2b, self.x2bf.ap()[ts * 128:(ts + 1) * 128, :], x2t[:, :])
                self.wait(ph, "vector", dx2, t_f32)
                self.wait(ph, "vector", dx2b, t_bf)
            ph.emit()
            sc5.close()
            if self.phase_done():
                return

    def stage3(self):
        nc = self.nc
        mm, sv, ss = self.sem["mm"], self.sem["sv"], self.sem["ss"]
        w = self.w
        kind = lambda n: "ExternalOutput" if n in self.debug_taps else "Internal"
        if self.debug_taps:
            self.kbf = nc.dram_tensor("kbf", [S, DM], BF16, kind=kind("kbf")).ap()
            self.mixbf = nc.dram_tensor("mixbf", [S, DM], BF16, kind=kind("mixbf")).ap()
            self.ubf = nc.dram_tensor("ubf", [S, DM], BF16, kind=kind("ubf")).ap()
        else:
            self.kbf = self.ctxd.ap().rearrange("h s c -> (h s c)").rearrange("(a b) -> a b", b=DM)
            self.mixbf = self.ctxdF.ap().rearrange("h c s -> (h c s)").rearrange("(a b) -> a b", b=DM)
            self.ubf = self.w3bf.ap()
        with contextlib.ExitStack() as sc:
            sb = lambda name, shape, dt: sc.enter_context(nc.sbuf_tensor(name, shape, dt))
            A = sb("s3A", [128, NK * S], BF16)
            B = sb("s3B", [128, NK * S], BF16)
            kv_sb = sb("kv_sb", [128, H * 128], BF16)
            sc6 = contextlib.ExitStack()
            sb6 = lambda name, shape, dt: sc6.enter_context(nc.sbuf_tensor(name, shape, dt))
            sc67 = contextlib.ExitStack()
            sb67 = lambda name, shape, dt: sc67.enter_context(nc.sbuf_tensor(name, shape, dt))
            wsl = sb67("wsl3", [128, 2 * NK * 512], BF16)

            def proj_pass(ph, wname, brow, out_emit, d_par, bias_row_ready):
                done = {}
                mm_g = {}
                mm_entry = mm.v
                for g in range(4):
                    par = g % 2
                    buf = wsl[:, par * NK * 512:(par + 1) * NK * 512]
                    self.pump_bg4(ph, 3)
                    if g >= 2:
                        self.wait(ph, "gpsimd", mm, mm_g[g - 2])
                    elif mm_entry > 0:
                        self.wait(ph, "gpsimd", mm, mm_entry)
                    twv = self.dma(ph, "gpsimd", d_par[par],
                                   buf.rearrange("p (kc n) -> p kc n", n=512),
                                   w[wname].ap().rearrange("(kc p) n -> p kc n", p=128)[:, :, g * 512:(g + 1) * 512])
                    self.wait(ph, "tensor", d_par[par], twv)
                    if g == 0:
                        self.wait(ph, "tensor", bias_row_ready[0], bias_row_ready[1])
                    for ts in range(NT):
                        i = g * NT + ts
                        b = i % 2
                        if i >= 2:
                            sem, val = done[i - 2]
                            self.wait(ph, "tensor", sem, val)
                        for kc in range(NK):
                            ph.add("tensor", lambda e, b=b, kc=kc, ts=ts, buf=buf: e.matmul(
                                self.bank(b), A[:, kc * S + ts * 128: kc * S + ts * 128 + 128],
                                buf[:, kc * 512:(kc + 1) * 512], start=(kc == 0), stop=False))
                        tmv = mm.bump(1)
                        ph.add("tensor", lambda e, b=b, g=g, brow=brow, m=mm.h: e.matmul(
                            self.bank(b), self.ones1_bf[0:1, 0:128],
                            brow[0:1, g * 512:(g + 1) * 512], start=False, stop=True).then_inc(m, 1))
                        done[i] = out_emit(ph, g, ts, b, tmv, i)
                    mm_g[g] = mm.v
                return done

            ph = Phase(nc)
            elu1 = sb67("elu1", [128, 512], F32)
            elu2 = sb67("elu2", [128, 512], F32)
            elu3 = sb67("elu3", [128, 512], F32)
            browk, d_brk, _ = self.bias_row(ph, sc67, "brow_qk", "L1_attn_bqk", "brk")
            d_xfm3 = self.dsem("xfm3")
            for dc in range(NK):
                self.dmaT(ph, "sync", d_xfm3, A[:, dc * S:(dc + 1) * S],
                          self.x2bf.ap()[:, dc * 128:(dc + 1) * 128])
            d_wqk = [self.dsem("wqk0"), self.dsem("wqk1")]
            d_kbf = self.dsem("kbfs")
            self.wait(ph, "tensor", d_xfm3, d_xfm3.v)

            kspill = {}
            def elu_emit(ph, g, ts, b, tmv, i):
                self.wait(ph, "vector", mm, tmv)
                if i >= 1:
                    self.wait(ph, "vector", ss, ss.v)
                tv1 = self.vchain(ph, "vector", [
                    lambda e, b=b: e.tensor_scalar(elu1[:, :], self.bank(b), 0.0, None, ALU.min),
                    lambda e, b=b: e.tensor_relu(elu3[:, :], self.bank(b)),
                ])
                self.wait(ph, "scalar", sv, tv1)
                tsx = ss.bump(1)
                ph.add("scalar", lambda e, m=ss.h: e.activation(
                    elu2[:, :], elu1[:, :], AF.Exp).then_inc(m, 1))
                self.wait(ph, "vector", ss, tsx)
                tst = sv.bump(1)
                ph.add("vector", lambda e, ts=ts, g=g, m=sv.h: e.scalar_tensor_tensor(
                    B[:, ts * DM + g * 512: ts * DM + (g + 1) * 512],
                    elu2[:, :], -1.0,
                    elu3[:, :], ALU.add, ALU.add).then_inc(m, 1))
                kspill.setdefault(ts, []).append(tst)
                return (sv, tst)

            proj_pass(ph, "L1_attn_Wqk", browk, elu_emit, d_wqk, (d_brk, d_brk.v))
            for ts in range(NT):
                self.wait(ph, "sync", sv, max(kspill[ts]))
                self.dma(ph, "sync", d_kbf, self.kbf[ts * 128:(ts + 1) * 128, :],
                         B[:, ts * DM:(ts + 1) * DM])
            ph.emit()
            if self.phase_done():
                sc6.close()
                return

            ph = Phase(nc)
            vbuf = sb67("vbuf", [128, 2 * 512], BF16)
            browv, d_brv, _ = self.bias_row(ph, sc67, "brow_v3", "L1_attn_bv", "brv")
            browu, d_bru, _ = self.bias_row(ph, sc67, "brow_u3", "L1_attn_bu", "bru")
            d_wv3 = [self.dsem("wv30"), self.dsem("wv31")]
            d_wu3 = [self.dsem("wu30"), self.dsem("wu31")]
            kv_last = {}

            def v_emit(ph, g, ts, b, tmv, i):
                vp = i % 2
                self.wait(ph, "scalar", mm, tmv)
                tsl = ss.bump(1)
                ph.add("scalar", lambda e, b=b, vp=vp, m=ss.h: e.activation(
                    vbuf[:, vp * 512:(vp + 1) * 512], self.bank(b), AF.Silu).then_inc(m, 1))
                self.wait(ph, "tensor", ss, tsl)
                kvbk = 2 + g
                for hh in range(4):
                    h = g * 4 + hh
                    upd = (mm.h, 1) if hh == 3 else None
                    ph.add("tensor", lambda e, h=h, hh=hh, ts=ts, vp=vp, kvbk=kvbk, upd=upd: e.matmul(
                        self.PS[:, kvbk * 512 + hh * 128: kvbk * 512 + (hh + 1) * 128],
                        B[:, ts * DM + h * 128: ts * DM + (h + 1) * 128],
                        vbuf[:, vp * 512 + hh * 128: vp * 512 + (hh + 1) * 128],
                        start=(ts == 0 and hh == 0),
                        stop=(ts == NT - 1 and hh == 3)).then_maybe_inc(upd))
                tkv = mm.bump(1)
                kv_last[(g, ts)] = tkv
                return (ss, tsl)

            proj_pass(ph, "L1_attn_Wv", browv, v_emit, d_wv3, (d_brv, d_brv.v))
            for g in range(4):
                self.wait(ph, "vector", mm, kv_last[(g, NT - 1)])
                tve = sv.bump(1)
                ph.add("vector", lambda e, g=g, m=sv.h: e.tensor_copy(
                    kv_sb[:, g * 512:(g + 1) * 512],
                    self.PS[:, (2 + g) * 512:(3 + g) * 512]).then_inc(m, 1))
            t_kv_done = sv.v

            d_ub = self.dsem("ubs")
            ubuf = sb67("ubuf", [128, 2 * 512], BF16)
            uspill = {}

            def u_emit(ph, g, ts, b, tmv, i):
                up = i % 2
                self.wait(ph, "scalar", mm, tmv)
                if i >= 2:
                    self.wait(ph, "scalar", d_ub, uspill[i - 2])
                tsl = ss.bump(1)
                ph.add("scalar", lambda e, b=b, up=up, m=ss.h: e.activation(
                    ubuf[:, up * 512:(up + 1) * 512], self.bank(b), AF.Silu).then_inc(m, 1))
                self.wait(ph, "sync", ss, tsl)
                tu = self.dma(ph, "sync", d_ub,
                              self.ubf[ts * 128:(ts + 1) * 128, g * 512:(g + 1) * 512],
                              ubuf[:, up * 512:(up + 1) * 512])
                uspill[i] = tu
                return (ss, tsl)

            proj_pass(ph, "L1_attn_Wu", browu, u_emit, d_wu3, (d_bru, d_bru.v))
            t_u_done = d_ub.v
            ph.emit()
            sc67.close()
            if self.phase_done():
                sc6.close()
                return

            ph = Phase(nc)
            y3_sb = sb6("y3_sb", [128, DM], F32)
            sq3 = sb6("sq3", [128, DM], F32)
            stat3 = sb6("stat3", [128, 8], F32)
            sc8 = contextlib.ExitStack()
            sb8 = lambda name, shape, dt: sc8.enter_context(nc.sbuf_tensor(name, shape, dt))
            yn_sb = sb8("yn_sb", [128, DM], F32)
            mixt = sb8("mixt", [128, DM], BF16)
            ut = sb8("ut", [128, 2 * DM], BF16)
            d_ut = [self.dsem("ut0"), self.dsem("ut1")]
            lrow = sb8("lrow", [1, 2 * DM], F32)
            Af32 = A[:, 0:8192].bitcast(F32)
            gbc = Af32[:, 0:DM]
            bbc = Af32[:, DM:2 * DM]
            d_qfm = self.dsem("qfm")
            d_lnr = self.dsem("lnr")
            self.dma(ph, "sync", d_lnr, lrow[:, 0:DM], w["L1_attn_ln_g"].ap().unsqueeze(0))
            self.dma(ph, "sync", d_lnr, lrow[:, DM:2 * DM], w["L1_attn_ln_b"].ap().unsqueeze(0))
            self.wait(ph, "sync", sv, t_kv_done)
            for dc in range(NK):
                self.dmaT(ph, "sync", d_qfm, B[:, dc * S:(dc + 1) * S],
                          self.kbf[:, dc * 128:(dc + 1) * 128])
            self.wait(ph, "tensor", d_lnr, d_lnr.v)
            for which in range(2):
                for gg in range(4):
                    bk = which * 4 + gg
                    upd = (mm.h, 1) if (which == 1 and gg == 3) else None
                    if upd:
                        mm.bump(1)
                    ph.add("tensor", lambda e, bk=bk, which=which, gg=gg, upd=upd: e.matmul(
                        self.bank(bk), self.onesr_f32[0:1, :],
                        lrow[0:1, which * DM + gg * 512: which * DM + (gg + 1) * 512],
                        start=True, stop=True).then_maybe_inc(upd))
            self.wait(ph, "vector", mm, mm.v)
            tgb = self.vchain(ph, "vector", [
                lambda e: e.tensor_copy(gbc, self.PS[:, 0:2048]),
                lambda e: e.tensor_copy(bbc, self.PS[:, 2048:4096]),
            ])
            brow_o3, d_bro3, _ = self.bias_row(ph, sc8, "brow_o3", "L1_attn_bo", "bro3")
            d_mix = self.dsem("mixs")
            self.wait(ph, "tensor", d_qfm, d_qfm.v)
            self.wait(ph, "tensor", sv, tgb)
            self.wait(ph, "tensor", sv, t_kv_done)
            mix_done = {}
            for ts in range(NT):
                self.pump_bg4(ph, 2)
                if ts >= 1:
                    self.wait(ph, "tensor", sv, mix_done[ts - 1])
                for h in range(H):
                    bk = 4 + h // 4
                    upd = (mm.h, 1) if h == H - 1 else None
                    if h == H - 1:
                        mm.bump(1)
                    ph.add("tensor", lambda e, h=h, bk=bk, ts=ts, upd=upd: e.matmul(
                        self.PS[:, bk * 512 + (h % 4) * 128: bk * 512 + (h % 4 + 1) * 128],
                        B[:, h * S + ts * 128: h * S + ts * 128 + 128],
                        kv_sb[:, h * 128:(h + 1) * 128],
                        start=True, stop=True).then_maybe_inc(upd))
                t_out = mm.v
                self.wait(ph, "vector", mm, t_out)
                sv.bump(1)
                ph.add("vector", lambda e, m=sv.h: e.memset(stat3[:, 0:2], 0.0).then_inc(m, 1))
                self.wait(ph, "vector", sv, sv.v)
                tev = sv.bump(1)
                ph.add("vector", lambda e, m=sv.h: e.tensor_scalar(
                    y3_sb[:, :], self.PS[:, 4 * 512:8 * 512], 1.0, 0.0, ALU.mult,
                    ALU.add, accum_out=stat3[:, 0:1]).then_inc(m, 1))
                self.wait(ph, "scalar", sv, tev)
                tsq = ss.bump(1)
                ph.add("scalar", lambda e, m=ss.h: e.activation(
                    sq3[:, :], y3_sb[:, :], AF.Square, accum_out=stat3[:, 1:2]).then_inc(m, 1))
                self.wait(ph, "vector", ss, tsq)
                if ts == 0:
                    self.wait(ph, "vector", sv, tgb)
                tln = self.ln_tail(ph, y3_sb[:, :], stat3, yn_sb[:, :])
                upar = ts % 2
                if ts == 0:
                    self.wait(ph, "gpsimd", d_ub, t_u_done)
                if ts >= 2:
                    self.wait(ph, "gpsimd", sv, mix_done[ts - 2])
                tuld = self.dma(ph, "gpsimd", d_ut[upar], ut[:, upar * DM:(upar + 1) * DM],
                                self.ubf[ts * 128:(ts + 1) * 128, :])
                self.wait(ph, "vector", sv, tln)
                self.wait(ph, "vector", d_ut[upar], tuld)
                tmx = self.vchain(ph, "vector", [
                    lambda e: e.tensor_tensor(yn_sb[:, :], yn_sb[:, :], gbc, ALU.mult),
                    lambda e: e.tensor_tensor(yn_sb[:, :], yn_sb[:, :], bbc, ALU.add),
                    lambda e, upar=upar: e.tensor_tensor(mixt[:, :], yn_sb[:, :],
                                                         ut[:, upar * DM:(upar + 1) * DM], ALU.mult),
                ])
                mix_done[ts] = tmx
                self.wait(ph, "sync", sv, tmx)
                tms = self.dma(ph, "sync", d_mix, self.mixbf[ts * 128:(ts + 1) * 128, :], mixt[:, :])
                self.wait(ph, "vector", d_mix, tms)
            ph.emit()
            sc8.close()
            if self.phase_done():
                sc6.close()
                return

            ph = Phase(nc)
            x2ft = sb6("x2ft", [128, 2 * DM], F32)
            x3t = sb6("x3t", [128, DM], F32)
            brow_o9, d_bro9, _ = self.bias_row(ph, sc6, "brow_o9", "L1_attn_bo", "bro9")
            d_mfm = self.dsem("mfm")
            for dc in range(NK):
                self.dmaT(ph, "sync", d_mfm, B[:, dc * S:(dc + 1) * S],
                          self.mixbf[:, dc * 128:(dc + 1) * 128])
            d_wo3 = [self.dsem("wo30"), self.dsem("wo31")]
            d_x2f = [self.dsem("x2f0"), self.dsem("x2f1")]
            dx3 = self.dsem("x3s")
            dx3b = self.dsem("x3b")
            self.wait(ph, "tensor", d_mfm, d_mfm.v)
            self.wait(ph, "tensor", d_bro9, d_bro9.v)
            ln_done9 = {}
            mm_ts9 = {}
            wof = A
            d_wof = self.dsem("wof")
            for g in range(4):
                self.dma(ph, "gpsimd", d_wof,
                         wof[:, g * NK * 512:(g + 1) * NK * 512].rearrange("p (kc n) -> p kc n", n=512),
                         w["L1_attn_Wo"].ap().rearrange("(kc p) n -> p kc n", p=128)[:, :, g * 512:(g + 1) * 512])
            self.wait(ph, "tensor", d_wof, d_wof.v)
            for ts in range(NT):
                par = ts % 2
                self.pump_bg4(ph, 3)
                if ts >= 2:
                    self.wait(ph, "scalar", sv, ln_done9[ts - 2])
                tx2 = self.dma(ph, "scalar", d_x2f[par], x2ft[:, par * DM:(par + 1) * DM],
                               self.x2f.ap()[ts * 128:(ts + 1) * 128, :])
                for g in range(4):
                    bk = 4 + g
                    for kc in range(NK):
                        ph.add("tensor", lambda e, bk=bk, kc=kc, g=g, ts=ts: e.matmul(
                            self.bank(bk), B[:, kc * S + ts * 128: kc * S + ts * 128 + 128],
                            wof[:, g * NK * 512 + kc * 512: g * NK * 512 + (kc + 1) * 512],
                            start=(kc == 0), stop=False))
                    upd = (mm.h, 1) if g == 3 else None
                    ph.add("tensor", lambda e, bk=bk, g=g, upd=upd: e.matmul(
                        self.bank(bk), self.ones1_bf[0:1, 0:128],
                        brow_o9[0:1, g * 512:(g + 1) * 512], start=False, stop=True).then_maybe_inc(upd))
                mm_ts9[ts] = mm.bump(1)
                self.wait(ph, "vector", mm, mm_ts9[ts])
                self.wait(ph, "vector", d_x2f[par], tx2)
                sv.bump(1)
                ph.add("vector", lambda e, m=sv.h: e.memset(stat3[:, 0:2], 0.0).then_inc(m, 1))
                self.wait(ph, "vector", sv, sv.v)
                tev = sv.bump(1)
                ph.add("vector", lambda e, par=par, m=sv.h: e.scalar_tensor_tensor(
                    y3_sb[:, :], self.PS[:, 4 * 512:8 * 512], 1.0,
                    x2ft[:, par * DM:(par + 1) * DM], ALU.mult, ALU.add,
                    accum_out=stat3[:, 0:1]).then_inc(m, 1))
                self.wait(ph, "tensor", sv, tev)
                self.wait(ph, "scalar", sv, tev)
                tsq = ss.bump(1)
                ph.add("scalar", lambda e, m=ss.h: e.activation(
                    sq3[:, :], y3_sb[:, :], AF.Square, accum_out=stat3[:, 1:2]).then_inc(m, 1))
                self.wait(ph, "vector", ss, tsq)
                tln = self.ln_tail(ph, y3_sb[:, :], stat3, x3t[:, :])
                ln_done9[ts] = tln
                self.wait(ph, "sync", sv, tln)
                self.wait(ph, "gpsimd", sv, tln)
                t_f32 = self.dma(ph, "sync", dx3, self.x3f.ap()[ts * 128:(ts + 1) * 128, :], x3t[:, :])
                t_bf = self.dma(ph, "gpsimd", dx3b, self.x3bf.ap()[ts * 128:(ts + 1) * 128, :], x3t[:, :])
                self.wait(ph, "vector", dx3, t_f32)
                self.wait(ph, "vector", dx3b, t_bf)
            ph.emit()
            sc6.close()
            if self.phase_done():
                return

    def stage4(self):
        nc = self.nc
        mm, sv, ss = self.sem["mm"], self.sem["sv"], self.sem["ss"]
        w = self.w
        NP = 4
        TP = S // NP
        C1 = np.float32(1.0 / (np.sqrt(2048.0) * 64.0))
        C2 = np.float32(1.0 / (np.sqrt(8192.0) * 64.0))
        dr = nc.dram_tensor
        kindd = lambda n: "ExternalOutput" if n in self.debug_taps else "Internal"
        if self.debug_taps:
            self.xtd = dr("xtd", [NK, 128, S], BF16, kind=kindd("xtd")).ap()
        else:
            self.xtd = self.ctxd.ap().rearrange("h s c -> (h s c)").rearrange("(k p s) -> k p s", p=128, s=S)

        with contextlib.ExitStack() as sc:
            sb = lambda name, shape, dt: sc.enter_context(nc.sbuf_tensor(name, shape, dt))
            HSU2 = sb("HSU2", [128, NFC * 128], BF16)
            HSU1 = sb("HSU1", [128, NK * 128], BF16)
            wsu_col = sb("wsu_col", [128, 128], F32)
            wsdpb = sb("wsdpb", [128, DM], F32)

            ph = Phase(nc)
            sc10 = contextlib.ExitStack()
            sb10 = lambda name, shape, dt: sc10.enter_context(nc.sbuf_tensor(name, shape, dt))
            self.drain_bg4(ph)
            d_wbf, d_wTl, d_wTsl = self._bg4_sems
            t_wT_ready = d_wTsl.v
            sus = sb10("sus", [64, 128], F32)
            sug = sb10("sug", [16, 128], F32)
            wss = sb10("wss", [128, 128], F32)
            wsd = sb10("wsd", [1, DM], F32)
            sucol1 = sb10("sucol1", [128, 16], F32)
            sucol2 = sb10("sucol2", [128, 64], F32)
            d_s4c = self.dsem("s4c")
            self.dma(ph, "sync", d_s4c, sug[:, :], w["L1_ffn_SU_ug"].ap().rearrange("(a b) -> a b", b=128))
            self.dma(ph, "sync", d_s4c, sus[:, :], w["L1_ffn_SU_dp"].ap().rearrange("(a b) -> a b", b=128))
            self.dma(ph, "sync", d_s4c, wss[:, :], w["L1_ffn_Wscale_ug"].ap().rearrange("(a b) -> a b", b=128))
            self.dma(ph, "sync", d_s4c, wsd[:, :], w["L1_ffn_Wscale_dp"].ap().unsqueeze(0))
            self.wait(ph, "tensor", d_s4c, d_s4c.v)
            tv_list = []
            for (stg, width, bank) in ((sug, 16, 0), (sus, 64, 1), (wss, 128, 2)):
                tmv = mm.bump(1)
                ph.add("tensor", lambda e, stg=stg, width=width, bank=bank, m=mm.h: e.transpose(
                    self.PS[0:128, bank * 512: bank * 512 + width],
                    stg[:, :], self.ident_f32[0:stg.shape[0], 0:stg.shape[0]]).then_inc(m, 1))
            self.wait(ph, "vector", mm, mm.v)
            t7 = self.vchain(ph, "vector", [
                lambda e: e.tensor_copy(sucol1[:, :], self.PS[:, 0:16]),
                lambda e: e.tensor_copy(sucol2[:, :], self.PS[:, 512:512 + 64]),
                lambda e: e.tensor_scalar(wsu_col[:, :], self.PS[:, 1024:1024 + 128], 64.0, None, ALU.mult),
                lambda e: e.tensor_scalar(wsd[0:1, :], wsd[0:1, :], 64.0, None, ALU.mult),
            ])
            self.wait(ph, "tensor", sv, t7)
            for gg in range(4):
                upd = (mm.h, 1) if gg == 3 else None
                if upd:
                    mm.bump(1)
                ph.add("tensor", lambda e, gg=gg, upd=upd: e.matmul(
                    self.bank(4 + gg), self.onesr_f32[0:1, :],
                    wsd[0:1, gg * 512:(gg + 1) * 512], start=True, stop=True).then_maybe_inc(upd))
            self.wait(ph, "vector", mm, mm.v)
            t8 = self.vchain(ph, "vector", [
                lambda e: e.tensor_copy(wsdpb[:, :], self.PS[:, 4 * 512:8 * 512]),
            ])
            hs_ops = []
            for dc in range(NK):
                hs_ops.append(lambda e, dc=dc: e.tensor_scalar(
                    HSU1[:, dc * 128:(dc + 1) * 128], self.h128[:, :],
                    sucol1[:, dc:dc + 1], None, ALU.mult))
            for fc in range(NFC):
                hs_ops.append(lambda e, fc=fc: e.tensor_scalar(
                    HSU2[:, fc * 128:(fc + 1) * 128], self.h128[:, :],
                    sucol2[:, fc:fc + 1], None, ALU.mult))
            self.wait(ph, "vector", sv, t8)
            for i, op in enumerate(hs_ops):
                if i == len(hs_ops) - 1:
                    tHSU = sv.bump(1)
                    ph.add("vector", lambda e, op=op, m=sv.h: op(e).then_inc(m, 1))
                else:
                    ph.add("vector", lambda e, op=op: op(e))
            ph.emit()
            sc10.close()
            if self.phase_done():
                return

            ph = Phase(nc)
            sc11 = contextlib.ExitStack()
            sb11 = lambda name, shape, dt: sc11.enter_context(nc.sbuf_tensor(name, shape, dt))
            B1 = sb11("s4B1", [128, NK * S], BF16)
            A1 = sb11("s4A1", [128, NK * S], BF16)
            d_xfm4 = self.dsem("xfm4")
            for dc in range(NK):
                self.dmaT(ph, "sync", d_xfm4, B1[:, dc * S:(dc + 1) * S],
                          self.x3bf.ap()[:, dc * 128:(dc + 1) * 128])
            self.wait(ph, "tensor", d_xfm4, d_xfm4.v)
            self.wait(ph, "tensor", sv, tHSU)
            ev_at = {}
            for dc in range(NK):
                for g in range(4):
                    i = dc * 4 + g
                    b = i % 2
                    if i >= 2:
                        self.wait(ph, "tensor", ss, ev_at[i - 2])
                    tmv = mm.bump(1)
                    ph.add("tensor", lambda e, b=b, dc=dc, g=g, m=mm.h: e.matmul(
                        self.bank(b), HSU1[:, dc * 128:(dc + 1) * 128],
                        B1[:, dc * S + g * 512: dc * S + (g + 1) * 512],
                        start=True, stop=True).then_inc(m, 1))
                    self.wait(ph, "scalar", mm, tmv)
                    tevs = ss.bump(1)
                    ph.add("scalar", lambda e, b=b, dc=dc, g=g, m=ss.h: e.activation(
                        A1[:, dc * S + g * 512: dc * S + (g + 1) * 512],
                        self.bank(b), AF.Identity, scale=C1).then_inc(m, 1))
                    ev_at[i] = tevs
            self.wait(ph, "vector", ss, ss.v)
            bufs = [A1, B1]
            tbf = None
            for s in range(4):
                blk = 1 << s
                src_t = bufs[s % 2]
                dst_t = bufs[(s + 1) % 2]
                msz = blk * S
                na = NK // (2 * blk)
                lowS = src_t[:, :].rearrange("p (a b m) -> p a b m", b=2, m=msz)[:, :, 0, :]
                highS = src_t[:, :].rearrange("p (a b m) -> p a b m", b=2, m=msz)[:, :, 1, :]
                lowD = dst_t[:, :].rearrange("p (a b m) -> p a b m", b=2, m=msz)[:, :, 0, :]
                highD = dst_t[:, :].rearrange("p (a b m) -> p a b m", b=2, m=msz)[:, :, 1, :]
                tbf = self.vchain(ph, "vector", [
                    lambda e, lowD=lowD, lowS=lowS, highS=highS: e.tensor_tensor(lowD, lowS, highS, ALU.add),
                    lambda e, highD=highD, lowS=lowS, highS=highS: e.tensor_tensor(highD, lowS, highS, ALU.subtract),
                ])
            self.wait(ph, "sync", sv, tbf)
            d_xt = self.dsem("xt")
            for dc in range(NK):
                self.dma(ph, "sync", d_xt, self.xtd[dc], A1[:, dc * S:(dc + 1) * S])
            self.wait(ph, "sync", d_xt, d_xt.v)
            ph.emit()
            sc11.close()
            if self.phase_done():
                return

            FG = 2
            for p in range(NP):
                ph = Phase(nc)
                scp = contextlib.ExitStack()
                sbp = lambda name, shape, dt: scp.enter_context(nc.sbuf_tensor(name, shape, dt))
                z1 = sbp("z1_%d" % p, [128, NFC * TP], BF16)
                xtp = sbp("xtp%d" % p, [128, NK * TP], BF16)
                wug = sbp("wug%d" % p, [128, 2 * 2 * NK * FG * 128], BF16)
                gbuf = sbp("gbuf%d" % p, [128, 2 * TP], BF16)
                ubuf2 = sbp("ubuf2%d" % p, [128, 2 * TP], BF16)
                hbuf = sbp("hbuf%d" % p, [128, 2 * TP], BF16)
                d_xtp = self.dsem("xtp")
                self.dma(ph, "sync", d_xtp, xtp[:, :].rearrange("p (k t) -> p k t", t=TP),
                         self.xtd[:, :, p * TP:(p + 1) * TP].rearrange("k q t -> q k t"))
                d_ug = [self.dsem("ug_0"), self.dsem("ug_1")]
                if p == 0:
                    self.wait(ph, "sync", d_wTsl, t_wT_ready)
                self.wait(ph, "tensor", d_xtp, d_xtp.v)
                WS = 2 * NK * FG * 128
                UGW = NK * FG * 128
                nfg = NFC // FG
                hz_done = {}
                ug_mm = {}
                for fg in range(nfg):
                    par = fg % 2
                    if fg >= 2:
                        self.wait(ph, "sync", mm, ug_mm[fg - 2])
                    tld = self.dma(ph, "sync", d_ug[par],
                                   wug[:, par * WS: par * WS + UGW].rearrange("p (k f) -> p k f", f=FG * 128),
                                   self.wupT.rearrange("(k q) f -> q k f", q=128)[:, :, fg * FG * 128:(fg + 1) * FG * 128])
                    tld2 = self.dma(ph, "sync", d_ug[par],
                                    wug[:, par * WS + UGW: par * WS + 2 * UGW].rearrange("p (k f) -> p k f", f=FG * 128),
                                    self.wgateT.rearrange("(k q) f -> q k f", q=128)[:, :, fg * FG * 128:(fg + 1) * FG * 128])
                    self.wait(ph, "tensor", d_ug[par], tld2)
                    for sub in range(FG):
                        fc = fg * FG + sub
                        ib = fc % 2
                        if fc >= 2:
                            self.wait(ph, "tensor", sv, hz_done[fc - 2][0])
                            self.wait(ph, "tensor", ss, hz_done[fc - 2][1])
                        for which in range(2):
                            bnk = which * 2 + ib
                            for kc in range(NK):
                                upd = (mm.h, 1) if kc == NK - 1 else None
                                if upd:
                                    mm.bump(1)
                                ph.add("tensor", lambda e, bnk=bnk, kc=kc, which=which, par=par, sub=sub, upd=upd: e.matmul(
                                    self.bank(bnk),
                                    wug[:, par * WS + which * UGW + kc * FG * 128 + sub * 128:
                                        par * WS + which * UGW + kc * FG * 128 + (sub + 1) * 128],
                                    xtp[:, kc * TP:(kc + 1) * TP],
                                    start=(kc == 0), stop=(kc == NK - 1)).then_maybe_inc(upd))
                        t_upgate = mm.v
                        self.wait(ph, "scalar", mm, t_upgate)
                        hp = fc % 2
                        tung = self.vchain(ph, "scalar", [
                            lambda e, ib=ib, hp=hp, fc=fc: e.activation(
                                ubuf2[:, hp * TP:(hp + 1) * TP], self.bank(ib),
                                AF.Identity, scale=wsu_col[:, fc:fc + 1]),
                            lambda e, ib=ib, hp=hp, fc=fc: e.activation(
                                gbuf[:, hp * TP:(hp + 1) * TP], self.bank(2 + ib),
                                AF.Silu, scale=wsu_col[:, 64 + fc:64 + fc + 1]),
                        ])
                        self.wait(ph, "vector", ss, tung)
                        th = sv.bump(1)
                        ph.add("vector", lambda e, hp=hp, m=sv.h: e.tensor_tensor(
                            hbuf[:, hp * TP:(hp + 1) * TP], gbuf[:, hp * TP:(hp + 1) * TP],
                            ubuf2[:, hp * TP:(hp + 1) * TP], ALU.mult).then_inc(m, 1))
                        self.wait(ph, "tensor", sv, th)
                        tmz = mm.bump(1)
                        ph.add("tensor", lambda e, fc=fc, hp=hp, ib=ib, m=mm.h: e.matmul(
                            self.bank(4 + ib), HSU2[:, fc * 128:(fc + 1) * 128],
                            hbuf[:, hp * TP:(hp + 1) * TP],
                            start=True, stop=True).then_inc(m, 1))
                        self.wait(ph, "scalar", mm, tmz)
                        tz = ss.bump(1)
                        ph.add("scalar", lambda e, fc=fc, ib=ib, m=ss.h: e.activation(
                            z1[:, fc * TP:(fc + 1) * TP], self.bank(4 + ib),
                            AF.Identity, scale=C2).then_inc(m, 1))
                        hz_done[fc] = (th, tz)
                    ug_mm[fg] = mm.v
                self.wait(ph, "vector", ss, ss.v)
                tbf2 = None
                for s in range(6):
                    blk = 1 << s
                    msz = blk * TP
                    low = z1[:, :].rearrange("p (a b m) -> p a b m", b=2, m=msz)[:, :, 0, :]
                    high = z1[:, :].rearrange("p (a b m) -> p a b m", b=2, m=msz)[:, :, 1, :]
                    tbf2 = self.vchain(ph, "vector", [
                        lambda e, low=low, high=high: e.tensor_tensor(high, low, high, ALU.subtract),
                        lambda e, low=low, high=high: e.scalar_tensor_tensor(low, low, 2.0, high, ALU.mult, ALU.subtract),
                    ])
                scp2 = contextlib.ExitStack()
                sbp2 = lambda name, shape, dt: scp2.enter_context(nc.sbuf_tensor(name, shape, dt))
                yacc = sbp2("yacc_%d" % p, [128, 4 * DM], F32)
                x3ft = sbp2("x3ft_%d" % p, [128, DM], F32)
                y4 = sbp2("y4_%d" % p, [128, DM], F32)
                sq4 = xtp[:, 0:2 * DM].bitcast(F32)
                stat4 = sbp2("stat4_%d" % p, [128, 8], F32)
                d_wd = self.dsem("wd")
                d_x3f = [self.dsem("x3f_0"), self.dsem("x3f_1")]
                d_out = self.dsem("outs")
                self.wait(ph, "tensor", sv, tbf2)
                ev_dn = {}
                NW = 256
                for n5 in range(DM // NW):
                    if n5 == 0:
                        self.wait(ph, "sync", mm, ug_mm[nfg - 1])
                    else:
                        self.wait(ph, "sync", mm, dn_mm)
                    tldw = self.dma(ph, "sync", d_wd,
                                    wug[:, 0:NFC * NW].rearrange("p (k f) -> p k f", f=NW),
                                    self.wdownT.rearrange("(k q) f -> q k f", q=128)[:, :, n5 * NW:(n5 + 1) * NW])
                    self.wait(ph, "tensor", d_wd, tldw)
                    for st in range(4):
                        bnk = 4 + st
                        if n5 >= 1:
                            self.wait(ph, "tensor", sv, ev_dn[(n5 - 1, st)])
                        for fc in range(NFC):
                            upd = (mm.h, 1) if fc == NFC - 1 else None
                            if upd:
                                mm.bump(1)
                            ph.add("tensor", lambda e, bnk=bnk, fc=fc, st=st, upd=upd: e.matmul(
                                self.bank(bnk, NW), z1[:, fc * TP + st * 128: fc * TP + (st + 1) * 128],
                                wug[:, fc * NW:(fc + 1) * NW],
                                start=(fc == 0), stop=(fc == NFC - 1)).then_maybe_inc(upd))
                        dn_mm = mm.v
                        self.wait(ph, "vector", mm, dn_mm)
                        tevd = sv.bump(1)
                        ph.add("vector", lambda e, st=st, n5=n5, m=sv.h: e.tensor_tensor(
                            yacc[:, st * DM + n5 * NW: st * DM + (n5 + 1) * NW],
                            self.PS[:, (4 + st) * 512:(4 + st) * 512 + NW],
                            wsdpb[:, n5 * NW:(n5 + 1) * NW], ALU.mult).then_inc(m, 1))
                        ev_dn[(n5, st)] = tevd
                ln_done4 = {}
                for st in range(4):
                    ts = p * 4 + st
                    if st >= 1:
                        self.wait(ph, "scalar", sv, ln_done4[st - 1])
                    tx3 = self.dma(ph, "scalar", d_x3f[0], x3ft[:, :],
                                   self.x3f.ap()[ts * 128:(ts + 1) * 128, :])
                    self.wait(ph, "vector", d_x3f[0], tx3)
                    self.wait(ph, "vector", sv, ev_dn[(DM // NW - 1, st)])
                    sv.bump(1)
                    ph.add("vector", lambda e, m=sv.h: e.memset(stat4[:, 0:2], 0.0).then_inc(m, 1))
                    self.wait(ph, "vector", sv, sv.v)
                    tev = sv.bump(1)
                    ph.add("vector", lambda e, st=st, m=sv.h: e.scalar_tensor_tensor(
                        y4[:, :], yacc[:, st * DM:(st + 1) * DM], 1.0,
                        x3ft[:, :], ALU.mult, ALU.add,
                        accum_out=stat4[:, 0:1]).then_inc(m, 1))
                    self.wait(ph, "scalar", sv, tev)
                    tsq = ss.bump(1)
                    ph.add("scalar", lambda e, m=ss.h: e.activation(
                        sq4[:, 0:DM], y4[:, :], AF.Square, accum_out=stat4[:, 1:2]).then_inc(m, 1))
                    self.wait(ph, "vector", ss, tsq)
                    tln = self.ln_tail(ph, y4[:, :], stat4, y4[:, :])
                    ln_done4[st] = tln
                    self.wait(ph, "sync", sv, tln)
                    tso = self.dma(ph, "sync", d_out, self.out.ap()[ts * 128:(ts + 1) * 128, :], y4[:, :])
                    self.wait(ph, "vector", d_out, tso)
                    self.wait(ph, "scalar", d_out, tso)
                if p == NP - 1:
                    self.wait(ph, "sync", d_out, d_out.v)
                ph.emit()
                scp2.close()
                scp.close()

    def finish_partial(self, src_dram):
        nc = self.nc
        ld, st = self.dsem("finl"), self.dsem("fins")
        with contextlib.ExitStack() as sc:
            buf = sc.enter_context(nc.sbuf_tensor("fin", [128, DM], F32))
            ph = Phase(nc)
            for ts in range(NT):
                t1 = self.dma(ph, "sync", ld, buf[:, :], src_dram.ap()[ts * 128:(ts + 1) * 128, :])
                self.wait(ph, "sync", ld, t1)
                t2 = self.dma(ph, "sync", st, self.out.ap()[ts * 128:(ts + 1) * 128, :], buf[:, :])
                self.wait(ph, "sync", st, t2)
            ph.emit()

    def build(self):
        if self.stages >= 4:
            self.prep_bg4()
        self.stage1()
        if self.max_phase < 99:
            self.stack.close()
            return self.nc
        if self.stages >= 2:
            self.stage2()
        if self.stages >= 3:
            self.stage3()
        if self.stages >= 4:
            self.stage4()
        else:
            src = {1: self.x1f, 2: self.x2f, 3: self.x3f}[self.stages]
            self.finish_partial(src)
        self.stack.close()
        return self.nc


def build_program(stages=4, debug_taps=(), max_phase=99):
    return Builder(stages=stages, debug_taps=debug_taps, max_phase=max_phase).build()


def make_in_maps(inputs, consts=None):
    consts = consts or host_constants()
    x = np.ascontiguousarray(inputs["L0_attn_x"])
    in_maps = []
    for core in range(8):
        m = {"x0": np.ascontiguousarray(x[core])}
        for name in WEIGHT_SPECS:
            m[name] = np.ascontiguousarray(inputs[name])
        for name in CONST_SPECS:
            m[name] = consts[name]
        in_maps.append(m)
    return in_maps


def kernel(**inputs):
    nc = build_program(stages=4)
    in_maps = make_in_maps(inputs)
    res = run_bass_kernel_spmd(nc, in_maps, core_ids=list(range(8)))
    return np.stack([r["out"] for r in res.results], axis=0)
